# version 10
# speedup vs baseline: 1.3681x; 1.0028x over previous
import os
import numpy as np
import concourse.bass as bass
import concourse.mybir as mybir
from concourse.bass_utils import run_bass_kernel_spmd

F32 = mybir.dt.float32
BF16 = mybir.dt.bfloat16
I8 = mybir.dt.int8
I16 = mybir.dt.int16
Alu = mybir.AluOpType
Act = mybir.ActivationFunctionType
AX = mybir.AxisListType.X

C = 1024
HD = 4096
ND = 8
NH = 32
W = 2053
TC = 2048
P = 128

SB_BASE = 16512
SB_TOP = 229344
DEBUG_CAPTURE = False
DBG_BLOCK = 1


def _blocks(lo, hi, step=512):
    out = []
    c = lo
    while c < hi:
        out.append((c, min(c + step, hi)))
        c += step
    return out


class Sem:
    def __init__(self, nc, name):
        self.h = nc.alloc_semaphore(name)
        self.n = 0

    def inc(self, instr, k=1):
        instr.then_inc(self.h, k)
        self.n += k
        return instr


class Arena:
    def __init__(self, nc):
        self.nc = nc
        self.cursor = SB_BASE
        self.uid = 0

    def take(self, shape, dtype, at=None):
        per_part = int(np.prod(shape[1:])) * mybir.dt.size(dtype)
        off = self.cursor if at is None else at
        off = (off + 31) & ~31
        self.uid += 1
        t = self.nc.alloc_sbuf_tensor_at(f"sb{self.uid}", list(shape), dtype, offset=off)
        if at is None:
            self.cursor = off + per_part
        assert off + per_part <= SB_TOP, (off, per_part, shape)
        return t


class Builder:
    def __init__(self, shadow):
        self.shadow = shadow
        nc = bass.Bass("TRN2", target_bir_lowering=False, detect_race_conditions=False)
        self.nc = nc
        self.ar = Arena(nc)
        d = nc.dram_tensor
        self.xw = d("xw", [ND, P, W], F32, kind="ExternalInput")
        self.w1t = d("w1t", [ND, P, HD], F32, kind="ExternalInput")
        self.w2t = d("w2t", [NH, P, C], F32, kind="ExternalInput")
        self.wkt = d("wkt", [ND, P, C], F32, kind="ExternalInput")
        self.wvt = d("wvt", [ND, P, C], F32, kind="ExternalInput")
        self.wrt = d("wrt", [ND, P, C], F32, kind="ExternalInput")
        self.wot = d("wot", [ND, P, C], F32, kind="ExternalInput")
        self.td_t = d("td_t", [P, ND], F32, kind="ExternalInput")
        self.u_t = d("u_t", [P, ND], F32, kind="ExternalInput")
        self.mk_t = d("mk_t", [P, ND], F32, kind="ExternalInput")
        self.mv_t = d("mv_t", [P, ND], F32, kind="ExternalInput")
        self.mr_t = d("mr_t", [P, ND], F32, kind="ExternalInput")
        self.wl_t = d("wl_t", [P, NH], F32, kind="ExternalInput")
        self.wc_t = d("wc_t", [P, NH], F32, kind="ExternalInput")
        self.wr2_t = d("wr2_t", [P, NH], F32, kind="ExternalInput")
        self.cb_t = d("cb_t", [P, NH], F32, kind="ExternalInput")
        self.iden_in = d("iden_in", [P, P], F32, kind="ExternalInput")
        self.startneg_in = d("startneg_in", [P, 1], F32, kind="ExternalInput")
        self.startmask_in = d("startmask_in", [P, 1], F32, kind="ExternalInput")
        self.endmask_in = d("endmask_in", [P, 1], F32, kind="ExternalInput")
        self.onehot_in = d("onehot_in", [P, 8], F32, kind="ExternalInput")
        self.out = d("out", [ND, P, TC], F32, kind="ExternalOutput")

        self.xres_a = d("xres_a", [ND, P, W], F32, kind="Internal")
        self.xres_b = d("xres_b", [ND, P, W], F32, kind="Internal")
        self.w1q_st = d("w1q_st", [ND, P, HD], BF16, kind="Internal")
        self.w2q_st = d("w2q_st", [NH, P, C], BF16, kind="Internal")
        self.xr_st = d("xr_st", [ND, P, TC + 2], BF16, kind="Internal")
        self.n2_st = d("n2_st", [NH, P, W + 3], I8, kind="Internal")
        self.kv_st = d("kv_st", [2, ND, P, TC + 8], F32, kind="Internal")
        self.cc_in = d("cc_in", [P, 24], F32, kind="Internal")
        self.cc_out = d("cc_out", [8, P, 24], F32, kind="Internal", addr_space="Local")
        if DEBUG_CAPTURE:
            self.dbg_h = d("dbg_h", [NH, P, 520], F32, kind="Internal")
            self.dbg_g = d("dbg_g", [NH, P, 512], F32, kind="Internal")
            self.dbg_xq = d("dbg_xq", [ND, P, 512], BF16, kind="Internal")

        self.sL = Sem(nc, "sL")
        self.sL2 = Sem(nc, "sL2")
        self.sS = Sem(nc, "sS")
        self.sS2 = Sem(nc, "sS2")
        self.sO = Sem(nc, "sO")
        self.sT = Sem(nc, "sT")
        self.sA = Sem(nc, "sA")
        self.sV = Sem(nc, "sV")
        self.sG = Sem(nc, "sG")
        self.sC = Sem(nc, "sC")

        ar = self.ar
        self.iden = ar.take([P, P], F32)
        self.ones_col = ar.take([P, 1], F32)
        self.ones_row = ar.take([1, P], F32)
        self.td = ar.take([P, ND], F32)
        self.u_c = ar.take([P, ND], F32)
        self.w_c = ar.take([P, ND], F32)
        self.absw = ar.take([P, ND], F32)
        self.mk = ar.take([P, ND], F32)
        self.mv = ar.take([P, ND], F32)
        self.mr = ar.take([P, ND], F32)
        self.wl = ar.take([P, NH], F32)
        self.wcv = ar.take([P, NH], F32)
        self.wr2 = ar.take([P, NH], F32)
        self.cb = ar.take([P, NH], F32)
        self.wlm = ar.take([P, NH], F32)
        self.wcm = ar.take([P, NH], F32)
        self.wrm = ar.take([P, NH], F32)
        self.startneg = ar.take([P, 1], F32)
        self.startmask = ar.take([P, 1], F32)
        self.endmask = ar.take([P, 1], F32)
        self.onehot = ar.take([P, 8], F32)
        self.mu1c = ar.take([P, 1], F32)
        self.swcol = ar.take([P, 1], F32)
        self.scal = ar.take([1, 8], F32)
        self.eps_col = ar.take([P, 1], F32)
        self.mxcol = ar.take([P, 4], F32)
        self.s2col = ar.take([P, 4], F32)
        self.sinp = ar.take([P, 24], F32)
        self.rowA = ar.take([1, 512], F32)
        self.rowB = ar.take([1, 512], F32)
        self.rowC = ar.take([1, 512], F32)
        self.rowD = ar.take([1, 512], F32)
        self.row_s2 = ar.take([1, 512], F32)
        self.row_s2i = ar.take([1, W + 64], F32)
        self.t_xs = ar.take([P, 512], F32)
        self.t_qs = ar.take([P, 512], F32)
        self.t_i16 = ar.take([P, 512], I16)
        self.t_cg = ar.take([P, 512], F32) if shadow else None
        self.PERS_END = ar.cursor
        self.ps = [nc.alloc_psum_tensor(f"ps{i}", [P, 512], F32) for i in range(8)]
        self.R_W = (self.PERS_END + 31) & ~31
        self.R_BIG = self.R_W + 65568
        self.R_MED = self.R_BIG + 65568
        assert self.R_MED + 45 * 1024 < SB_TOP, (self.R_MED, SB_TOP)

    def ld(self, out_ap, in_ap):
        return self.sL.inc(self.nc.sync.dma_start(out_ap, in_ap), 16)

    def ld2(self, out_ap, in_ap):
        return self.sL2.inc(self.nc.gpsimd.dma_start(out_ap, in_ap), 16)

    def st(self, out_ap, in_ap, final=False):
        sem = self.sO if final else self.sS
        return sem.inc(self.nc.sync.dma_start(out_ap, in_ap), 16)

    def st2(self, out_ap, in_ap):
        return self.sS2.inc(self.nc.sync.dma_start(out_ap, in_ap), 16)

    def wA(self, eng):
        eng.wait_ge(self.sA.h, self.sA.n)

    def wV(self, eng):
        eng.wait_ge(self.sV.h, self.sV.n)

    def wG(self, eng):
        eng.wait_ge(self.sG.h, self.sG.n)

    def wT(self, eng):
        eng.wait_ge(self.sT.h, self.sT.n)

    def syncS(self):
        self.nc.sync.wait_ge(self.sS.h, self.sS.n)
        if self.sS2.n:
            self.nc.sync.wait_ge(self.sS2.h, self.sS2.n)

    def wL(self, eng):
        eng.wait_ge(self.sL.h, self.sL.n)
        if self.sL2.n:
            eng.wait_ge(self.sL2.h, self.sL2.n)

    def wS(self, eng):
        eng.wait_ge(self.sS.h, self.sS.n)
        if self.sS2.n:
            eng.wait_ge(self.sS2.h, self.sS2.n)

    def vf(self, instr):
        self.sV.inc(instr)
        self.nc.vector.wait_ge(self.sV.h, self.sV.n)
        return instr

    def af(self, instr):
        self.sA.inc(instr)
        self.nc.scalar.wait_ge(self.sA.h, self.sA.n)
        return instr

    def gf(self, instr):
        self.sG.inc(instr)
        self.nc.gpsimd.wait_ge(self.sG.h, self.sG.n)
        return instr

    def p0(self):
        nc = self.nc
        v, a, g = nc.vector, nc.scalar, nc.gpsimd
        self.ld(self.iden[:], self.iden_in[:])
        self.ld(self.td[:], self.td_t[:])
        self.ld(self.u_c[:], self.u_t[:])
        self.ld(self.mk[:], self.mk_t[:])
        self.ld(self.mv[:], self.mv_t[:])
        self.ld(self.mr[:], self.mr_t[:])
        self.ld(self.wl[:], self.wl_t[:])
        self.ld(self.wcv[:], self.wc_t[:])
        self.ld(self.wr2[:], self.wr2_t[:])
        self.ld(self.cb[:], self.cb_t[:])
        self.ld(self.startneg[:], self.startneg_in[:])
        self.ld(self.startmask[:], self.startmask_in[:])
        self.ld(self.endmask[:], self.endmask_in[:])
        self.ld(self.onehot[:], self.onehot_in[:])
        g.memset(self.ones_col[:], 1.0)
        g.memset(self.ones_row[:], 1.0)
        g.memset(self.mxcol[:], 0.0)
        g.memset(self.s2col[:], 0.0)
        g.memset(self.row_s2i[:], 1.0)
        self.gf(g.memset(self.eps_col[:], 1e-5))
        self.wL(a)
        self.af(a.activation(self.absw[:], self.td[:], Act.Exp, bias=0.0, scale=1.0))
        self.wA(v)
        self.vf(v.tensor_scalar(self.w_c[:], self.absw[:], -1.0, None, Alu.mult))

    def ternarize(self, src_dram, nchunks, width, wq_dst, st_dram, mu_slot,
                  region=None, use_ld2=False):
        nc, ar = self.nc, self.ar
        v, a, te = nc.vector, nc.scalar, nc.tensor
        ar.cursor = self.R_BIG if region is None else region
        ldf = self.ld2 if use_ld2 else self.ld
        lsem = self.sL2 if use_ld2 else self.sL
        stage = [ar.take([P, width], F32) for _ in range(2)]
        junk = ar.take([P, width], F32)
        i16 = ar.take([P, width], I16)
        qstage = [ar.take([P, width], BF16) for _ in range(2)] if wq_dst is None else None
        abscol = ar.take([P, nchunks + 4], F32)
        acc_col = ar.take([P, 4], F32)

        base = lsem.n
        ldf(stage[0][:], src_dram[0])
        if nchunks > 1:
            ldf(stage[1][:], src_dram[1])
        for i in range(nchunks):
            a.wait_ge(lsem.h, base + 16 * (i + 1))
            self.sA.inc(a.activation(junk[:], stage[i % 2][:], Act.Abs, bias=0.0,
                                     scale=1.0, accum_out=abscol[:, i:i + 1]))
            if i + 2 < nchunks:
                if use_ld2:
                    nc.gpsimd.wait_ge(self.sA.h, self.sA.n)
                else:
                    nc.sync.wait_ge(self.sA.h, self.sA.n)
                ldf(stage[i % 2][:], src_dram[i + 2])
        self.wA(v)
        self.vf(v.tensor_reduce(acc_col[:, 0:1], abscol[:, 0:nchunks], AX, Alu.add))
        self.wV(te)
        self.wG(te)
        self.sT.inc(te.matmul(self.ps[0][0:1, 0:1], self.ones_col[:], acc_col[:, 0:1],
                              start=True, stop=True))
        self.wT(v)
        inv_n = 1.0 / float(nchunks * P * width)
        self.vf(v.tensor_scalar(self.scal[:, mu_slot + 1:mu_slot + 2], self.ps[0][0:1, 0:1],
                                inv_n, 1e-5, Alu.mult, Alu.max))
        self.vf(v.reciprocal(self.scal[:, mu_slot:mu_slot + 1],
                             self.scal[:, mu_slot + 1:mu_slot + 2]))
        self.vf(v.reciprocal(self.scal[:, mu_slot + 1:mu_slot + 2],
                             self.scal[:, mu_slot:mu_slot + 1]))
        self.wV(te)
        self.sT.inc(te.matmul(self.ps[1][:, 0:1], self.ones_row[:],
                              self.scal[:, mu_slot:mu_slot + 1], start=True, stop=True))
        self.wT(v)
        self.vf(v.tensor_copy(self.swcol[:], self.ps[1][:, 0:1]))

        base = lsem.n
        ldf(stage[0][:], src_dram[0])
        if nchunks > 1:
            ldf(stage[1][:], src_dram[1])
        st_base = self.sS2.n
        for i in range(nchunks):
            v.wait_ge(lsem.h, base + 16 * (i + 1))
            dst = wq_dst[i][:] if wq_dst is not None else qstage[i % 2][:]
            if wq_dst is None and i >= 2:
                v.wait_ge(self.sS2.h, st_base + 16 * (i - 1))
            v.tensor_scalar(junk[:], stage[i % 2][:], self.swcol[:], None, Alu.mult)
            if not self.shadow:
                v.tensor_scalar(i16[:], junk[:], 0.0, None, Alu.add)
                self.sV.inc(v.tensor_scalar(dst, i16[:], 1.0, -1.0, Alu.min, Alu.max))
            else:
                v.tensor_scalar(junk[:], junk[:], 256.5, None, Alu.add)
                v.tensor_scalar(i16[:], junk[:], 0.0, None, Alu.add)
                cg = stage[i % 2]
                v.tensor_tensor(cg[:], i16[:], junk[:], Alu.is_gt)
                v.scalar_tensor_tensor(junk[:], i16[:], 256.0, cg[:], Alu.subtract,
                                       Alu.subtract)
                self.sV.inc(v.tensor_scalar(dst, junk[:], 1.0, -1.0, Alu.min, Alu.max))
            nc.sync.wait_ge(self.sV.h, self.sV.n)
            self.st2(st_dram[i], dst)
            if i + 2 < nchunks:
                if use_ld2:
                    nc.gpsimd.wait_ge(self.sV.h, self.sV.n)
                ldf(stage[i % 2][:], src_dram[i + 2])

    def pw(self):
        nc = self.nc
        v, te = nc.vector, nc.tensor
        self.w1q = [self.ar.take([P, HD], BF16, at=self.R_W + i * HD * 2) for i in range(ND)]
        self.w2q = [self.ar.take([P, C], BF16, at=self.R_W + i * C * 2) for i in range(NH)]
        self.ternarize(self.w1t, ND, HD, self.w1q, self.w1q_st, mu_slot=0)
        self.ternarize(self.w2t, NH, C, None, self.w2q_st, mu_slot=2,
                       region=self.R_MED, use_ld2=True)
        self.wV(te)
        self.sT.inc(te.matmul(self.ps[2][:, 0:1], self.ones_row[:], self.scal[:, 1:2],
                              start=True, stop=True))
        self.wT(v)
        self.vf(v.tensor_copy(self.mu1c[:], self.ps[2][:, 0:1]))
        self.vf(v.tensor_scalar(self.wlm[:], self.wl[:], self.mu1c[:], None, Alu.mult))
        self.vf(v.tensor_scalar(self.wcm[:], self.wcv[:], self.mu1c[:], None, Alu.mult))
        self.vf(v.tensor_scalar(self.wrm[:], self.wr2[:], self.mu1c[:], None, Alu.mult))
        self.vf(v.tensor_scalar(self.scal[:, 4:5], self.scal[:, 3:4], 0.5, None, Alu.mult))

    def ln_stats(self, xblk, sq_strips, bw, ps_a, ps_b, act_guard=False):
        nc = self.nc
        v, a, te, g = nc.vector, nc.scalar, nc.tensor, nc.gpsimd
        w64 = max(bw, 64)
        self.wL(a)
        self.wV(a)
        self.wG(a)
        for dd in range(ND):
            ins = a.activation(sq_strips[dd][:, 0:w64], xblk[dd][:, 0:w64], Act.Square,
                               bias=0.0, scale=1.0)
        self.sA.inc(ins)
        self.wA(v)
        self.wA(g)
        self.wV(g)
        v.wait_ge(self.sT.h, getattr(self, 'last_tp', 0))
        xs, qs = self.t_xs, self.t_qs
        v.tensor_tensor(xs[:, 0:w64], xblk[0][:, 0:w64], xblk[1][:, 0:w64], Alu.add)
        g.tensor_tensor(qs[:, 0:w64], sq_strips[0][:, 0:w64], sq_strips[1][:, 0:w64],
                        Alu.add)
        for dd in range(2, ND):
            iv = v.tensor_tensor(xs[:, 0:w64], xs[:, 0:w64], xblk[dd][:, 0:w64], Alu.add)
            ig = g.tensor_tensor(qs[:, 0:w64], qs[:, 0:w64], sq_strips[dd][:, 0:w64],
                                 Alu.add)
        self.sV.inc(iv)
        self.sG.inc(ig)
        self.wV(te)
        self.wG(te)
        if act_guard:
            self.wA(te)
        te.matmul(ps_a[0:1, 0:bw], self.ones_col[:], xs[:, 0:bw], start=True, stop=True)
        self.sT.inc(te.matmul(ps_b[0:1, 0:bw], self.ones_col[:], qs[:, 0:bw],
                              start=True, stop=True))

    def ln_rows(self, bw, ps_a, ps_b):
        nc = self.nc
        v, a = nc.vector, nc.scalar
        w64 = max(bw, 64)
        self.wT(v)
        v.tensor_scalar(self.rowA[0:1, 0:w64], ps_a[0:1, 0:w64], 1.0 / C, None, Alu.mult)
        v.tensor_scalar(self.rowB[0:1, 0:w64], ps_b[0:1, 0:w64], 1.0 / C, None, Alu.mult)
        v.tensor_tensor(self.rowC[0:1, 0:w64], self.rowA[0:1, 0:w64],
                        self.rowA[0:1, 0:w64], Alu.mult)
        v.tensor_tensor(self.rowB[0:1, 0:w64], self.rowB[0:1, 0:w64],
                        self.rowC[0:1, 0:w64], Alu.subtract)
        self.sV.inc(v.tensor_scalar(self.rowB[0:1, 0:w64], self.rowB[0:1, 0:w64], 0.0,
                                    None, Alu.max))
        self.wV(a)
        self.sA.inc(a.activation(self.rowC[0:1, 0:w64], self.rowB[0:1, 0:w64], Act.Sqrt,
                                 bias=self.eps_col[0:1, 0:1], scale=1.0))
        self.wA(v)
        self.sV.inc(v.reciprocal(self.rowD[0:1, 0:w64], self.rowC[0:1, 0:w64]))

    def quant1(self, c0, c1, in_dram, xblk, sq, xqd, tmpB, preloaded=False,
               next_rng=None):
        nc = self.nc
        v, a, te, g = nc.vector, nc.scalar, nc.tensor, nc.gpsimd
        bw = c1 - c0
        w64 = max(bw, 64)
        if not preloaded:
            for dd in range(ND):
                self.ld(xblk[dd][:, 0:bw], in_dram[dd][:, c0:c1])
        self.ln_stats(xblk, sq, bw, self.ps[0], self.ps[1])
        self.ln_rows(bw, self.ps[0], self.ps[1])
        self.wV(te)
        self.wA(te)
        self.sT.inc(te.matmul(self.ps[2][:, 0:bw], self.ones_row[:], self.rowA[0:1, 0:bw],
                              start=True, stop=True))
        self.wT(v)
        self.wT(g)
        self.wG(v)
        for dd in range(ND):
            self.sV.inc(v.tensor_tensor(sq[dd][:, 0:w64], xblk[dd][:, 0:w64],
                                        self.ps[2][:, 0:w64], Alu.subtract))
        if next_rng is not None:
            nc.sync.wait_ge(self.sV.h, self.sV.n)
            nc0, nc1 = next_rng
            for dd in range(ND):
                self.ld(xblk[dd][:, 0:nc1 - nc0], in_dram[dd][:, nc0:nc1])
        self.wV(g)
        xs = self.t_xs
        tmps = [self.t_qs, tmpB]
        vmax_done = []
        for dd in range(ND):
            if dd >= 2:
                g.wait_ge(self.sV.h, vmax_done[dd - 2])
            self.sG.inc(g.tensor_tensor(tmps[dd % 2][:, 0:w64], sq[dd][:, 0:w64],
                                        sq[dd][:, 0:w64], Alu.mult))
            v.wait_ge(self.sG.h, self.sG.n)
            if dd == 0:
                iv = v.tensor_copy(xs[:, 0:w64], tmps[0][:, 0:w64])
            else:
                iv = v.tensor_tensor(xs[:, 0:w64], xs[:, 0:w64], tmps[dd % 2][:, 0:w64],
                                     Alu.max)
            self.sV.inc(iv)
            vmax_done.append(self.sV.n)
        self.wV(te)
        nsb = (bw + 127) // 128
        for j in range(nsb):
            sw_ = min(128, bw - j * 128)
            it = te.transpose(self.ps[3][0:sw_, j * 128:j * 128 + 128],
                              xs[:, j * 128:j * 128 + sw_], self.iden[:])
        self.sT.inc(it)
        self.last_tp = self.sT.n
        v.wait_ge(self.sT.h, self.last_tp)
        self.wA(v)
        for j in range(nsb):
            sw_ = min(128, bw - j * 128)
            self.vf(v.tensor_reduce(self.mxcol[0:sw_, j:j + 1],
                                    self.ps[3][0:sw_, j * 128:(j + 1) * 128], AX, Alu.max))
        self.wV(a)
        self.af(a.activation(self.mxcol[:, 0:4], self.mxcol[:, 0:4], Act.Sqrt,
                             bias=0.0, scale=1.0))
        self.wA(te)
        self.wV(te)
        for j in range(nsb):
            it = te.transpose(self.ps[3][0:1, j * 128:(j + 1) * 128],
                              self.mxcol[:, j:j + 1], self.iden[:])
        self.sT.inc(it)
        self.wT(v)
        v.tensor_copy(self.rowC[0:1, 0:w64], self.ps[3][0:1, 0:w64])
        v.tensor_tensor(self.rowC[0:1, 0:w64], self.rowD[0:1, 0:w64],
                        self.rowC[0:1, 0:w64], Alu.mult)
        v.tensor_scalar(self.rowC[0:1, 0:w64], self.rowC[0:1, 0:w64], 1e-5, None, Alu.max)
        v.reciprocal(self.rowB[0:1, 0:w64], self.rowC[0:1, 0:w64])
        self.sV.inc(v.tensor_scalar(self.rowB[0:1, 0:w64], self.rowB[0:1, 0:w64], 127.0,
                                    None, Alu.mult))
        self.sV.inc(v.reciprocal(self.rowC[0:1, 0:w64], self.rowB[0:1, 0:w64]))
        self.sV.inc(v.tensor_tensor(self.rowD[0:1, 0:w64], self.rowD[0:1, 0:w64],
                                    self.rowB[0:1, 0:w64], Alu.mult))
        self.wV(te)
        te.matmul(self.ps[2][:, 0:bw], self.ones_row[:], self.rowD[0:1, 0:bw],
                  start=True, stop=True)
        self.sT.inc(te.matmul(self.ps[3][:, 0:bw], self.ones_row[:], self.rowC[0:1, 0:bw],
                              start=True, stop=True))
        self.wT(v)
        i16 = self.t_i16
        shift = 256.5 if self.shadow else 0.0
        for dd in range(ND):
            v.tensor_tensor(sq[dd][:, 0:w64], sq[dd][:, 0:w64], self.ps[2][:, 0:w64],
                            Alu.mult)
            if not self.shadow:
                v.tensor_scalar(i16[:, 0:w64], sq[dd][:, 0:w64], 0.0, None, Alu.add)
                iv = v.scalar_tensor_tensor(xqd[dd][:, 0:w64], i16[:, 0:w64], 0.0,
                                            self.ps[3][:, 0:w64], Alu.subtract, Alu.mult)
            else:
                cg = self.t_cg
                v.tensor_scalar(sq[dd][:, 0:w64], sq[dd][:, 0:w64], shift, None, Alu.add)
                v.tensor_scalar(i16[:, 0:w64], sq[dd][:, 0:w64], 0.0, None, Alu.add)
                v.tensor_tensor(cg[:, 0:w64], i16[:, 0:w64], sq[dd][:, 0:w64], Alu.is_gt)
                v.scalar_tensor_tensor(cg[:, 0:w64], i16[:, 0:w64], 256.0, cg[:, 0:w64],
                                       Alu.subtract, Alu.subtract)
                iv = v.tensor_tensor(xqd[dd][:, 0:w64], cg[:, 0:w64], self.ps[3][:, 0:w64],
                                     Alu.mult)
            self.sV.inc(iv)
        self.wV(v)

    def ffn_p1(self, lo, hi, in_dram, final):
        nc, ar = self.nc, self.ar
        v, a, te, g = nc.vector, nc.scalar, nc.tensor, nc.gpsimd
        ar.cursor = self.R_BIG
        gstr = [ar.take([P, 512], F32) for _ in range(NH)]
        ar.cursor = self.R_MED
        xblk = [ar.take([P, 512], F32) for _ in range(ND)]
        sq = [ar.take([P, 512], F32) for _ in range(ND)]
        xqd = [ar.take([P, 512], BF16) for _ in range(ND)]
        Hs = [ar.take([P, 520], F32) for _ in range(2)]
        tails = ar.take([P, 2 * NH], F32)
        n2o = [ar.take([P, 512], I8) for _ in range(2)]
        rmA = ar.take([P, 512], F32)
        rmB = ar.take([P, 512], F32)
        s2b_s = ar.take([P, 512], F32)
        u_s = self.t_qs

        self.wA(g)
        self.wV(g)
        g.memset(Hs[0][:], 0.0)
        self.gf(g.memset(Hs[1][:], 0.0))

        blocks = _blocks(lo, hi)
        for bi, (c0, c1) in enumerate(blocks):
            bw = c1 - c0
            first, last = (c0 == lo), (c1 == hi)
            g0 = c0 + 1 if first else c0 - 1
            g1 = hi - 1 if last else c1 - 1
            gw = g1 - g0
            gw64 = max(gw, 64)
            nxt = blocks[bi + 1] if bi + 1 < len(blocks) else None
            self.quant1(c0, c1, in_dram, xblk, sq, xqd, rmA,
                        preloaded=(bi > 0), next_rng=nxt)
            self.wG(a)
            mm_base = self.sT.n
            ev_base = self.sA.n
            self.rm_done = self.sV.n
            conv_done = []
            t1_done = []
            silu_done = []
            o_l = g0 - c0 + 1
            for ch in range(NH):
                pch = self.ps[4 + (ch % 4)]
                if ch >= 4:
                    te.wait_ge(self.sA.h, ev_base + 3 * (ch - 4) + 1)
                elif ch == 0:
                    self.wV(te)
                for dd in range(ND):
                    mm = te.matmul(pch[:, 0:bw], self.w1q[dd][:, ch * 128:(ch + 1) * 128],
                                   xqd[dd][:, 0:bw], start=(dd == 0), stop=(dd == ND - 1))
                self.sT.inc(mm)
                hb = Hs[ch % 2]
                a.wait_ge(self.sT.h, mm_base + ch + 1)
                if ch >= 2:
                    a.wait_ge(self.sV.h, conv_done[ch - 2])
                self.sA.inc(a.copy(hb[:, 2:2 + bw], pch[:, 0:bw]))
                v.wait_ge(self.sA.h, ev_base + 3 * ch + 1)
                if not first:
                    self.sV.inc(v.tensor_copy(hb[:, 0:2], tails[:, 2 * ch:2 * ch + 2]))
                if final and last:
                    lc = (hi - 1 - c0) + 2
                    self.sV.inc(v.tensor_scalar(hb[:, lc:lc + 1], hb[:, lc:lc + 1],
                                                self.endmask[:], None, Alu.mult))
                if final and first:
                    self.sV.inc(v.tensor_scalar(hb[:, 2:3], hb[:, 2:3],
                                                self.startmask[:], None, Alu.mult))
                self.wV(v)
                hb_ready = self.sV.n
                wlc = self.wlm[:, ch:ch + 1]
                wcc = self.wcm[:, ch:ch + 1]
                wrc = self.wrm[:, ch:ch + 1]
                t1 = sq[ch % 2]
                ub = sq[2 + ch % 2]
                a.wait_ge(self.sV.h, hb_ready)
                self.sA.inc(a.activation(t1[:, 0:gw64], hb[:, o_l:o_l + gw64], Act.Copy,
                                         bias=0.0, scale=wlc))
                if ch >= 2:
                    v.wait_ge(self.sA.h, silu_done[ch - 2])
                v.wait_ge(self.sA.h, ev_base + 3 * ch + 2)
                v.scalar_tensor_tensor(ub[:, 0:gw64], hb[:, o_l + 1:o_l + 1 + gw64], wcc,
                                       t1[:, 0:gw64], Alu.mult, Alu.add)
                self.sV.inc(v.scalar_tensor_tensor(ub[:, 0:gw64],
                                                   hb[:, o_l + 2:o_l + 2 + gw64], wrc,
                                                   ub[:, 0:gw64], Alu.mult, Alu.add))
                self.sV.inc(v.tensor_copy(tails[:, 2 * ch:2 * ch + 2], hb[:, bw:bw + 2]))
                conv_done.append(self.sV.n)
                a.wait_ge(self.sV.h, conv_done[-1])
                self.sA.inc(a.activation(gstr[ch][:, 0:gw64], ub[:, 0:gw64], Act.Silu,
                                         bias=self.cb[:, ch:ch + 1], scale=1.0))
                silu_done.append(self.sA.n)
                g.wait_ge(self.sA.h, silu_done[-1])
                if ch >= 1:
                    g.wait_ge(self.sV.h, self.rm_done)
                src0 = gstr[ch][:, 0:gw64]
                self.sG.inc(g.tensor_tensor(rmB[:, 0:gw64], src0, src0, Alu.mult))
                v.wait_ge(self.sG.h, self.sG.n)
                if ch == 0:
                    iv = v.tensor_copy(rmA[:, 0:gw64], rmB[:, 0:gw64])
                else:
                    iv = v.tensor_tensor(rmA[:, 0:gw64], rmA[:, 0:gw64], rmB[:, 0:gw64],
                                         Alu.max)
                self.sV.inc(iv)
                self.rm_done = self.sV.n
            self.wG(te)
            self.wV(te)
            nsb = (gw + 127) // 128
            for j in range(nsb):
                sw_ = min(128, gw - j * 128)
                it = te.transpose(self.ps[0][0:sw_, j * 128:j * 128 + 128],
                                  rmA[:, j * 128:j * 128 + sw_], self.iden[:])
            self.sT.inc(it)
            self.wT(v)
            for j in range(nsb):
                sw_ = min(128, gw - j * 128)
                self.vf(v.tensor_reduce(self.mxcol[0:sw_, j:j + 1],
                                        self.ps[0][0:sw_, j * 128:(j + 1) * 128], AX,
                                        Alu.max))
            self.wV(a)
            self.af(a.activation(self.mxcol[:, 0:4], self.mxcol[:, 0:4], Act.Sqrt,
                                 bias=0.0, scale=1.0))
            self.wA(v)
            self.vf(v.tensor_scalar(self.mxcol[:, 0:4], self.mxcol[:, 0:4], 1e-5, None,
                                    Alu.max))
            self.vf(v.reciprocal(self.s2col[:, 0:4], self.mxcol[:, 0:4]))
            self.vf(v.tensor_scalar(self.s2col[:, 0:4], self.s2col[:, 0:4], 127.0,
                                    None, Alu.mult))
            self.wV(te)
            for j in range(nsb):
                it = te.transpose(self.ps[1][0:1, j * 128:(j + 1) * 128],
                                  self.s2col[:, j:j + 1], self.iden[:])
            self.sT.inc(it)
            self.wT(v)
            self.sV.inc(v.tensor_copy(self.row_s2[0:1, 0:gw64], self.ps[1][0:1, 0:gw64]))
            v.tensor_scalar(self.row_s2[0:1, 0:gw64], self.row_s2[0:1, 0:gw64], 1e-5,
                            None, Alu.max)
            self.sV.inc(v.reciprocal(self.row_s2i[0:1, g0:g0 + gw64],
                                     self.row_s2[0:1, 0:gw64]))
            self.wV(te)
            self.sT.inc(te.matmul(self.ps[2][:, 0:gw], self.ones_row[:],
                                  self.row_s2[0:1, 0:gw], start=True, stop=True))
            self.wT(a)
            self.sA.inc(a.copy(s2b_s[:, 0:gw], self.ps[2][:, 0:gw]))
            st_base = self.sS2.n
            for ch in range(NH):
                eng, sem = (g, self.sG) if (ch % 2) else (v, self.sV)
                self.wA(eng)
                sem.inc(eng.tensor_tensor(gstr[ch][:, 0:gw64], gstr[ch][:, 0:gw64],
                                          s2b_s[:, 0:gw64], Alu.mult))
                nb = n2o[ch % 2]
                v.wait_ge(sem.h, sem.n)
                if ch >= 2:
                    v.wait_ge(self.sS2.h, st_base + 16 * (ch - 1))
                if not self.shadow:
                    iv = v.tensor_scalar(nb[:, 0:gw64], gstr[ch][:, 0:gw64], 0.0, None,
                                         Alu.add)
                else:
                    v.tensor_scalar(self.t_cg[:, 0:gw64], gstr[ch][:, 0:gw64], 256.5,
                                    None, Alu.add)
                    v.tensor_scalar(self.t_i16[:, 0:gw64], self.t_cg[:, 0:gw64], 0.0,
                                    None, Alu.add)
                    v.tensor_tensor(u_s[:, 0:gw64], self.t_i16[:, 0:gw64],
                                    self.t_cg[:, 0:gw64], Alu.is_gt)
                    v.scalar_tensor_tensor(u_s[:, 0:gw64], self.t_i16[:, 0:gw64], 256.0,
                                           u_s[:, 0:gw64], Alu.subtract, Alu.subtract)
                    iv = v.tensor_scalar(nb[:, 0:gw64], u_s[:, 0:gw64], 0.0, None, Alu.add)
                self.sV.inc(iv)
                nc.sync.wait_ge(self.sV.h, self.sV.n)
                self.st2(self.n2_st[ch][:, g0:g1], nb[:, 0:gw])

    def ffn_p2(self, lo, hi, in_dram, out_dram, final):
        nc, ar = self.nc, self.ar
        v, a, te = nc.vector, nc.scalar, nc.tensor
        ar.cursor = self.R_BIG
        n2b = [ar.take([P, 512], BF16) for _ in range(NH)]
        n2i = [ar.take([P, 512], I8) for _ in range(NH)]
        ar.cursor = self.R_MED
        xblk = [ar.take([P, 512], F32) for _ in range(ND)]
        ostr = [ar.take([P, 512], F32) for _ in range(2)]
        c2s = ar.take([P, 512], F32)

        for bi, (g0, g1) in enumerate(_blocks(lo + 1, hi - 1)):
            gw = g1 - g0
            gw64 = max(gw, 64)
            nc.sync.wait_ge(self.sA.h, self.sA.n)
            nc.sync.wait_ge(self.sV.h, self.sV.n)
            nc.sync.wait_ge(self.sG.h, self.sG.n)
            base = self.sL.n
            for ch in range(NH):
                self.ld(n2i[ch][:, 0:gw], self.n2_st[ch][:, g0:g1])
            for dd in range(ND):
                self.ld(xblk[dd][:, 0:gw], in_dram[dd][:, g0:g1])
            xd_ready = self.sL.n
            a.wait_ge(self.sL.h, base + 16 * NH)
            self.wT(a)
            for ch in range(NH):
                ia = a.copy(n2b[ch][:, 0:gw], n2i[ch][:, 0:gw])
            self.sA.inc(ia)
            self.wA(v)
            self.sV.inc(v.tensor_scalar(self.rowA[0:1, 0:gw64],
                                        self.row_s2i[0:1, g0:g0 + gw64],
                                        self.scal[:, 4:5], None, Alu.mult))
            self.wV(te)
            self.sT.inc(te.matmul(self.ps[2][:, 0:gw], self.ones_row[:],
                                  self.rowA[0:1, 0:gw], start=True, stop=True))
            self.wT(a)
            self.sA.inc(a.copy(c2s[:, 0:gw], self.ps[2][:, 0:gw]))
            mm_base = self.sT.n
            ev_base = self.sV.n
            ost_base = self.sO.n if final else self.sS2.n
            for od in range(ND):
                pso = self.ps[6 + (od % 2)]
                if od >= 2:
                    te.wait_ge(self.sV.h, ev_base + (od - 1))
                else:
                    self.wA(te)
                    self.wV(te)
                for ch in range(NH):
                    mm = te.matmul(pso[:, 0:gw], self.w2q[ch][:, od * 128:(od + 1) * 128],
                                   n2b[ch][:, 0:gw], start=(ch == 0), stop=(ch == NH - 1))
                self.sT.inc(mm)
                ob = ostr[od % 2]
                v.wait_ge(self.sT.h, mm_base + od + 1)
                v.wait_ge(self.sL.h, xd_ready)
                v.wait_ge(self.sA.h, self.sA.n)
                if od >= 2:
                    sem = self.sO if final else self.sS2
                    v.wait_ge(sem.h, ost_base + 16 * (od - 1))
                v.tensor_tensor(ob[:, 0:gw64], pso[:, 0:gw64], c2s[:, 0:gw64], Alu.mult)
                self.sV.inc(v.tensor_tensor(ob[:, 0:gw64], ob[:, 0:gw64],
                                            xblk[od][:, 0:gw64], Alu.add))
                nc.sync.wait_ge(self.sV.h, self.sV.n)
                if final:
                    self.st(out_dram[od][:, g0 - 3:g1 - 3], ob[:, 0:gw], final=True)
                else:
                    self.st2(out_dram[od][:, g0:g1], ob[:, 0:gw])

    def ffn(self, lo, hi, in_dram, out_dram, final):
        nc = self.nc
        self.ffn_p1(lo, hi, in_dram, final)
        nc.sync.wait_ge(self.sT.h, self.sT.n)
        self.syncS()
        for i in range(NH):
            self.ld(self.w2q[i][:], self.w2q_st[i])
        w2q_ready = self.sL.n
        nc.tensor.wait_ge(self.sL.h, w2q_ready)
        self.ffn_p2(lo, hi, in_dram, out_dram, final)

    def ln_only(self, c0, c1, in_dram, xblk, xn_out, preloaded=False, next_rng=None):
        nc = self.nc
        v, te = nc.vector, nc.tensor
        bw = c1 - c0
        if not preloaded:
            nc.sync.wait_ge(self.sV.h, self.sV.n)
            for dd in range(ND):
                self.ld(xblk[dd][:, 0:bw], in_dram[dd][:, c0:c1])
        self.ln_stats(xblk, xn_out, bw, self.ps[0], self.ps[1], act_guard=True)
        self.ln_rows(bw, self.ps[0], self.ps[1])
        self.wV(te)
        te.matmul(self.ps[2][:, 0:bw], self.ones_row[:], self.rowA[0:1, 0:bw],
                  start=True, stop=True)
        self.sT.inc(te.matmul(self.ps[3][:, 0:bw], self.ones_row[:], self.rowD[0:1, 0:bw],
                              start=True, stop=True))
        self.wT(v)
        w64 = max(bw, 64)
        for dd in range(ND):
            v.tensor_tensor(xn_out[dd][:, 0:w64], xblk[dd][:, 0:w64], self.ps[2][:, 0:w64],
                            Alu.subtract)
            iv = v.tensor_tensor(xn_out[dd][:, 0:w64], xn_out[dd][:, 0:w64],
                                 self.ps[3][:, 0:w64], Alu.mult)
        self.sV.inc(iv)
        if next_rng is not None:
            nc.sync.wait_ge(self.sV.h, self.sV.n)
            n0, n1 = next_rng
            for dd in range(ND):
                self.ld(xblk[dd][:, 0:n1 - n0], in_dram[dd][:, n0:n1])

    def attn(self):
        nc, ar = self.nc, self.ar
        v, a, te, g = nc.vector, nc.scalar, nc.tensor, nc.gpsimd
        AW = 256
        wbufA = [self.ar.take([P, C], BF16, at=self.R_W + i * C * 2) for i in range(ND)]
        wbufB = [self.ar.take([P, C], BF16, at=self.R_W + (ND + i) * C * 2)
                 for i in range(ND)]
        ar.cursor = self.R_BIG
        PASS2_OFF = ar.cursor
        xnw = [[ar.take([P, AW], F32) for _ in range(ND)] for _ in range(2)]
        ramp = ar.take([P, TC], F32)
        ar.cursor = self.R_MED
        kwin = [ar.take([P, AW], F32) for _ in range(ND)]
        vwin = [ar.take([P, AW], F32) for _ in range(ND)]
        xb1 = [ar.take([P, AW], F32) for _ in range(ND)]
        xk_b = [ar.take([P, AW], BF16) for _ in range(ND)]
        xv_b = [ar.take([P, AW], BF16) for _ in range(ND)]
        xr_b = [ar.take([P, AW], BF16) for _ in range(2)]
        t_a = ar.take([P, AW], F32)
        sumbuf = ar.take([P, 24], F32)
        kapw = ar.take([P, ND], F32)
        kapn = ar.take([P, 1], F32)
        sa_w = ar.take([P, ND], F32)
        sb_w = ar.take([P, ND], F32)
        tmp8 = ar.take([P, ND], F32)
        tmp8b = ar.take([P, ND], F32)
        tmp8c = ar.take([P, ND], F32)
        aggb = ar.take([P, 8 * 24], F32)
        ccs = ar.take([P, 24], F32)

        self.wV(g)
        self.wA(g)
        self.sG.inc(g.iota(ramp[:], [[1, TC]], channel_multiplier=0,
                           allow_small_or_imprecise_dtypes=True))

        self.syncS()
        self.wV(g)
        if self.sS2.n:
            g.wait_ge(self.sS2.h, self.sS2.n)
        for i in range(ND):
            self.ld2(wbufA[i][:], self.wkt[i])
            self.ld2(wbufB[i][:], self.wvt[i])
        wk_ready = self.sL2.n

        ln_blocks = _blocks(1, W - 1, AW)
        mr_blocks = _blocks(2, W - 1, AW)
        nb = len(ln_blocks)
        xr_sts = []

        def ln2(j):
            nxt = ln_blocks[j + 1] if j + 1 < len(ln_blocks) else None
            self.ln_only(ln_blocks[j][0], ln_blocks[j][1], self.xres_a, xb1, xnw[j % 2],
                         preloaded=(j > 0), next_rng=nxt)
            if j == 0:
                self.wV(v)
                for dd in range(ND):
                    self.sV.inc(v.tensor_scalar(xnw[0][dd][:, 1:2], xnw[0][dd][:, 1:2],
                                                self.startmask[:], None, Alu.mult))
                self.wV(v)

        ln2(0)
        for j in range(len(mr_blocks)):
            if j + 1 < nb:
                ln2(j + 1)
            c0, c1 = mr_blocks[j]
            bw = c1 - c0
            bw64 = max(bw, 64)
            xj = xnw[j % 2]
            xj1 = xnw[(j + 1) % 2]
            l0 = c0 - ln_blocks[j][0]
            wid_in = ln_blocks[j][1] - c0
            for dd in range(ND):
                if wid_in >= bw64:
                    v.tensor_tensor(t_a[:, 0:bw64], xj[dd][:, l0:l0 + bw64],
                                    xj[dd][:, l0 - 1:l0 - 1 + bw64], Alu.subtract)
                elif wid_in >= bw:
                    wd = min(bw64, AW - l0)
                    v.tensor_tensor(t_a[:, 0:wd], xj[dd][:, l0:l0 + wd],
                                    xj[dd][:, l0 - 1:l0 - 1 + wd], Alu.subtract)
                else:
                    v.tensor_tensor(t_a[:, 0:wid_in], xj[dd][:, l0:l0 + wid_in],
                                    xj[dd][:, l0 - 1:l0 - 1 + wid_in], Alu.subtract)
                    self.sV.inc(v.tensor_tensor(t_a[:, wid_in:bw],
                                                xj1[dd][:, 0:bw - wid_in],
                                                xj[dd][:, l0 - 1 + wid_in:l0 - 1 + bw],
                                                Alu.subtract))
                    self.wV(v)
                sh0 = l0 - 1
                wd = min(bw64, AW - sh0)
                v.scalar_tensor_tensor(xk_b[dd][:, 0:wd], t_a[:, 0:wd],
                                       self.mk[:, dd:dd + 1], xj[dd][:, sh0:sh0 + wd],
                                       Alu.mult, Alu.add)
                v.scalar_tensor_tensor(xv_b[dd][:, 0:wd], t_a[:, 0:wd],
                                       self.mv[:, dd:dd + 1], xj[dd][:, sh0:sh0 + wd],
                                       Alu.mult, Alu.add)
                if len(xr_sts) >= 2:
                    v.wait_ge(self.sS2.h, xr_sts[-2])
                self.sV.inc(v.scalar_tensor_tensor(xr_b[dd % 2][:, 0:wd], t_a[:, 0:wd],
                                                   self.mr[:, dd:dd + 1],
                                                   xj[dd][:, sh0:sh0 + wd],
                                                   Alu.mult, Alu.add))
                nc.sync.wait_ge(self.sV.h, self.sV.n)
                self.st2(self.xr_st[dd][:, c0 - 2:c0 - 2 + bw], xr_b[dd % 2][:, 0:bw])
                xr_sts.append(self.sS2.n)
            self.wV(te)
            te.wait_ge(self.sL2.h, wk_ready)
            a.wait_ge(self.sS2.h, getattr(self, '_kv_stash_done', 0))
            kv_ev = self.sA.n
            for oc in range(ND):
                if oc >= 2:
                    te.wait_ge(self.sA.h, kv_ev + (oc - 1))
                for dd in range(ND):
                    mm = te.matmul(self.ps[0 + (oc % 2)][:, 0:bw],
                                   wbufA[dd][:, oc * 128:(oc + 1) * 128],
                                   xk_b[dd][:, 0:bw], start=(dd == 0), stop=(dd == ND - 1))
                self.sT.inc(mm)
                self.wT(a)
                self.sA.inc(a.copy(kwin[oc][:, 0:bw], self.ps[0 + (oc % 2)][:, 0:bw]))
            kv_ev = self.sA.n
            for oc in range(ND):
                if oc >= 2:
                    te.wait_ge(self.sA.h, kv_ev + (oc - 1))
                for dd in range(ND):
                    mm = te.matmul(self.ps[2 + (oc % 2)][:, 0:bw],
                                   wbufB[dd][:, oc * 128:(oc + 1) * 128],
                                   xv_b[dd][:, 0:bw], start=(dd == 0), stop=(dd == ND - 1))
                self.sT.inc(mm)
                self.wT(a)
                self.sA.inc(a.copy(vwin[oc][:, 0:bw], self.ps[2 + (oc % 2)][:, 0:bw]))
            self.wA(v)
            if j == 0:
                for oc in range(ND):
                    self.sV.inc(v.tensor_tensor(kwin[oc][:, 0:1], kwin[oc][:, 0:1],
                                                self.startneg[:], Alu.add))
                self.wV(v)
            nc.sync.wait_ge(self.sV.h, self.sV.n)
            nc.sync.wait_ge(self.sA.h, self.sA.n)
            for oc in range(ND):
                self.st2(self.kv_st[0, oc][:, c0 - 2:c0 - 2 + bw], kwin[oc][:, 0:bw])
                self.st2(self.kv_st[1, oc][:, c0 - 2:c0 - 2 + bw], vwin[oc][:, 0:bw])
            self._kv_stash_done = self.sS2.n
            if c0 < TC + 2:
                sw = min(c1, TC + 2) - c0
                l_off = c0 - 2
                self.wG(v)
                for oc in range(ND):
                    v.scalar_tensor_tensor(t_a[:, 0:sw], ramp[:, l_off:l_off + sw],
                                           self.absw[:, oc:oc + 1], kwin[oc][:, 0:sw],
                                           Alu.mult, Alu.add)
                    self.vf(v.tensor_reduce(kapw[:, oc:oc + 1], t_a[:, 0:sw], AX, Alu.max))
                    self.vf(v.tensor_scalar(kapn[:, 0:1], kapw[:, oc:oc + 1], -1.0,
                                            None, Alu.mult))
                    self.wV(a)
                    self.sA.inc(a.activation(t_a[:, 0:sw], t_a[:, 0:sw], Act.Exp,
                                             bias=kapn[:, 0:1], scale=1.0,
                                             accum_out=sb_w[:, oc:oc + 1]))
                    self.wA(v)
                    self.vf(v.scalar_tensor_tensor(t_a[:, 0:sw], t_a[:, 0:sw], 1.0,
                                                   vwin[oc][:, 0:sw], Alu.mult, Alu.mult,
                                                   accum_out=sa_w[:, oc:oc + 1]))
                if j == 0:
                    self.vf(v.tensor_copy(sumbuf[:, 0:8], sa_w[:]))
                    self.vf(v.tensor_copy(sumbuf[:, 8:16], sb_w[:]))
                    self.vf(v.tensor_copy(sumbuf[:, 16:24], kapw[:]))
                else:
                    self.vf(v.tensor_tensor(tmp8b[:], sumbuf[:, 16:24], kapw[:], Alu.max))
                    self.vf(v.tensor_tensor(tmp8c[:], sumbuf[:, 16:24], tmp8b[:],
                                            Alu.subtract))
                    self.vf(v.tensor_tensor(tmp8[:], kapw[:], tmp8b[:], Alu.subtract))
                    self.wV(a)
                    self.sA.inc(a.activation(tmp8c[:], tmp8c[:], Act.Exp, bias=0.0,
                                             scale=1.0))
                    self.af(a.activation(tmp8[:], tmp8[:], Act.Exp, bias=0.0, scale=1.0))
                    self.wA(v)
                    self.vf(v.tensor_tensor(sumbuf[:, 0:8], sumbuf[:, 0:8], tmp8c[:],
                                            Alu.mult))
                    self.vf(v.tensor_tensor(sumbuf[:, 8:16], sumbuf[:, 8:16], tmp8c[:],
                                            Alu.mult))
                    self.vf(v.scalar_tensor_tensor(tmp8c[:], sa_w[:], 1.0, tmp8[:],
                                                   Alu.mult, Alu.mult))
                    self.vf(v.tensor_tensor(sumbuf[:, 0:8], sumbuf[:, 0:8], tmp8c[:],
                                            Alu.add))
                    self.vf(v.scalar_tensor_tensor(tmp8c[:], sb_w[:], 1.0, tmp8[:],
                                                   Alu.mult, Alu.mult))
                    self.vf(v.tensor_tensor(sumbuf[:, 8:16], sumbuf[:, 8:16], tmp8c[:],
                                            Alu.add))
                    self.vf(v.tensor_copy(sumbuf[:, 16:24], tmp8b[:]))
        self.vf(v.scalar_tensor_tensor(sumbuf[:, 16:24], self.w_c[:], 2047.0,
                                       sumbuf[:, 16:24], Alu.mult, Alu.add))
        nc.sync.wait_ge(self.sV.h, self.sV.n)
        self.st(self.cc_in[:], sumbuf[:, 0:24])
        g.wait_ge(self.sS.h, self.sS.n)
        self.sC.inc(g.collective_compute(
            "AllGather", Alu.bypass, replica_groups=[[0, 1, 2, 3, 4, 5, 6, 7]],
            ins=[self.cc_in[:]], outs=[self.cc_out[:]]))
        nc.sync.wait_ge(self.sC.h, self.sC.n)
        for j in range(8):
            self.ld(aggb[:, j * 24:(j + 1) * 24], self.cc_out[j])
        self.wL(v)
        self.wL(g)

        def slot(jj, q):
            return aggb[:, jj * 24 + q * 8:jj * 24 + (q + 1) * 8]

        first_cand = True
        for b in range(2):
            self.wV(g)
            g.memset(ccs[:, 0:16], 0.0)
            self.gf(g.memset(ccs[:, 16:24], -1e30))
            self.wG(v)
            for cpos in range(4):
                jj = b * 4 + cpos
                if first_cand:
                    for q in range(3):
                        self.vf(v.tensor_scalar(self.sinp[:, q * 8:(q + 1) * 8],
                                                ccs[:, q * 8:(q + 1) * 8],
                                                self.onehot[:, jj:jj + 1], None, Alu.mult))
                    first_cand = False
                else:
                    for q in range(3):
                        self.vf(v.scalar_tensor_tensor(self.sinp[:, q * 8:(q + 1) * 8],
                                                       ccs[:, q * 8:(q + 1) * 8],
                                                       self.onehot[:, jj:jj + 1],
                                                       self.sinp[:, q * 8:(q + 1) * 8],
                                                       Alu.mult, Alu.add))
                if cpos < 3:
                    self.vf(v.scalar_tensor_tensor(tmp8[:], self.w_c[:], 2048.0,
                                                   ccs[:, 16:24], Alu.mult, Alu.add))
                    self.vf(v.tensor_tensor(tmp8b[:], tmp8[:], slot(jj, 2), Alu.max))
                    self.vf(v.tensor_tensor(tmp8c[:], tmp8[:], tmp8b[:], Alu.subtract))
                    self.vf(v.tensor_tensor(tmp8[:], slot(jj, 2), tmp8b[:], Alu.subtract))
                    self.wV(a)
                    self.af(a.activation(tmp8c[:], tmp8c[:], Act.Exp, bias=0.0, scale=1.0))
                    self.af(a.activation(tmp8[:], tmp8[:], Act.Exp, bias=0.0, scale=1.0))
                    self.wA(v)
                    self.vf(v.tensor_tensor(ccs[:, 0:8], ccs[:, 0:8], tmp8c[:], Alu.mult))
                    self.vf(v.tensor_tensor(ccs[:, 8:16], ccs[:, 8:16], tmp8c[:],
                                            Alu.mult))
                    self.vf(v.scalar_tensor_tensor(tmp8c[:], slot(jj, 0), 1.0, tmp8[:],
                                                   Alu.mult, Alu.mult))
                    self.vf(v.tensor_tensor(ccs[:, 0:8], ccs[:, 0:8], tmp8c[:], Alu.add))
                    self.vf(v.scalar_tensor_tensor(tmp8c[:], slot(jj, 1), 1.0, tmp8[:],
                                                   Alu.mult, Alu.mult))
                    self.vf(v.tensor_tensor(ccs[:, 8:16], ccs[:, 8:16], tmp8c[:], Alu.add))
                    self.vf(v.tensor_copy(ccs[:, 16:24], tmp8b[:]))

        self.wT(g)
        for i in range(ND):
            self.ld2(wbufA[i][:], self.wrt[i])
            self.ld2(wbufB[i][:], self.wot[i])
        wr_ready = self.sL2.n

        ar.cursor = PASS2_OFF
        ppb = [ar.take([P, AW + 8], F32) for _ in range(ND)]
        aab = [ar.take([P, AW + 8], F32) for _ in range(ND)]
        bbb = [ar.take([P, AW + 8], F32) for _ in range(ND)]
        wz = [ar.take([P, AW], F32) for _ in range(ND)]
        T1 = [ar.take([P, AW], F32) for _ in range(ND)]
        T2 = [ar.take([P, AW], F32) for _ in range(ND)]
        ar.cursor = self.R_MED
        kw2 = [ar.take([P, AW], F32) for _ in range(ND)]
        vw2 = [ar.take([P, AW], F32) for _ in range(ND)]
        rw2 = [ar.take([P, AW], F32) for _ in range(ND)]
        ry2 = [ar.take([P, AW], BF16) for _ in range(ND)]
        xr2 = [ar.take([P, AW], BF16) for _ in range(ND)]
        x1o = [ar.take([P, AW], F32) for _ in range(2)]
        T3 = [ar.take([P, AW], F32) for _ in range(ND)]
        T4 = [ar.take([P, AW], F32) for _ in range(ND)]

        self.wV(g)
        self.wA(g)
        for c in range(ND):
            ig = g.memset(wz[c][:], 0.0)
        self.sG.inc(ig)
        self.wG(v)
        for c in range(ND):
            self.sV.inc(v.tensor_scalar(wz[c][:], wz[c][:], self.w_c[:, c:c + 1], None,
                                        Alu.add))
        for c in range(ND):
            self.sV.inc(v.tensor_copy(aab[c][:, 0:1], self.sinp[:, c:c + 1]))
            self.sV.inc(v.tensor_copy(bbb[c][:, 0:1], self.sinp[:, 8 + c:9 + c]))
            self.sV.inc(v.tensor_copy(ppb[c][:, 0:1], self.sinp[:, 16 + c:17 + c]))
        self.wV(v)

        self.syncS()
        p2a_prev = [self.sA.n] * 2
        p2v_prev = [self.sV.n] * 2
        p2g_prev = [self.sG.n] * 2
        p2_ranges = _blocks(2, W - 1, AW)
        nc.sync.wait_ge(self.sV.h, self.sV.n)
        nc.sync.wait_ge(self.sT.h, self.sT.n)
        c0_, c1_ = p2_ranges[0]
        for oc in range(ND):
            self.ld(xr2[oc][:, 0:c1_ - c0_], self.xr_st[oc][:, c0_ - 2:c1_ - 2])
        xr_ready = self.sL.n
        for oc in range(ND):
            self.ld(kw2[oc][:, 0:c1_ - c0_], self.kv_st[0, oc][:, c0_ - 2:c1_ - 2])
            self.ld(vw2[oc][:, 0:c1_ - c0_], self.kv_st[1, oc][:, c0_ - 2:c1_ - 2])
        kv_ready = self.sL.n
        prev_sig_end = 0
        for j, (c0, c1) in enumerate(p2_ranges):
            bw = c1 - c0
            nrw = bw < 64

            def vI(instr):
                if nrw:
                    self.sV.inc(instr)
                return instr

            def vE(instr):
                self.sV.inc(instr)
                if nrw:
                    v.wait_ge(self.sV.h, self.sV.n)
                return instr

            te.wait_ge(self.sL2.h, wr_ready)
            te.wait_ge(self.sL.h, xr_ready)
            r_ev = self.sA.n
            for oc in range(ND):
                if oc >= 2:
                    te.wait_ge(self.sA.h, r_ev + (oc - 1))
                elif prev_sig_end:
                    te.wait_ge(self.sA.h, prev_sig_end)
                for dd in range(ND):
                    mm = te.matmul(self.ps[0 + (oc % 2)][:, 0:bw],
                                   wbufA[dd][:, oc * 128:(oc + 1) * 128],
                                   xr2[dd][:, 0:bw], start=(dd == 0), stop=(dd == ND - 1))
                self.sT.inc(mm)
                self.wT(a)
                self.sA.inc(a.activation(rw2[oc][:, 0:bw], self.ps[0 + (oc % 2)][:, 0:bw],
                                         Act.Sigmoid, bias=0.0, scale=1.0))
            sig_done = self.sA.n
            prev_sig_end = sig_done
            rmm_end = self.sT.n
            if j + 1 < len(p2_ranges):
                n0, n1 = p2_ranges[j + 1]
                nc.sync.wait_ge(self.sT.h, rmm_end)
                for oc in range(ND):
                    self.ld(xr2[oc][:, 0:n1 - n0], self.xr_st[oc][:, n0 - 2:n1 - 2])
                xr_ready = self.sL.n
            v.wait_ge(self.sL.h, kv_ready)
            g.wait_ge(self.sL.h, kv_ready)
            H2 = [range(0, 4), range(4, 8)]
            rec = {}

            def gE(instrs):
                for i_, ig in enumerate(instrs):
                    if nrw or i_ == len(instrs) - 1:
                        self.sG.inc(ig)

            def aE(instrs):
                for i_, ia in enumerate(instrs):
                    if nrw or i_ == len(instrs) - 1:
                        self.sA.inc(ia)

            for h, chans in enumerate(H2):
                for c in chans:
                    vI(v.tensor_tensor_scan(ppb[c][:, 1:1 + bw], wz[c][:, 0:bw],
                                            kw2[c][:, 0:bw], ppb[c][:, 0:1],
                                            Alu.add, Alu.max))
                vE(v.engine_nop())
            for h, chans in enumerate(H2):
                v.wait_ge(self.sA.h, p2a_prev[h])
                v.wait_ge(self.sG.h, p2g_prev[h])
                for c in chans:
                    vI(v.tensor_tensor(T1[c][:, 0:bw], ppb[c][:, 0:bw],
                                       ppb[c][:, 1:1 + bw], Alu.subtract))
                    vI(v.tensor_tensor(T2[c][:, 0:bw], kw2[c][:, 0:bw],
                                       ppb[c][:, 1:1 + bw], Alu.subtract))
                vE(v.engine_nop())
                rec['s2', h] = self.sV.n
            for h, chans in enumerate(H2):
                a.wait_ge(self.sV.h, rec['s2', h])
                ii = []
                for c in chans:
                    ii.append(a.activation(T1[c][:, 0:bw], T1[c][:, 0:bw], Act.Exp,
                                           bias=self.w_c[:, c:c + 1], scale=1.0))
                    ii.append(a.activation(T2[c][:, 0:bw], T2[c][:, 0:bw], Act.Exp,
                                           bias=0.0, scale=1.0))
                aE(ii)
                rec['s3', h] = self.sA.n
            for h, chans in enumerate(H2):
                g.wait_ge(self.sA.h, rec['s3', h])
                g.wait_ge(self.sV.h, p2v_prev[h])
                gE([g.tensor_tensor(T3[c][:, 0:bw], vw2[c][:, 0:bw], T2[c][:, 0:bw],
                                    Alu.mult) for c in chans])
                rec['s4', h] = self.sG.n
            for h, chans in enumerate(H2):
                v.wait_ge(self.sG.h, rec['s4', h])
                for c in chans:
                    vI(v.tensor_tensor_scan(aab[c][:, 1:1 + bw], T1[c][:, 0:bw],
                                            T3[c][:, 0:bw], aab[c][:, 0:1],
                                            Alu.mult, Alu.add))
                    vI(v.tensor_tensor_scan(bbb[c][:, 1:1 + bw], T1[c][:, 0:bw],
                                            T2[c][:, 0:bw], bbb[c][:, 0:1],
                                            Alu.mult, Alu.add))
                vE(v.engine_nop())
            for h, chans in enumerate(H2):
                for c in chans:
                    vI(v.tensor_scalar(T3[c][:, 0:bw], ppb[c][:, 0:bw],
                                       self.u_c[:, c:c + 1], None, Alu.add))
                vE(v.engine_nop())
                for c in chans:
                    vI(v.tensor_tensor(T4[c][:, 0:bw], T3[c][:, 0:bw], kw2[c][:, 0:bw],
                                       Alu.max))
                vE(v.engine_nop())
                for c in chans:
                    vI(v.tensor_tensor(T3[c][:, 0:bw], T3[c][:, 0:bw], T4[c][:, 0:bw],
                                       Alu.subtract))
                    vI(v.tensor_tensor(T4[c][:, 0:bw], kw2[c][:, 0:bw], T4[c][:, 0:bw],
                                       Alu.subtract))
                vE(v.engine_nop())
                rec['s9', h] = self.sV.n
            for h, chans in enumerate(H2):
                a.wait_ge(self.sV.h, rec['s9', h])
                ii = []
                for c in chans:
                    ii.append(a.activation(T3[c][:, 0:bw], T3[c][:, 0:bw], Act.Exp,
                                           bias=0.0, scale=1.0))
                    ii.append(a.activation(T4[c][:, 0:bw], T4[c][:, 0:bw], Act.Exp,
                                           bias=0.0, scale=1.0))
                aE(ii)
                rec['s10', h] = self.sA.n
            for h, chans in enumerate(H2):
                g.wait_ge(self.sA.h, rec['s10', h])
                g.wait_ge(self.sV.h, rec['s9', h])
                gE([g.tensor_tensor(T1[c][:, 0:bw], vw2[c][:, 0:bw], T4[c][:, 0:bw],
                                    Alu.mult) for c in chans])
                rec['s11', h] = self.sG.n
            for h, chans in enumerate(H2):
                v.wait_ge(self.sA.h, rec['s10', h])
                for c in chans:
                    vI(v.tensor_tensor(T2[c][:, 0:bw], aab[c][:, 0:bw], T3[c][:, 0:bw],
                                       Alu.mult))
                vE(v.engine_nop())
                rec['s12a', h] = self.sV.n
            for h, chans in enumerate(H2):
                g.wait_ge(self.sV.h, rec['s12a', h])
                gE([g.tensor_tensor(T2[c][:, 0:bw], T2[c][:, 0:bw], T1[c][:, 0:bw],
                                    Alu.add) for c in chans])
                gE([g.tensor_tensor(T1[c][:, 0:bw], bbb[c][:, 0:bw], T3[c][:, 0:bw],
                                    Alu.mult) for c in chans])
                rec['s13', h] = self.sG.n
            for h, chans in enumerate(H2):
                v.wait_ge(self.sG.h, rec['s13', h])
                for c in chans:
                    vI(v.scalar_tensor_tensor(T1[c][:, 0:bw], T4[c][:, 0:bw], 1e-6,
                                              T1[c][:, 0:bw], Alu.add, Alu.add))
                vE(v.engine_nop())
                for c in chans:
                    vI(v.reciprocal(T1[c][:, 0:bw], T1[c][:, 0:bw]))
                vE(v.engine_nop())
                rec['s15', h] = self.sV.n
            for h, chans in enumerate(H2):
                g.wait_ge(self.sV.h, rec['s15', h])
                gE([g.tensor_tensor(T2[c][:, 0:bw], T2[c][:, 0:bw], T1[c][:, 0:bw],
                                    Alu.mult) for c in chans])
                rec['s16', h] = self.sG.n
            for h, chans in enumerate(H2):
                v.wait_ge(self.sA.h, sig_done)
                v.wait_ge(self.sG.h, rec['s16', h])
                for c in chans:
                    vI(v.tensor_tensor(ry2[c][:, 0:bw], rw2[c][:, 0:bw], T2[c][:, 0:bw],
                                       Alu.mult))
                vE(v.engine_nop())
            for c in range(ND):
                self.sV.inc(v.tensor_copy(ppb[c][:, 0:1], ppb[c][:, bw:bw + 1]))
                self.sV.inc(v.tensor_copy(aab[c][:, 0:1], aab[c][:, bw:bw + 1]))
                self.sV.inc(v.tensor_copy(bbb[c][:, 0:1], bbb[c][:, bw:bw + 1]))
            self.wV(v)
            p2a_prev = [rec['s10', 0], rec['s10', 1]]
            p2v_prev = [rec['s12a', 0], rec['s12a', 1]]
            p2g_prev = [rec['s16', 0], rec['s16', 1]]
            s9_done = rec['s9', 1]
            s11_done = rec['s11', 1]
            mm_base = self.sT.n
            ost_base = self.sS2.n
            ld_base = self.sL.n
            add_base = self.sV.n
            for od in range(ND):
                pso = self.ps[4 + (od % 2)]
                self.wV(te)
                for oc in range(ND):
                    mm = te.matmul(pso[:, 0:bw], wbufB[oc][:, od * 128:(od + 1) * 128],
                                   ry2[oc][:, 0:bw], start=(oc == 0), stop=(oc == ND - 1))
                self.sT.inc(mm)
                ob = x1o[od % 2]
                if od >= 2:
                    nc.sync.wait_ge(self.sS2.h, ost_base + 16 * (od - 1))
                else:
                    prev = getattr(self, '_wo_prev_end', 0)
                    if prev:
                        nc.sync.wait_ge(self.sS2.h, prev - 16 * (1 - od))
                self.ld(ob[:, 0:bw], self.xres_a[od][:, c0:c1])
                v.wait_ge(self.sT.h, mm_base + od + 1)
                v.wait_ge(self.sL.h, ld_base + 16 * (od + 1))
                self.sV.inc(v.tensor_tensor(ob[:, 0:bw], ob[:, 0:bw], pso[:, 0:bw],
                                            Alu.add))
                nc.sync.wait_ge(self.sV.h, add_base + od + 1)
                self.st2(self.xres_b[od][:, c0:c1], ob[:, 0:bw])
            self._wo_prev_end = self.sS2.n
            if j + 1 < len(p2_ranges):
                n0, n1 = p2_ranges[j + 1]
                nc.sync.wait_ge(self.sV.h, s9_done)
                nc.sync.wait_ge(self.sG.h, s11_done)
                for oc in range(ND):
                    self.ld(kw2[oc][:, 0:n1 - n0], self.kv_st[0, oc][:, n0 - 2:n1 - 2])
                    self.ld(vw2[oc][:, 0:n1 - n0], self.kv_st[1, oc][:, n0 - 2:n1 - 2])
                kv_ready = self.sL.n

    def build(self):
        nc = self.nc
        self.p0()
        self.pw()
        self.ffn(0, W, self.xw, self.xres_a, final=False)
        self.attn()
        nc.sync.wait_ge(self.sT.h, self.sT.n)
        self.syncS()
        for i in range(ND):
            self.ld(self.w1q[i][:], self.w1q_st[i])
        nc.tensor.wait_ge(self.sL.h, self.sL.n)
        self.ffn(2, W - 1, self.xres_b, self.out, final=True)
        nc.sync.wait_ge(self.sO.h, self.sO.n)
        nc.vector.wait_ge(self.sO.h, self.sO.n)
        nc.finalize()
        return nc


_CACHE = {}


def _build(shadow):
    key = bool(shadow)
    if key not in _CACHE:
        _CACHE[key] = Builder(shadow).build()
    return _CACHE[key]


def kernel(**inputs):
    x = np.asarray(inputs["L0_attn_x"], np.float32)
    td = np.asarray(inputs["L0_attn_time_decay"], np.float32)
    u = np.asarray(inputs["L0_attn_time_first"], np.float32)
    mk = np.asarray(inputs["L0_attn_time_mix_k"], np.float32).reshape(-1)
    mv = np.asarray(inputs["L0_attn_time_mix_v"], np.float32).reshape(-1)
    mr = np.asarray(inputs["L0_attn_time_mix_r"], np.float32).reshape(-1)
    Wk = np.asarray(inputs["L0_attn_Wk"], np.float32)
    Wv = np.asarray(inputs["L0_attn_Wv"], np.float32)
    Wr = np.asarray(inputs["L0_attn_Wr"], np.float32)
    Wo = np.asarray(inputs["L0_attn_Wo"], np.float32)
    w1 = np.asarray(inputs["L0_ffn_w1"], np.float32)
    cw = np.asarray(inputs["L0_ffn_conv_w"], np.float32)
    cb = np.asarray(inputs["L0_ffn_conv_b"], np.float32)
    w2 = np.asarray(inputs["L0_ffn_w2"], np.float32)

    def fm(vec, nch):
        return np.ascontiguousarray(vec.reshape(nch, P).T)

    shadow = os.environ.get("BASS_SIM_FP64") == "1"
    nc = _build(shadow)

    xfm = np.ascontiguousarray(np.transpose(x, (0, 2, 1)))
    T = x.shape[1]
    common = {
        "w1t": np.ascontiguousarray(w1.T).reshape(ND, P, HD),
        "w2t": np.ascontiguousarray(w2.T).reshape(NH, P, C),
        "wkt": np.ascontiguousarray(Wk.T).reshape(ND, P, C),
        "wvt": np.ascontiguousarray(Wv.T).reshape(ND, P, C),
        "wrt": np.ascontiguousarray(Wr.T).reshape(ND, P, C),
        "wot": np.ascontiguousarray(Wo.T).reshape(ND, P, C),
        "td_t": fm(td, ND), "u_t": fm(u, ND),
        "mk_t": fm(mk, ND), "mv_t": fm(mv, ND), "mr_t": fm(mr, ND),
        "wl_t": fm(cw[:, 0, 0].copy(), NH), "wc_t": fm(cw[:, 0, 1].copy(), NH),
        "wr2_t": fm(cw[:, 0, 2].copy(), NH), "cb_t": fm(cb, NH),
        "iden_in": np.eye(P, dtype=np.float32),
    }
    in_maps = []
    for i in range(8):
        b, c = i // 4, i % 4
        s, e = c * TC, (c + 1) * TC
        win = np.zeros((C, W), np.float32)
        glo, ghi = s - 3, e + 2
        slo, shi = max(glo, 0), min(ghi, T)
        win[:, slo - glo:shi - glo] = xfm[b, :, slo:shi]
        m = dict(common)
        m["xw"] = np.ascontiguousarray(win.reshape(ND, P, W))
        m["startneg_in"] = np.full((P, 1), -1e30 if c == 0 else 0.0, np.float32)
        m["startmask_in"] = np.full((P, 1), 0.0 if c == 0 else 1.0, np.float32)
        m["endmask_in"] = np.full((P, 1), 0.0 if c == 3 else 1.0, np.float32)
        oh = np.zeros((P, 8), np.float32)
        oh[:, i] = 1.0
        m["onehot_in"] = oh
        in_maps.append(m)

    res = run_bass_kernel_spmd(nc, in_maps, core_ids=list(range(8)))
    out = np.zeros((2, C, T), np.float32)
    for i in range(8):
        b, c = i // 4, i % 4
        o = np.asarray(res.results[i]["out"]).reshape(C, TC)
        out[b, :, c * TC:(c + 1) * TC] = o
    return np.ascontiguousarray(np.transpose(out, (0, 2, 1)))


# version 11
# speedup vs baseline: 1.3990x; 1.0226x over previous
import os
import numpy as np
import concourse.bass as bass
import concourse.mybir as mybir
from concourse.bass_utils import run_bass_kernel_spmd

F32 = mybir.dt.float32
BF16 = mybir.dt.bfloat16
I8 = mybir.dt.int8
I16 = mybir.dt.int16
Alu = mybir.AluOpType
Act = mybir.ActivationFunctionType
AX = mybir.AxisListType.X

C = 1024
HD = 4096
ND = 8
NH = 32
W = 2053
TC = 2048
P = 128

SB_BASE = 16512
SB_TOP = 229344
DEBUG_CAPTURE = False
DBG_BLOCK = 1


def _blocks(lo, hi, step=512):
    out = []
    c = lo
    while c < hi:
        out.append((c, min(c + step, hi)))
        c += step
    return out


class Sem:
    def __init__(self, nc, name):
        self.h = nc.alloc_semaphore(name)
        self.n = 0

    def inc(self, instr, k=1):
        instr.then_inc(self.h, k)
        self.n += k
        return instr


class Arena:
    def __init__(self, nc):
        self.nc = nc
        self.cursor = SB_BASE
        self.uid = 0

    def take(self, shape, dtype, at=None):
        per_part = int(np.prod(shape[1:])) * mybir.dt.size(dtype)
        off = self.cursor if at is None else at
        off = (off + 31) & ~31
        self.uid += 1
        t = self.nc.alloc_sbuf_tensor_at(f"sb{self.uid}", list(shape), dtype, offset=off)
        if at is None:
            self.cursor = off + per_part
        assert off + per_part <= SB_TOP, (off, per_part, shape)
        return t


class Builder:
    def __init__(self, shadow):
        self.shadow = shadow
        nc = bass.Bass("TRN2", target_bir_lowering=False, detect_race_conditions=False)
        self.nc = nc
        self.ar = Arena(nc)
        d = nc.dram_tensor
        self.xw = d("xw", [ND, P, W], F32, kind="ExternalInput")
        self.w1t = d("w1t", [ND, P, HD], F32, kind="ExternalInput")
        self.w2t = d("w2t", [NH, P, C], F32, kind="ExternalInput")
        self.wkt = d("wkt", [ND, P, C], F32, kind="ExternalInput")
        self.wvt = d("wvt", [ND, P, C], F32, kind="ExternalInput")
        self.wrt = d("wrt", [ND, P, C], F32, kind="ExternalInput")
        self.wot = d("wot", [ND, P, C], F32, kind="ExternalInput")
        self.td_t = d("td_t", [P, ND], F32, kind="ExternalInput")
        self.u_t = d("u_t", [P, ND], F32, kind="ExternalInput")
        self.mk_t = d("mk_t", [P, ND], F32, kind="ExternalInput")
        self.mv_t = d("mv_t", [P, ND], F32, kind="ExternalInput")
        self.mr_t = d("mr_t", [P, ND], F32, kind="ExternalInput")
        self.wl_t = d("wl_t", [P, NH], F32, kind="ExternalInput")
        self.wc_t = d("wc_t", [P, NH], F32, kind="ExternalInput")
        self.wr2_t = d("wr2_t", [P, NH], F32, kind="ExternalInput")
        self.cb_t = d("cb_t", [P, NH], F32, kind="ExternalInput")
        self.iden_in = d("iden_in", [P, P], F32, kind="ExternalInput")
        self.startneg_in = d("startneg_in", [P, 1], F32, kind="ExternalInput")
        self.startmask_in = d("startmask_in", [P, 1], F32, kind="ExternalInput")
        self.endmask_in = d("endmask_in", [P, 1], F32, kind="ExternalInput")
        self.onehot_in = d("onehot_in", [P, 8], F32, kind="ExternalInput")
        self.out = d("out", [ND, P, TC], F32, kind="ExternalOutput")

        self.xres_a = d("xres_a", [ND, P, W], F32, kind="Internal")
        self.xres_b = d("xres_b", [ND, P, W], F32, kind="Internal")
        self.w1q_st = d("w1q_st", [ND, P, HD], BF16, kind="Internal")
        self.w2q_st = d("w2q_st", [NH, P, C], BF16, kind="Internal")
        self.xr_st = d("xr_st", [ND, P, TC + 2], BF16, kind="Internal")
        self.n2_st = d("n2_st", [NH, P, W + 3], I8, kind="Internal")
        self.kv_st = d("kv_st", [2, ND, P, TC + 8], F32, kind="Internal")
        self.cc_in = d("cc_in", [P, 24], F32, kind="Internal")
        self.cc_out = d("cc_out", [8, P, 24], F32, kind="Internal", addr_space="Local")
        if DEBUG_CAPTURE:
            self.dbg_h = d("dbg_h", [NH, P, 520], F32, kind="Internal")
            self.dbg_g = d("dbg_g", [NH, P, 512], F32, kind="Internal")
            self.dbg_xq = d("dbg_xq", [ND, P, 512], BF16, kind="Internal")

        self.sL = Sem(nc, "sL")
        self.sL2 = Sem(nc, "sL2")
        self.sS = Sem(nc, "sS")
        self.sS2 = Sem(nc, "sS2")
        self.sO = Sem(nc, "sO")
        self.sT = Sem(nc, "sT")
        self.sA = Sem(nc, "sA")
        self.sV = Sem(nc, "sV")
        self.sG = Sem(nc, "sG")
        self.sC = Sem(nc, "sC")

        ar = self.ar
        self.iden = ar.take([P, P], F32)
        self.ones_col = ar.take([P, 1], F32)
        self.ones_row = ar.take([1, P], F32)
        self.td = ar.take([P, ND], F32)
        self.u_c = ar.take([P, ND], F32)
        self.w_c = ar.take([P, ND], F32)
        self.absw = ar.take([P, ND], F32)
        self.mk = ar.take([P, ND], F32)
        self.mv = ar.take([P, ND], F32)
        self.mr = ar.take([P, ND], F32)
        self.wl = ar.take([P, NH], F32)
        self.wcv = ar.take([P, NH], F32)
        self.wr2 = ar.take([P, NH], F32)
        self.cb = ar.take([P, NH], F32)
        self.wlm = ar.take([P, NH], F32)
        self.wcm = ar.take([P, NH], F32)
        self.wrm = ar.take([P, NH], F32)
        self.startneg = ar.take([P, 1], F32)
        self.startmask = ar.take([P, 1], F32)
        self.endmask = ar.take([P, 1], F32)
        self.onehot = ar.take([P, 8], F32)
        self.mu1c = ar.take([P, 1], F32)
        self.swcol = ar.take([P, 1], F32)
        self.scal = ar.take([1, 8], F32)
        self.eps_col = ar.take([P, 1], F32)
        self.mxcol = ar.take([P, 4], F32)
        self.s2col = ar.take([P, 4], F32)
        self.sinp = ar.take([P, 24], F32)
        self.rowA = ar.take([1, 512], F32)
        self.rowB = ar.take([1, 512], F32)
        self.rowC = ar.take([1, 512], F32)
        self.rowD = ar.take([1, 512], F32)
        self.row_s2 = ar.take([1, 512], F32)
        self.row_s2i = ar.take([1, W + 64], F32)
        self.t_xs = ar.take([P, 512], F32)
        self.t_qs = ar.take([P, 512], F32)
        self.t_i16 = ar.take([P, 512], I16)
        self.t_cg = ar.take([P, 512], F32) if shadow else None
        self.PERS_END = ar.cursor
        self.ps = [nc.alloc_psum_tensor(f"ps{i}", [P, 512], F32) for i in range(8)]
        self.R_W = (self.PERS_END + 31) & ~31
        self.R_BIG = self.R_W + 65568
        self.R_MED = self.R_BIG + 65568
        assert self.R_MED + 45 * 1024 < SB_TOP, (self.R_MED, SB_TOP)

    def ld(self, out_ap, in_ap):
        return self.sL.inc(self.nc.sync.dma_start(out_ap, in_ap), 16)

    def ld2(self, out_ap, in_ap):
        return self.sL2.inc(self.nc.gpsimd.dma_start(out_ap, in_ap), 16)

    def st(self, out_ap, in_ap, final=False):
        sem = self.sO if final else self.sS
        return sem.inc(self.nc.sync.dma_start(out_ap, in_ap), 16)

    def st2(self, out_ap, in_ap):
        return self.sS2.inc(self.nc.sync.dma_start(out_ap, in_ap), 16)

    def wA(self, eng):
        eng.wait_ge(self.sA.h, self.sA.n)

    def wV(self, eng):
        eng.wait_ge(self.sV.h, self.sV.n)

    def wG(self, eng):
        eng.wait_ge(self.sG.h, self.sG.n)

    def wT(self, eng):
        eng.wait_ge(self.sT.h, self.sT.n)

    def syncS(self):
        self.nc.sync.wait_ge(self.sS.h, self.sS.n)
        if self.sS2.n:
            self.nc.sync.wait_ge(self.sS2.h, self.sS2.n)

    def wL(self, eng):
        eng.wait_ge(self.sL.h, self.sL.n)
        if self.sL2.n:
            eng.wait_ge(self.sL2.h, self.sL2.n)

    def wS(self, eng):
        eng.wait_ge(self.sS.h, self.sS.n)
        if self.sS2.n:
            eng.wait_ge(self.sS2.h, self.sS2.n)

    def vf(self, instr):
        self.sV.inc(instr)
        self.nc.vector.wait_ge(self.sV.h, self.sV.n)
        return instr

    def af(self, instr):
        self.sA.inc(instr)
        self.nc.scalar.wait_ge(self.sA.h, self.sA.n)
        return instr

    def gf(self, instr):
        self.sG.inc(instr)
        self.nc.gpsimd.wait_ge(self.sG.h, self.sG.n)
        return instr

    def p0(self):
        nc = self.nc
        v, a, g = nc.vector, nc.scalar, nc.gpsimd
        self.ld(self.iden[:], self.iden_in[:])
        self.ld(self.td[:], self.td_t[:])
        self.ld(self.u_c[:], self.u_t[:])
        self.ld(self.mk[:], self.mk_t[:])
        self.ld(self.mv[:], self.mv_t[:])
        self.ld(self.mr[:], self.mr_t[:])
        self.ld(self.wl[:], self.wl_t[:])
        self.ld(self.wcv[:], self.wc_t[:])
        self.ld(self.wr2[:], self.wr2_t[:])
        self.ld(self.cb[:], self.cb_t[:])
        self.ld(self.startneg[:], self.startneg_in[:])
        self.ld(self.startmask[:], self.startmask_in[:])
        self.ld(self.endmask[:], self.endmask_in[:])
        self.ld(self.onehot[:], self.onehot_in[:])
        g.memset(self.ones_col[:], 1.0)
        g.memset(self.ones_row[:], 1.0)
        g.memset(self.mxcol[:], 0.0)
        g.memset(self.s2col[:], 0.0)
        g.memset(self.row_s2i[:], 1.0)
        self.gf(g.memset(self.eps_col[:], 1e-5))
        self.wL(a)
        self.af(a.activation(self.absw[:], self.td[:], Act.Exp, bias=0.0, scale=1.0))
        self.wA(v)
        self.vf(v.tensor_scalar(self.w_c[:], self.absw[:], -1.0, None, Alu.mult))

    def ternarize(self, src_dram, nchunks, width, wq_dst, st_dram, mu_slot,
                  region=None, use_ld2=False):
        nc, ar = self.nc, self.ar
        v, a, te = nc.vector, nc.scalar, nc.tensor
        ar.cursor = self.R_BIG if region is None else region
        ldf = self.ld2 if use_ld2 else self.ld
        lsem = self.sL2 if use_ld2 else self.sL
        stage = [ar.take([P, width], F32) for _ in range(2)]
        junk = ar.take([P, width], F32)
        i16 = ar.take([P, width], I16)
        qstage = [ar.take([P, width], BF16) for _ in range(2)] if wq_dst is None else None
        abscol = ar.take([P, nchunks + 4], F32)
        acc_col = ar.take([P, 4], F32)

        base = lsem.n
        ldf(stage[0][:], src_dram[0])
        if nchunks > 1:
            ldf(stage[1][:], src_dram[1])
        for i in range(nchunks):
            a.wait_ge(lsem.h, base + 16 * (i + 1))
            self.sA.inc(a.activation(junk[:], stage[i % 2][:], Act.Abs, bias=0.0,
                                     scale=1.0, accum_out=abscol[:, i:i + 1]))
            if i + 2 < nchunks:
                if use_ld2:
                    nc.gpsimd.wait_ge(self.sA.h, self.sA.n)
                else:
                    nc.sync.wait_ge(self.sA.h, self.sA.n)
                ldf(stage[i % 2][:], src_dram[i + 2])
        self.wA(v)
        self.vf(v.tensor_reduce(acc_col[:, 0:1], abscol[:, 0:nchunks], AX, Alu.add))
        self.wV(te)
        self.wG(te)
        self.sT.inc(te.matmul(self.ps[0][0:1, 0:1], self.ones_col[:], acc_col[:, 0:1],
                              start=True, stop=True))
        self.wT(v)
        inv_n = 1.0 / float(nchunks * P * width)
        self.vf(v.tensor_scalar(self.scal[:, mu_slot + 1:mu_slot + 2], self.ps[0][0:1, 0:1],
                                inv_n, 1e-5, Alu.mult, Alu.max))
        self.vf(v.reciprocal(self.scal[:, mu_slot:mu_slot + 1],
                             self.scal[:, mu_slot + 1:mu_slot + 2]))
        self.vf(v.reciprocal(self.scal[:, mu_slot + 1:mu_slot + 2],
                             self.scal[:, mu_slot:mu_slot + 1]))
        self.wV(te)
        self.sT.inc(te.matmul(self.ps[1][:, 0:1], self.ones_row[:],
                              self.scal[:, mu_slot:mu_slot + 1], start=True, stop=True))
        self.wT(v)
        self.vf(v.tensor_copy(self.swcol[:], self.ps[1][:, 0:1]))

        base = lsem.n
        ldf(stage[0][:], src_dram[0])
        if nchunks > 1:
            ldf(stage[1][:], src_dram[1])
        st_base = self.sS2.n
        for i in range(nchunks):
            v.wait_ge(lsem.h, base + 16 * (i + 1))
            dst = wq_dst[i][:] if wq_dst is not None else qstage[i % 2][:]
            if wq_dst is None and i >= 2:
                v.wait_ge(self.sS2.h, st_base + 16 * (i - 1))
            v.tensor_scalar(junk[:], stage[i % 2][:], self.swcol[:], None, Alu.mult)
            if not self.shadow:
                v.tensor_scalar(i16[:], junk[:], 0.0, None, Alu.add)
                self.sV.inc(v.tensor_scalar(dst, i16[:], 1.0, -1.0, Alu.min, Alu.max))
            else:
                v.tensor_scalar(junk[:], junk[:], 256.5, None, Alu.add)
                v.tensor_scalar(i16[:], junk[:], 0.0, None, Alu.add)
                cg = stage[i % 2]
                v.tensor_tensor(cg[:], i16[:], junk[:], Alu.is_gt)
                v.scalar_tensor_tensor(junk[:], i16[:], 256.0, cg[:], Alu.subtract,
                                       Alu.subtract)
                self.sV.inc(v.tensor_scalar(dst, junk[:], 1.0, -1.0, Alu.min, Alu.max))
            nc.sync.wait_ge(self.sV.h, self.sV.n)
            self.st2(st_dram[i], dst)
            if i + 2 < nchunks:
                if use_ld2:
                    nc.gpsimd.wait_ge(self.sV.h, self.sV.n)
                ldf(stage[i % 2][:], src_dram[i + 2])

    def pw(self):
        nc = self.nc
        v, te = nc.vector, nc.tensor
        self.w1q = [self.ar.take([P, HD], BF16, at=self.R_W + i * HD * 2) for i in range(ND)]
        self.w2q = [self.ar.take([P, C], BF16, at=self.R_W + i * C * 2) for i in range(NH)]
        self.ternarize(self.w1t, ND, HD, self.w1q, self.w1q_st, mu_slot=0)
        self.ternarize(self.w2t, NH, C, None, self.w2q_st, mu_slot=2,
                       region=self.R_MED, use_ld2=True)
        self.wV(te)
        self.sT.inc(te.matmul(self.ps[2][:, 0:1], self.ones_row[:], self.scal[:, 1:2],
                              start=True, stop=True))
        self.wT(v)
        self.vf(v.tensor_copy(self.mu1c[:], self.ps[2][:, 0:1]))
        self.vf(v.tensor_scalar(self.wlm[:], self.wl[:], self.mu1c[:], None, Alu.mult))
        self.vf(v.tensor_scalar(self.wcm[:], self.wcv[:], self.mu1c[:], None, Alu.mult))
        self.vf(v.tensor_scalar(self.wrm[:], self.wr2[:], self.mu1c[:], None, Alu.mult))
        self.vf(v.tensor_scalar(self.scal[:, 4:5], self.scal[:, 3:4], 0.5, None, Alu.mult))

    def ln_stats(self, xblk, sq_strips, bw, ps_a, ps_b, act_guard=False):
        nc = self.nc
        v, a, te, g = nc.vector, nc.scalar, nc.tensor, nc.gpsimd
        w64 = max(bw, 64)
        self.wL(a)
        self.wV(a)
        self.wG(a)
        for dd in range(ND):
            ins = a.activation(sq_strips[dd][:, 0:w64], xblk[dd][:, 0:w64], Act.Square,
                               bias=0.0, scale=1.0)
        self.sA.inc(ins)
        self.wA(v)
        self.wA(g)
        self.wV(g)
        v.wait_ge(self.sT.h, getattr(self, 'last_tp', 0))
        xs, qs = self.t_xs, self.t_qs
        v.tensor_tensor(xs[:, 0:w64], xblk[0][:, 0:w64], xblk[1][:, 0:w64], Alu.add)
        g.tensor_tensor(qs[:, 0:w64], sq_strips[0][:, 0:w64], sq_strips[1][:, 0:w64],
                        Alu.add)
        for dd in range(2, ND):
            iv = v.tensor_tensor(xs[:, 0:w64], xs[:, 0:w64], xblk[dd][:, 0:w64], Alu.add)
            ig = g.tensor_tensor(qs[:, 0:w64], qs[:, 0:w64], sq_strips[dd][:, 0:w64],
                                 Alu.add)
        self.sV.inc(iv)
        self.sG.inc(ig)
        self.wV(te)
        self.wG(te)
        if act_guard:
            self.wA(te)
        te.matmul(ps_a[0:1, 0:bw], self.ones_col[:], xs[:, 0:bw], start=True, stop=True)
        self.sT.inc(te.matmul(ps_b[0:1, 0:bw], self.ones_col[:], qs[:, 0:bw],
                              start=True, stop=True))

    def ln_rows(self, bw, ps_a, ps_b):
        nc = self.nc
        v, a = nc.vector, nc.scalar
        w64 = max(bw, 64)
        self.wT(v)
        v.tensor_scalar(self.rowA[0:1, 0:w64], ps_a[0:1, 0:w64], 1.0 / C, None, Alu.mult)
        v.tensor_scalar(self.rowB[0:1, 0:w64], ps_b[0:1, 0:w64], 1.0 / C, None, Alu.mult)
        v.tensor_tensor(self.rowC[0:1, 0:w64], self.rowA[0:1, 0:w64],
                        self.rowA[0:1, 0:w64], Alu.mult)
        v.tensor_tensor(self.rowB[0:1, 0:w64], self.rowB[0:1, 0:w64],
                        self.rowC[0:1, 0:w64], Alu.subtract)
        self.sV.inc(v.tensor_scalar(self.rowB[0:1, 0:w64], self.rowB[0:1, 0:w64], 0.0,
                                    None, Alu.max))
        self.wV(a)
        self.sA.inc(a.activation(self.rowC[0:1, 0:w64], self.rowB[0:1, 0:w64], Act.Sqrt,
                                 bias=self.eps_col[0:1, 0:1], scale=1.0))
        self.wA(v)
        self.sV.inc(v.reciprocal(self.rowD[0:1, 0:w64], self.rowC[0:1, 0:w64]))

    def quant1(self, c0, c1, in_dram, xblk, sq, xqd, tmpB, preloaded=False,
               next_rng=None):
        nc = self.nc
        v, a, te, g = nc.vector, nc.scalar, nc.tensor, nc.gpsimd
        bw = c1 - c0
        w64 = max(bw, 64)
        if not preloaded:
            for dd in range(ND):
                self.ld(xblk[dd][:, 0:bw], in_dram[dd][:, c0:c1])
        self.ln_stats(xblk, sq, bw, self.ps[0], self.ps[1])
        self.ln_rows(bw, self.ps[0], self.ps[1])
        self.wV(te)
        self.wA(te)
        self.sT.inc(te.matmul(self.ps[2][:, 0:bw], self.ones_row[:], self.rowA[0:1, 0:bw],
                              start=True, stop=True))
        self.wT(v)
        self.wT(g)
        self.wG(v)
        for dd in range(ND):
            self.sV.inc(v.tensor_tensor(sq[dd][:, 0:w64], xblk[dd][:, 0:w64],
                                        self.ps[2][:, 0:w64], Alu.subtract))
        if next_rng is not None:
            nc.sync.wait_ge(self.sV.h, self.sV.n)
            nc0, nc1 = next_rng
            for dd in range(ND):
                self.ld(xblk[dd][:, 0:nc1 - nc0], in_dram[dd][:, nc0:nc1])
        self.wV(g)
        xs = self.t_xs
        tmps = [self.t_qs, tmpB]
        vmax_done = []
        for dd in range(ND):
            if dd >= 2:
                g.wait_ge(self.sV.h, vmax_done[dd - 2])
            self.sG.inc(g.tensor_tensor(tmps[dd % 2][:, 0:w64], sq[dd][:, 0:w64],
                                        sq[dd][:, 0:w64], Alu.mult))
            v.wait_ge(self.sG.h, self.sG.n)
            if dd == 0:
                iv = v.tensor_copy(xs[:, 0:w64], tmps[0][:, 0:w64])
            else:
                iv = v.tensor_tensor(xs[:, 0:w64], xs[:, 0:w64], tmps[dd % 2][:, 0:w64],
                                     Alu.max)
            self.sV.inc(iv)
            vmax_done.append(self.sV.n)
        self.wV(te)
        nsb = (bw + 127) // 128
        for j in range(nsb):
            sw_ = min(128, bw - j * 128)
            it = te.transpose(self.ps[3][0:sw_, j * 128:j * 128 + 128],
                              xs[:, j * 128:j * 128 + sw_], self.iden[:])
        self.sT.inc(it)
        self.last_tp = self.sT.n
        v.wait_ge(self.sT.h, self.last_tp)
        self.wA(v)
        for j in range(nsb):
            sw_ = min(128, bw - j * 128)
            self.vf(v.tensor_reduce(self.mxcol[0:sw_, j:j + 1],
                                    self.ps[3][0:sw_, j * 128:(j + 1) * 128], AX, Alu.max))
        self.wV(a)
        self.af(a.activation(self.mxcol[:, 0:4], self.mxcol[:, 0:4], Act.Sqrt,
                             bias=0.0, scale=1.0))
        self.wA(te)
        self.wV(te)
        for j in range(nsb):
            it = te.transpose(self.ps[3][0:1, j * 128:(j + 1) * 128],
                              self.mxcol[:, j:j + 1], self.iden[:])
        self.sT.inc(it)
        self.wT(v)
        v.tensor_copy(self.rowC[0:1, 0:w64], self.ps[3][0:1, 0:w64])
        v.tensor_tensor(self.rowC[0:1, 0:w64], self.rowD[0:1, 0:w64],
                        self.rowC[0:1, 0:w64], Alu.mult)
        v.tensor_scalar(self.rowC[0:1, 0:w64], self.rowC[0:1, 0:w64], 1e-5, None, Alu.max)
        v.reciprocal(self.rowB[0:1, 0:w64], self.rowC[0:1, 0:w64])
        self.sV.inc(v.tensor_scalar(self.rowB[0:1, 0:w64], self.rowB[0:1, 0:w64], 127.0,
                                    None, Alu.mult))
        self.sV.inc(v.reciprocal(self.rowC[0:1, 0:w64], self.rowB[0:1, 0:w64]))
        self.sV.inc(v.tensor_tensor(self.rowD[0:1, 0:w64], self.rowD[0:1, 0:w64],
                                    self.rowB[0:1, 0:w64], Alu.mult))
        self.wV(te)
        te.matmul(self.ps[2][:, 0:bw], self.ones_row[:], self.rowD[0:1, 0:bw],
                  start=True, stop=True)
        self.sT.inc(te.matmul(self.ps[3][:, 0:bw], self.ones_row[:], self.rowC[0:1, 0:bw],
                              start=True, stop=True))
        self.wT(v)
        i16 = self.t_i16
        shift = 256.5 if self.shadow else 0.0
        for dd in range(ND):
            v.tensor_tensor(sq[dd][:, 0:w64], sq[dd][:, 0:w64], self.ps[2][:, 0:w64],
                            Alu.mult)
            if not self.shadow:
                v.tensor_scalar(i16[:, 0:w64], sq[dd][:, 0:w64], 0.0, None, Alu.add)
                iv = v.scalar_tensor_tensor(xqd[dd][:, 0:w64], i16[:, 0:w64], 0.0,
                                            self.ps[3][:, 0:w64], Alu.subtract, Alu.mult)
            else:
                cg = self.t_cg
                v.tensor_scalar(sq[dd][:, 0:w64], sq[dd][:, 0:w64], shift, None, Alu.add)
                v.tensor_scalar(i16[:, 0:w64], sq[dd][:, 0:w64], 0.0, None, Alu.add)
                v.tensor_tensor(cg[:, 0:w64], i16[:, 0:w64], sq[dd][:, 0:w64], Alu.is_gt)
                v.scalar_tensor_tensor(cg[:, 0:w64], i16[:, 0:w64], 256.0, cg[:, 0:w64],
                                       Alu.subtract, Alu.subtract)
                iv = v.tensor_tensor(xqd[dd][:, 0:w64], cg[:, 0:w64], self.ps[3][:, 0:w64],
                                     Alu.mult)
            self.sV.inc(iv)
        self.wV(v)

    def ffn_p1(self, lo, hi, in_dram, final):
        nc, ar = self.nc, self.ar
        v, a, te, g = nc.vector, nc.scalar, nc.tensor, nc.gpsimd
        ar.cursor = self.R_BIG
        gstr = [ar.take([P, 512], F32) for _ in range(NH)]
        ar.cursor = self.R_MED
        xblk = [ar.take([P, 512], F32) for _ in range(ND)]
        sq = [ar.take([P, 512], F32) for _ in range(ND)]
        xqd = [ar.take([P, 512], BF16) for _ in range(ND)]
        Hs = [ar.take([P, 520], F32) for _ in range(2)]
        tails = ar.take([P, 2 * NH], F32)
        n2o = [ar.take([P, 512], I8) for _ in range(2)]
        rmA = ar.take([P, 512], F32)
        rmB = ar.take([P, 512], F32)
        s2b_s = ar.take([P, 512], F32)
        u_s = self.t_qs

        self.wA(g)
        self.wV(g)
        g.memset(Hs[0][:], 0.0)
        self.gf(g.memset(Hs[1][:], 0.0))

        blocks = _blocks(lo, hi)
        for bi, (c0, c1) in enumerate(blocks):
            bw = c1 - c0
            first, last = (c0 == lo), (c1 == hi)
            g0 = c0 + 1 if first else c0 - 1
            g1 = hi - 1 if last else c1 - 1
            gw = g1 - g0
            gw64 = max(gw, 64)
            nxt = blocks[bi + 1] if bi + 1 < len(blocks) else None
            self.quant1(c0, c1, in_dram, xblk, sq, xqd, rmA,
                        preloaded=(bi > 0), next_rng=nxt)
            self.wG(a)
            mm_base = self.sT.n
            ev_base = self.sA.n
            self.rm_done = self.sV.n
            conv_done = []
            t1_done = []
            silu_done = []
            o_l = g0 - c0 + 1
            for ch in range(NH):
                pch = self.ps[4 + (ch % 4)]
                if ch >= 4:
                    te.wait_ge(self.sA.h, ev_base + 3 * (ch - 4) + 1)
                elif ch == 0:
                    self.wV(te)
                for dd in range(ND):
                    mm = te.matmul(pch[:, 0:bw], self.w1q[dd][:, ch * 128:(ch + 1) * 128],
                                   xqd[dd][:, 0:bw], start=(dd == 0), stop=(dd == ND - 1))
                self.sT.inc(mm)
                hb = Hs[ch % 2]
                a.wait_ge(self.sT.h, mm_base + ch + 1)
                if ch >= 2:
                    a.wait_ge(self.sV.h, conv_done[ch - 2])
                self.sA.inc(a.copy(hb[:, 2:2 + bw], pch[:, 0:bw]))
                v.wait_ge(self.sA.h, ev_base + 3 * ch + 1)
                if not first:
                    self.sV.inc(v.tensor_copy(hb[:, 0:2], tails[:, 2 * ch:2 * ch + 2]))
                if final and last:
                    lc = (hi - 1 - c0) + 2
                    self.sV.inc(v.tensor_scalar(hb[:, lc:lc + 1], hb[:, lc:lc + 1],
                                                self.endmask[:], None, Alu.mult))
                if final and first:
                    self.sV.inc(v.tensor_scalar(hb[:, 2:3], hb[:, 2:3],
                                                self.startmask[:], None, Alu.mult))
                self.wV(v)
                hb_ready = self.sV.n
                wlc = self.wlm[:, ch:ch + 1]
                wcc = self.wcm[:, ch:ch + 1]
                wrc = self.wrm[:, ch:ch + 1]
                t1 = sq[ch % 2]
                ub = sq[2 + ch % 2]
                a.wait_ge(self.sV.h, hb_ready)
                self.sA.inc(a.activation(t1[:, 0:gw64], hb[:, o_l:o_l + gw64], Act.Copy,
                                         bias=0.0, scale=wlc))
                if ch >= 2:
                    v.wait_ge(self.sA.h, silu_done[ch - 2])
                v.wait_ge(self.sA.h, ev_base + 3 * ch + 2)
                v.scalar_tensor_tensor(ub[:, 0:gw64], hb[:, o_l + 1:o_l + 1 + gw64], wcc,
                                       t1[:, 0:gw64], Alu.mult, Alu.add)
                self.sV.inc(v.scalar_tensor_tensor(ub[:, 0:gw64],
                                                   hb[:, o_l + 2:o_l + 2 + gw64], wrc,
                                                   ub[:, 0:gw64], Alu.mult, Alu.add))
                self.sV.inc(v.tensor_copy(tails[:, 2 * ch:2 * ch + 2], hb[:, bw:bw + 2]))
                conv_done.append(self.sV.n)
                a.wait_ge(self.sV.h, conv_done[-1])
                self.sA.inc(a.activation(gstr[ch][:, 0:gw64], ub[:, 0:gw64], Act.Silu,
                                         bias=self.cb[:, ch:ch + 1], scale=1.0))
                silu_done.append(self.sA.n)
                g.wait_ge(self.sA.h, silu_done[-1])
                if ch >= 1:
                    g.wait_ge(self.sV.h, self.rm_done)
                src0 = gstr[ch][:, 0:gw64]
                self.sG.inc(g.tensor_tensor(rmB[:, 0:gw64], src0, src0, Alu.mult))
                v.wait_ge(self.sG.h, self.sG.n)
                if ch == 0:
                    iv = v.tensor_copy(rmA[:, 0:gw64], rmB[:, 0:gw64])
                else:
                    iv = v.tensor_tensor(rmA[:, 0:gw64], rmA[:, 0:gw64], rmB[:, 0:gw64],
                                         Alu.max)
                self.sV.inc(iv)
                self.rm_done = self.sV.n
            self.wG(te)
            self.wV(te)
            nsb = (gw + 127) // 128
            for j in range(nsb):
                sw_ = min(128, gw - j * 128)
                it = te.transpose(self.ps[0][0:sw_, j * 128:j * 128 + 128],
                                  rmA[:, j * 128:j * 128 + sw_], self.iden[:])
            self.sT.inc(it)
            self.wT(v)
            for j in range(nsb):
                sw_ = min(128, gw - j * 128)
                self.vf(v.tensor_reduce(self.mxcol[0:sw_, j:j + 1],
                                        self.ps[0][0:sw_, j * 128:(j + 1) * 128], AX,
                                        Alu.max))
            self.wV(a)
            self.af(a.activation(self.mxcol[:, 0:4], self.mxcol[:, 0:4], Act.Sqrt,
                                 bias=0.0, scale=1.0))
            self.wA(v)
            self.vf(v.tensor_scalar(self.mxcol[:, 0:4], self.mxcol[:, 0:4], 1e-5, None,
                                    Alu.max))
            self.vf(v.reciprocal(self.s2col[:, 0:4], self.mxcol[:, 0:4]))
            self.vf(v.tensor_scalar(self.s2col[:, 0:4], self.s2col[:, 0:4], 127.0,
                                    None, Alu.mult))
            self.wV(te)
            for j in range(nsb):
                it = te.transpose(self.ps[1][0:1, j * 128:(j + 1) * 128],
                                  self.s2col[:, j:j + 1], self.iden[:])
            self.sT.inc(it)
            self.wT(v)
            self.sV.inc(v.tensor_copy(self.row_s2[0:1, 0:gw64], self.ps[1][0:1, 0:gw64]))
            v.tensor_scalar(self.row_s2[0:1, 0:gw64], self.row_s2[0:1, 0:gw64], 1e-5,
                            None, Alu.max)
            self.sV.inc(v.reciprocal(self.row_s2i[0:1, g0:g0 + gw64],
                                     self.row_s2[0:1, 0:gw64]))
            self.wV(te)
            self.sT.inc(te.matmul(self.ps[2][:, 0:gw], self.ones_row[:],
                                  self.row_s2[0:1, 0:gw], start=True, stop=True))
            self.wT(a)
            self.sA.inc(a.copy(s2b_s[:, 0:gw], self.ps[2][:, 0:gw]))
            st_base = self.sS2.n
            for ch in range(NH):
                eng, sem = (g, self.sG) if (ch % 2) else (v, self.sV)
                self.wA(eng)
                sem.inc(eng.tensor_tensor(gstr[ch][:, 0:gw64], gstr[ch][:, 0:gw64],
                                          s2b_s[:, 0:gw64], Alu.mult))
                nb = n2o[ch % 2]
                v.wait_ge(sem.h, sem.n)
                if ch >= 2:
                    v.wait_ge(self.sS2.h, st_base + 16 * (ch - 1))
                if not self.shadow:
                    iv = v.tensor_scalar(nb[:, 0:gw64], gstr[ch][:, 0:gw64], 0.0, None,
                                         Alu.add)
                else:
                    v.tensor_scalar(self.t_cg[:, 0:gw64], gstr[ch][:, 0:gw64], 256.5,
                                    None, Alu.add)
                    v.tensor_scalar(self.t_i16[:, 0:gw64], self.t_cg[:, 0:gw64], 0.0,
                                    None, Alu.add)
                    v.tensor_tensor(u_s[:, 0:gw64], self.t_i16[:, 0:gw64],
                                    self.t_cg[:, 0:gw64], Alu.is_gt)
                    v.scalar_tensor_tensor(u_s[:, 0:gw64], self.t_i16[:, 0:gw64], 256.0,
                                           u_s[:, 0:gw64], Alu.subtract, Alu.subtract)
                    iv = v.tensor_scalar(nb[:, 0:gw64], u_s[:, 0:gw64], 0.0, None, Alu.add)
                self.sV.inc(iv)
                nc.sync.wait_ge(self.sV.h, self.sV.n)
                self.st2(self.n2_st[ch][:, g0:g1], nb[:, 0:gw])

    def ffn_p2(self, lo, hi, in_dram, out_dram, final):
        nc, ar = self.nc, self.ar
        v, a, te = nc.vector, nc.scalar, nc.tensor
        ar.cursor = self.R_BIG
        n2b = [ar.take([P, 512], BF16) for _ in range(NH)]
        n2i = [[ar.take([P, 512], I8) for _ in range(NH)] for _ in range(2)]
        ar.cursor = self.R_MED
        xblk = [[ar.take([P, 512], F32) for _ in range(ND)] for _ in range(2)]
        ostr = [ar.take([P, 512], F32) for _ in range(2)]
        c2s = ar.take([P, 512], F32)

        blocks2 = _blocks(lo + 1, hi - 1)
        nc.sync.wait_ge(self.sA.h, self.sA.n)
        nc.sync.wait_ge(self.sV.h, self.sV.n)
        g0_, g1_ = blocks2[0]
        for ch in range(NH):
            self.ld(n2i[0][ch][:, 0:g1_ - g0_], self.n2_st[ch][:, g0_:g1_])
        n2_ready = self.sL.n
        for dd in range(ND):
            self.ld(xblk[0][dd][:, 0:g1_ - g0_], in_dram[dd][:, g0_:g1_])
        xd_ready = self.sL.n
        cast_end = 0
        add_end = 0
        for bi, (g0, g1) in enumerate(blocks2):
            gw = g1 - g0
            gw64 = max(gw, 64)
            pb = bi % 2
            a.wait_ge(self.sL.h, n2_ready)
            self.wT(a)
            for ch in range(NH):
                ia = a.copy(n2b[ch][:, 0:gw], n2i[pb][ch][:, 0:gw])
            self.sA.inc(ia)
            this_cast = self.sA.n
            this_xd = xd_ready
            this_x = xblk[pb]
            if bi + 1 < len(blocks2):
                n0, n1 = blocks2[bi + 1]
                if cast_end:
                    nc.sync.wait_ge(self.sA.h, cast_end)
                for ch in range(NH):
                    self.ld(n2i[1 - pb][ch][:, 0:n1 - n0], self.n2_st[ch][:, n0:n1])
                n2_ready = self.sL.n
                if add_end:
                    nc.sync.wait_ge(self.sV.h, add_end)
                for dd in range(ND):
                    self.ld(xblk[1 - pb][dd][:, 0:n1 - n0], in_dram[dd][:, n0:n1])
                xd_ready = self.sL.n
            cast_end = this_cast
            self.wA(v)
            self.sV.inc(v.tensor_scalar(self.rowA[0:1, 0:gw64],
                                        self.row_s2i[0:1, g0:g0 + gw64],
                                        self.scal[:, 4:5], None, Alu.mult))
            self.wV(te)
            self.sT.inc(te.matmul(self.ps[2][:, 0:gw], self.ones_row[:],
                                  self.rowA[0:1, 0:gw], start=True, stop=True))
            self.wT(a)
            self.sA.inc(a.copy(c2s[:, 0:gw], self.ps[2][:, 0:gw]))
            c2s_done = self.sA.n
            mm_base = self.sT.n
            ev_base = self.sV.n
            ost_base = self.sO.n if final else self.sS2.n
            for od in range(ND):
                pso = self.ps[6 + (od % 2)]
                if od >= 2:
                    te.wait_ge(self.sV.h, ev_base + (od - 1))
                else:
                    te.wait_ge(self.sA.h, this_cast)
                    self.wV(te)
                for ch in range(NH):
                    mm = te.matmul(pso[:, 0:gw], self.w2q[ch][:, od * 128:(od + 1) * 128],
                                   n2b[ch][:, 0:gw], start=(ch == 0), stop=(ch == NH - 1))
                self.sT.inc(mm)
                ob = ostr[od % 2]
                v.wait_ge(self.sT.h, mm_base + od + 1)
                v.wait_ge(self.sL.h, this_xd)
                v.wait_ge(self.sA.h, c2s_done)
                if od >= 2:
                    sem = self.sO if final else self.sS2
                    v.wait_ge(sem.h, ost_base + 16 * (od - 1))
                v.tensor_tensor(ob[:, 0:gw64], pso[:, 0:gw64], c2s[:, 0:gw64], Alu.mult)
                self.sV.inc(v.tensor_tensor(ob[:, 0:gw64], ob[:, 0:gw64],
                                            this_x[od][:, 0:gw64], Alu.add))
                nc.sync.wait_ge(self.sV.h, self.sV.n)
                if final:
                    self.st(out_dram[od][:, g0 - 3:g1 - 3], ob[:, 0:gw], final=True)
                else:
                    self.st2(out_dram[od][:, g0:g1], ob[:, 0:gw])
            add_end = self.sV.n

    def ffn(self, lo, hi, in_dram, out_dram, final):
        nc = self.nc
        self.ffn_p1(lo, hi, in_dram, final)
        nc.sync.wait_ge(self.sT.h, self.sT.n)
        self.syncS()
        for i in range(NH):
            self.ld(self.w2q[i][:], self.w2q_st[i])
        w2q_ready = self.sL.n
        nc.tensor.wait_ge(self.sL.h, w2q_ready)
        self.ffn_p2(lo, hi, in_dram, out_dram, final)

    def ln_only(self, c0, c1, in_dram, xblk, xn_out, preloaded=False, next_rng=None):
        nc = self.nc
        v, te = nc.vector, nc.tensor
        bw = c1 - c0
        if not preloaded:
            nc.sync.wait_ge(self.sV.h, self.sV.n)
            for dd in range(ND):
                self.ld(xblk[dd][:, 0:bw], in_dram[dd][:, c0:c1])
        self.ln_stats(xblk, xn_out, bw, self.ps[0], self.ps[1], act_guard=True)
        self.ln_rows(bw, self.ps[0], self.ps[1])
        self.wV(te)
        te.matmul(self.ps[2][:, 0:bw], self.ones_row[:], self.rowA[0:1, 0:bw],
                  start=True, stop=True)
        self.sT.inc(te.matmul(self.ps[3][:, 0:bw], self.ones_row[:], self.rowD[0:1, 0:bw],
                              start=True, stop=True))
        self.wT(v)
        w64 = max(bw, 64)
        for dd in range(ND):
            v.tensor_tensor(xn_out[dd][:, 0:w64], xblk[dd][:, 0:w64], self.ps[2][:, 0:w64],
                            Alu.subtract)
            iv = v.tensor_tensor(xn_out[dd][:, 0:w64], xn_out[dd][:, 0:w64],
                                 self.ps[3][:, 0:w64], Alu.mult)
        self.sV.inc(iv)
        if next_rng is not None:
            nc.sync.wait_ge(self.sV.h, self.sV.n)
            n0, n1 = next_rng
            for dd in range(ND):
                self.ld(xblk[dd][:, 0:n1 - n0], in_dram[dd][:, n0:n1])

    def attn(self):
        nc, ar = self.nc, self.ar
        v, a, te, g = nc.vector, nc.scalar, nc.tensor, nc.gpsimd
        AW = 256
        wbufA = [self.ar.take([P, C], BF16, at=self.R_W + i * C * 2) for i in range(ND)]
        wbufB = [self.ar.take([P, C], BF16, at=self.R_W + (ND + i) * C * 2)
                 for i in range(ND)]
        ar.cursor = self.R_BIG
        PASS2_OFF = ar.cursor
        xnw = [[ar.take([P, AW], F32) for _ in range(ND)] for _ in range(2)]
        ramp = ar.take([P, TC], F32)
        ar.cursor = self.R_MED
        kwin = [ar.take([P, AW], F32) for _ in range(ND)]
        vwin = [ar.take([P, AW], F32) for _ in range(ND)]
        xb1 = [ar.take([P, AW], F32) for _ in range(ND)]
        xk_b = [ar.take([P, AW], BF16) for _ in range(ND)]
        xv_b = [ar.take([P, AW], BF16) for _ in range(ND)]
        xr_b = [ar.take([P, AW], BF16) for _ in range(2)]
        t_a = ar.take([P, AW], F32)
        sumbuf = ar.take([P, 24], F32)
        kapw = ar.take([P, ND], F32)
        kapn = ar.take([P, 1], F32)
        sa_w = ar.take([P, ND], F32)
        sb_w = ar.take([P, ND], F32)
        tmp8 = ar.take([P, ND], F32)
        tmp8b = ar.take([P, ND], F32)
        tmp8c = ar.take([P, ND], F32)
        aggb = ar.take([P, 8 * 24], F32)
        ccs = ar.take([P, 24], F32)

        self.wV(g)
        self.wA(g)
        self.sG.inc(g.iota(ramp[:], [[1, TC]], channel_multiplier=0,
                           allow_small_or_imprecise_dtypes=True))

        self.syncS()
        self.wV(g)
        if self.sS2.n:
            g.wait_ge(self.sS2.h, self.sS2.n)
        for i in range(ND):
            self.ld2(wbufA[i][:], self.wkt[i])
            self.ld2(wbufB[i][:], self.wvt[i])
        wk_ready = self.sL2.n

        ln_blocks = _blocks(1, W - 1, AW)
        mr_blocks = _blocks(2, W - 1, AW)
        nb = len(ln_blocks)
        xr_sts = []

        def ln2(j):
            nxt = ln_blocks[j + 1] if j + 1 < len(ln_blocks) else None
            self.ln_only(ln_blocks[j][0], ln_blocks[j][1], self.xres_a, xb1, xnw[j % 2],
                         preloaded=(j > 0), next_rng=nxt)
            if j == 0:
                self.wV(v)
                for dd in range(ND):
                    self.sV.inc(v.tensor_scalar(xnw[0][dd][:, 1:2], xnw[0][dd][:, 1:2],
                                                self.startmask[:], None, Alu.mult))
                self.wV(v)

        ln2(0)
        for j in range(len(mr_blocks)):
            if j + 1 < nb:
                ln2(j + 1)
            c0, c1 = mr_blocks[j]
            bw = c1 - c0
            bw64 = max(bw, 64)
            xj = xnw[j % 2]
            xj1 = xnw[(j + 1) % 2]
            l0 = c0 - ln_blocks[j][0]
            wid_in = ln_blocks[j][1] - c0
            for dd in range(ND):
                if wid_in >= bw64:
                    v.tensor_tensor(t_a[:, 0:bw64], xj[dd][:, l0:l0 + bw64],
                                    xj[dd][:, l0 - 1:l0 - 1 + bw64], Alu.subtract)
                elif wid_in >= bw:
                    wd = min(bw64, AW - l0)
                    v.tensor_tensor(t_a[:, 0:wd], xj[dd][:, l0:l0 + wd],
                                    xj[dd][:, l0 - 1:l0 - 1 + wd], Alu.subtract)
                else:
                    v.tensor_tensor(t_a[:, 0:wid_in], xj[dd][:, l0:l0 + wid_in],
                                    xj[dd][:, l0 - 1:l0 - 1 + wid_in], Alu.subtract)
                    self.sV.inc(v.tensor_tensor(t_a[:, wid_in:bw],
                                                xj1[dd][:, 0:bw - wid_in],
                                                xj[dd][:, l0 - 1 + wid_in:l0 - 1 + bw],
                                                Alu.subtract))
                    self.wV(v)
                sh0 = l0 - 1
                wd = min(bw64, AW - sh0)
                v.scalar_tensor_tensor(xk_b[dd][:, 0:wd], t_a[:, 0:wd],
                                       self.mk[:, dd:dd + 1], xj[dd][:, sh0:sh0 + wd],
                                       Alu.mult, Alu.add)
                v.scalar_tensor_tensor(xv_b[dd][:, 0:wd], t_a[:, 0:wd],
                                       self.mv[:, dd:dd + 1], xj[dd][:, sh0:sh0 + wd],
                                       Alu.mult, Alu.add)
                if len(xr_sts) >= 2:
                    v.wait_ge(self.sS2.h, xr_sts[-2])
                self.sV.inc(v.scalar_tensor_tensor(xr_b[dd % 2][:, 0:wd], t_a[:, 0:wd],
                                                   self.mr[:, dd:dd + 1],
                                                   xj[dd][:, sh0:sh0 + wd],
                                                   Alu.mult, Alu.add))
                nc.sync.wait_ge(self.sV.h, self.sV.n)
                self.st2(self.xr_st[dd][:, c0 - 2:c0 - 2 + bw], xr_b[dd % 2][:, 0:bw])
                xr_sts.append(self.sS2.n)
            self.wV(te)
            te.wait_ge(self.sL2.h, wk_ready)
            a.wait_ge(self.sS2.h, getattr(self, '_kv_stash_done', 0))
            kv_ev = self.sA.n
            for oc in range(ND):
                if oc >= 2:
                    te.wait_ge(self.sA.h, kv_ev + (oc - 1))
                for dd in range(ND):
                    mm = te.matmul(self.ps[0 + (oc % 2)][:, 0:bw],
                                   wbufA[dd][:, oc * 128:(oc + 1) * 128],
                                   xk_b[dd][:, 0:bw], start=(dd == 0), stop=(dd == ND - 1))
                self.sT.inc(mm)
                self.wT(a)
                self.sA.inc(a.copy(kwin[oc][:, 0:bw], self.ps[0 + (oc % 2)][:, 0:bw]))
            kv_ev = self.sA.n
            for oc in range(ND):
                if oc >= 2:
                    te.wait_ge(self.sA.h, kv_ev + (oc - 1))
                for dd in range(ND):
                    mm = te.matmul(self.ps[2 + (oc % 2)][:, 0:bw],
                                   wbufB[dd][:, oc * 128:(oc + 1) * 128],
                                   xv_b[dd][:, 0:bw], start=(dd == 0), stop=(dd == ND - 1))
                self.sT.inc(mm)
                self.wT(a)
                self.sA.inc(a.copy(vwin[oc][:, 0:bw], self.ps[2 + (oc % 2)][:, 0:bw]))
            self.wA(v)
            if j == 0:
                for oc in range(ND):
                    self.sV.inc(v.tensor_tensor(kwin[oc][:, 0:1], kwin[oc][:, 0:1],
                                                self.startneg[:], Alu.add))
                self.wV(v)
            nc.sync.wait_ge(self.sV.h, self.sV.n)
            nc.sync.wait_ge(self.sA.h, self.sA.n)
            for oc in range(ND):
                self.st2(self.kv_st[0, oc][:, c0 - 2:c0 - 2 + bw], kwin[oc][:, 0:bw])
                self.st2(self.kv_st[1, oc][:, c0 - 2:c0 - 2 + bw], vwin[oc][:, 0:bw])
            self._kv_stash_done = self.sS2.n
            if c0 < TC + 2:
                sw = min(c1, TC + 2) - c0
                l_off = c0 - 2
                self.wG(v)
                for oc in range(ND):
                    v.scalar_tensor_tensor(t_a[:, 0:sw], ramp[:, l_off:l_off + sw],
                                           self.absw[:, oc:oc + 1], kwin[oc][:, 0:sw],
                                           Alu.mult, Alu.add)
                    self.vf(v.tensor_reduce(kapw[:, oc:oc + 1], t_a[:, 0:sw], AX, Alu.max))
                    self.vf(v.tensor_scalar(kapn[:, 0:1], kapw[:, oc:oc + 1], -1.0,
                                            None, Alu.mult))
                    self.wV(a)
                    self.sA.inc(a.activation(t_a[:, 0:sw], t_a[:, 0:sw], Act.Exp,
                                             bias=kapn[:, 0:1], scale=1.0,
                                             accum_out=sb_w[:, oc:oc + 1]))
                    self.wA(v)
                    self.vf(v.scalar_tensor_tensor(t_a[:, 0:sw], t_a[:, 0:sw], 1.0,
                                                   vwin[oc][:, 0:sw], Alu.mult, Alu.mult,
                                                   accum_out=sa_w[:, oc:oc + 1]))
                if j == 0:
                    self.vf(v.tensor_copy(sumbuf[:, 0:8], sa_w[:]))
                    self.vf(v.tensor_copy(sumbuf[:, 8:16], sb_w[:]))
                    self.vf(v.tensor_copy(sumbuf[:, 16:24], kapw[:]))
                else:
                    self.vf(v.tensor_tensor(tmp8b[:], sumbuf[:, 16:24], kapw[:], Alu.max))
                    self.vf(v.tensor_tensor(tmp8c[:], sumbuf[:, 16:24], tmp8b[:],
                                            Alu.subtract))
                    self.vf(v.tensor_tensor(tmp8[:], kapw[:], tmp8b[:], Alu.subtract))
                    self.wV(a)
                    self.sA.inc(a.activation(tmp8c[:], tmp8c[:], Act.Exp, bias=0.0,
                                             scale=1.0))
                    self.af(a.activation(tmp8[:], tmp8[:], Act.Exp, bias=0.0, scale=1.0))
                    self.wA(v)
                    self.vf(v.tensor_tensor(sumbuf[:, 0:8], sumbuf[:, 0:8], tmp8c[:],
                                            Alu.mult))
                    self.vf(v.tensor_tensor(sumbuf[:, 8:16], sumbuf[:, 8:16], tmp8c[:],
                                            Alu.mult))
                    self.vf(v.scalar_tensor_tensor(tmp8c[:], sa_w[:], 1.0, tmp8[:],
                                                   Alu.mult, Alu.mult))
                    self.vf(v.tensor_tensor(sumbuf[:, 0:8], sumbuf[:, 0:8], tmp8c[:],
                                            Alu.add))
                    self.vf(v.scalar_tensor_tensor(tmp8c[:], sb_w[:], 1.0, tmp8[:],
                                                   Alu.mult, Alu.mult))
                    self.vf(v.tensor_tensor(sumbuf[:, 8:16], sumbuf[:, 8:16], tmp8c[:],
                                            Alu.add))
                    self.vf(v.tensor_copy(sumbuf[:, 16:24], tmp8b[:]))
        self.vf(v.scalar_tensor_tensor(sumbuf[:, 16:24], self.w_c[:], 2047.0,
                                       sumbuf[:, 16:24], Alu.mult, Alu.add))
        nc.sync.wait_ge(self.sV.h, self.sV.n)
        self.st(self.cc_in[:], sumbuf[:, 0:24])
        g.wait_ge(self.sS.h, self.sS.n)
        self.sC.inc(g.collective_compute(
            "AllGather", Alu.bypass, replica_groups=[[0, 1, 2, 3, 4, 5, 6, 7]],
            ins=[self.cc_in[:]], outs=[self.cc_out[:]]))
        nc.sync.wait_ge(self.sC.h, self.sC.n)
        for j in range(8):
            self.ld(aggb[:, j * 24:(j + 1) * 24], self.cc_out[j])
        self.wL(v)
        self.wL(g)

        def slot(jj, q):
            return aggb[:, jj * 24 + q * 8:jj * 24 + (q + 1) * 8]

        first_cand = True
        for b in range(2):
            self.wV(g)
            g.memset(ccs[:, 0:16], 0.0)
            self.gf(g.memset(ccs[:, 16:24], -1e30))
            self.wG(v)
            for cpos in range(4):
                jj = b * 4 + cpos
                if first_cand:
                    for q in range(3):
                        self.vf(v.tensor_scalar(self.sinp[:, q * 8:(q + 1) * 8],
                                                ccs[:, q * 8:(q + 1) * 8],
                                                self.onehot[:, jj:jj + 1], None, Alu.mult))
                    first_cand = False
                else:
                    for q in range(3):
                        self.vf(v.scalar_tensor_tensor(self.sinp[:, q * 8:(q + 1) * 8],
                                                       ccs[:, q * 8:(q + 1) * 8],
                                                       self.onehot[:, jj:jj + 1],
                                                       self.sinp[:, q * 8:(q + 1) * 8],
                                                       Alu.mult, Alu.add))
                if cpos < 3:
                    self.vf(v.scalar_tensor_tensor(tmp8[:], self.w_c[:], 2048.0,
                                                   ccs[:, 16:24], Alu.mult, Alu.add))
                    self.vf(v.tensor_tensor(tmp8b[:], tmp8[:], slot(jj, 2), Alu.max))
                    self.vf(v.tensor_tensor(tmp8c[:], tmp8[:], tmp8b[:], Alu.subtract))
                    self.vf(v.tensor_tensor(tmp8[:], slot(jj, 2), tmp8b[:], Alu.subtract))
                    self.wV(a)
                    self.af(a.activation(tmp8c[:], tmp8c[:], Act.Exp, bias=0.0, scale=1.0))
                    self.af(a.activation(tmp8[:], tmp8[:], Act.Exp, bias=0.0, scale=1.0))
                    self.wA(v)
                    self.vf(v.tensor_tensor(ccs[:, 0:8], ccs[:, 0:8], tmp8c[:], Alu.mult))
                    self.vf(v.tensor_tensor(ccs[:, 8:16], ccs[:, 8:16], tmp8c[:],
                                            Alu.mult))
                    self.vf(v.scalar_tensor_tensor(tmp8c[:], slot(jj, 0), 1.0, tmp8[:],
                                                   Alu.mult, Alu.mult))
                    self.vf(v.tensor_tensor(ccs[:, 0:8], ccs[:, 0:8], tmp8c[:], Alu.add))
                    self.vf(v.scalar_tensor_tensor(tmp8c[:], slot(jj, 1), 1.0, tmp8[:],
                                                   Alu.mult, Alu.mult))
                    self.vf(v.tensor_tensor(ccs[:, 8:16], ccs[:, 8:16], tmp8c[:], Alu.add))
                    self.vf(v.tensor_copy(ccs[:, 16:24], tmp8b[:]))

        self.wT(g)
        for i in range(ND):
            self.ld2(wbufA[i][:], self.wrt[i])
            self.ld2(wbufB[i][:], self.wot[i])
        wr_ready = self.sL2.n

        ar.cursor = PASS2_OFF
        ppb = [ar.take([P, AW + 8], F32) for _ in range(ND)]
        aab = [ar.take([P, AW + 8], F32) for _ in range(ND)]
        bbb = [ar.take([P, AW + 8], F32) for _ in range(ND)]
        wz = [ar.take([P, AW], F32) for _ in range(ND)]
        T1 = [ar.take([P, AW], F32) for _ in range(ND)]
        T2 = [ar.take([P, AW], F32) for _ in range(ND)]
        ar.cursor = self.R_MED
        kw2 = [ar.take([P, AW], F32) for _ in range(ND)]
        vw2 = [ar.take([P, AW], F32) for _ in range(ND)]
        rw2 = [ar.take([P, AW], F32) for _ in range(ND)]
        ry2 = [ar.take([P, AW], BF16) for _ in range(ND)]
        xr2 = [ar.take([P, AW], BF16) for _ in range(ND)]
        x1o = [ar.take([P, AW], F32) for _ in range(2)]
        T3 = [ar.take([P, AW], F32) for _ in range(ND)]
        T4 = [ar.take([P, AW], F32) for _ in range(ND)]

        self.wV(g)
        self.wA(g)
        for c in range(ND):
            ig = g.memset(wz[c][:], 0.0)
        self.sG.inc(ig)
        self.wG(v)
        for c in range(ND):
            self.sV.inc(v.tensor_scalar(wz[c][:], wz[c][:], self.w_c[:, c:c + 1], None,
                                        Alu.add))
        for c in range(ND):
            self.sV.inc(v.tensor_copy(aab[c][:, 0:1], self.sinp[:, c:c + 1]))
            self.sV.inc(v.tensor_copy(bbb[c][:, 0:1], self.sinp[:, 8 + c:9 + c]))
            self.sV.inc(v.tensor_copy(ppb[c][:, 0:1], self.sinp[:, 16 + c:17 + c]))
        self.wV(v)

        self.syncS()
        p2a_prev = [self.sA.n] * 2
        p2v_prev = [self.sV.n] * 2
        p2g_prev = [self.sG.n] * 2
        p2_ranges = _blocks(2, W - 1, AW)
        nc.sync.wait_ge(self.sV.h, self.sV.n)
        nc.sync.wait_ge(self.sT.h, self.sT.n)
        c0_, c1_ = p2_ranges[0]
        for oc in range(ND):
            self.ld(xr2[oc][:, 0:c1_ - c0_], self.xr_st[oc][:, c0_ - 2:c1_ - 2])
        xr_ready = self.sL.n
        for oc in range(ND):
            self.ld(kw2[oc][:, 0:c1_ - c0_], self.kv_st[0, oc][:, c0_ - 2:c1_ - 2])
            self.ld(vw2[oc][:, 0:c1_ - c0_], self.kv_st[1, oc][:, c0_ - 2:c1_ - 2])
        kv_ready = self.sL.n
        prev_sig_end = 0
        for j, (c0, c1) in enumerate(p2_ranges):
            bw = c1 - c0
            nrw = bw < 64

            def vI(instr):
                if nrw:
                    self.sV.inc(instr)
                return instr

            def vE(instr):
                self.sV.inc(instr)
                if nrw:
                    v.wait_ge(self.sV.h, self.sV.n)
                return instr

            te.wait_ge(self.sL2.h, wr_ready)
            te.wait_ge(self.sL.h, xr_ready)
            r_ev = self.sA.n
            for oc in range(ND):
                if oc >= 2:
                    te.wait_ge(self.sA.h, r_ev + (oc - 1))
                elif prev_sig_end:
                    te.wait_ge(self.sA.h, prev_sig_end)
                for dd in range(ND):
                    mm = te.matmul(self.ps[0 + (oc % 2)][:, 0:bw],
                                   wbufA[dd][:, oc * 128:(oc + 1) * 128],
                                   xr2[dd][:, 0:bw], start=(dd == 0), stop=(dd == ND - 1))
                self.sT.inc(mm)
                self.wT(a)
                self.sA.inc(a.activation(rw2[oc][:, 0:bw], self.ps[0 + (oc % 2)][:, 0:bw],
                                         Act.Sigmoid, bias=0.0, scale=1.0))
            sig_done = self.sA.n
            prev_sig_end = sig_done
            rmm_end = self.sT.n
            if j + 1 < len(p2_ranges):
                n0, n1 = p2_ranges[j + 1]
                nc.sync.wait_ge(self.sT.h, rmm_end)
                for oc in range(ND):
                    self.ld(xr2[oc][:, 0:n1 - n0], self.xr_st[oc][:, n0 - 2:n1 - 2])
                xr_ready = self.sL.n
            v.wait_ge(self.sL.h, kv_ready)
            g.wait_ge(self.sL.h, kv_ready)
            H2 = [range(0, 4), range(4, 8)]
            rec = {}

            def gE(instrs):
                for i_, ig in enumerate(instrs):
                    if nrw or i_ == len(instrs) - 1:
                        self.sG.inc(ig)

            def aE(instrs):
                for i_, ia in enumerate(instrs):
                    if nrw or i_ == len(instrs) - 1:
                        self.sA.inc(ia)

            for h, chans in enumerate(H2):
                for c in chans:
                    vI(v.tensor_tensor_scan(ppb[c][:, 1:1 + bw], wz[c][:, 0:bw],
                                            kw2[c][:, 0:bw], ppb[c][:, 0:1],
                                            Alu.add, Alu.max))
                vE(v.engine_nop())
            for h, chans in enumerate(H2):
                v.wait_ge(self.sA.h, p2a_prev[h])
                v.wait_ge(self.sG.h, p2g_prev[h])
                for c in chans:
                    vI(v.tensor_tensor(T1[c][:, 0:bw], ppb[c][:, 0:bw],
                                       ppb[c][:, 1:1 + bw], Alu.subtract))
                    vI(v.tensor_tensor(T2[c][:, 0:bw], kw2[c][:, 0:bw],
                                       ppb[c][:, 1:1 + bw], Alu.subtract))
                vE(v.engine_nop())
                rec['s2', h] = self.sV.n
            for h, chans in enumerate(H2):
                a.wait_ge(self.sV.h, rec['s2', h])
                ii = []
                for c in chans:
                    ii.append(a.activation(T1[c][:, 0:bw], T1[c][:, 0:bw], Act.Exp,
                                           bias=self.w_c[:, c:c + 1], scale=1.0))
                    ii.append(a.activation(T2[c][:, 0:bw], T2[c][:, 0:bw], Act.Exp,
                                           bias=0.0, scale=1.0))
                aE(ii)
                rec['s3', h] = self.sA.n
            for h, chans in enumerate(H2):
                g.wait_ge(self.sA.h, rec['s3', h])
                g.wait_ge(self.sV.h, p2v_prev[h])
                gE([g.tensor_tensor(T3[c][:, 0:bw], vw2[c][:, 0:bw], T2[c][:, 0:bw],
                                    Alu.mult) for c in chans])
                rec['s4', h] = self.sG.n
            for h, chans in enumerate(H2):
                v.wait_ge(self.sG.h, rec['s4', h])
                for c in chans:
                    vI(v.tensor_tensor_scan(aab[c][:, 1:1 + bw], T1[c][:, 0:bw],
                                            T3[c][:, 0:bw], aab[c][:, 0:1],
                                            Alu.mult, Alu.add))
                    vI(v.tensor_tensor_scan(bbb[c][:, 1:1 + bw], T1[c][:, 0:bw],
                                            T2[c][:, 0:bw], bbb[c][:, 0:1],
                                            Alu.mult, Alu.add))
                vE(v.engine_nop())
            for h, chans in enumerate(H2):
                for c in chans:
                    vI(v.tensor_scalar(T3[c][:, 0:bw], ppb[c][:, 0:bw],
                                       self.u_c[:, c:c + 1], None, Alu.add))
                vE(v.engine_nop())
                for c in chans:
                    vI(v.tensor_tensor(T4[c][:, 0:bw], T3[c][:, 0:bw], kw2[c][:, 0:bw],
                                       Alu.max))
                vE(v.engine_nop())
                for c in chans:
                    vI(v.tensor_tensor(T3[c][:, 0:bw], T3[c][:, 0:bw], T4[c][:, 0:bw],
                                       Alu.subtract))
                    vI(v.tensor_tensor(T4[c][:, 0:bw], kw2[c][:, 0:bw], T4[c][:, 0:bw],
                                       Alu.subtract))
                vE(v.engine_nop())
                rec['s9', h] = self.sV.n
            for h, chans in enumerate(H2):
                a.wait_ge(self.sV.h, rec['s9', h])
                ii = []
                for c in chans:
                    ii.append(a.activation(T3[c][:, 0:bw], T3[c][:, 0:bw], Act.Exp,
                                           bias=0.0, scale=1.0))
                    ii.append(a.activation(T4[c][:, 0:bw], T4[c][:, 0:bw], Act.Exp,
                                           bias=0.0, scale=1.0))
                aE(ii)
                rec['s10', h] = self.sA.n
            for h, chans in enumerate(H2):
                g.wait_ge(self.sA.h, rec['s10', h])
                g.wait_ge(self.sV.h, rec['s9', h])
                gE([g.tensor_tensor(T1[c][:, 0:bw], vw2[c][:, 0:bw], T4[c][:, 0:bw],
                                    Alu.mult) for c in chans])
                rec['s11', h] = self.sG.n
            for h, chans in enumerate(H2):
                v.wait_ge(self.sA.h, rec['s10', h])
                for c in chans:
                    vI(v.tensor_tensor(T2[c][:, 0:bw], aab[c][:, 0:bw], T3[c][:, 0:bw],
                                       Alu.mult))
                vE(v.engine_nop())
                rec['s12a', h] = self.sV.n
            for h, chans in enumerate(H2):
                g.wait_ge(self.sV.h, rec['s12a', h])
                gE([g.tensor_tensor(T2[c][:, 0:bw], T2[c][:, 0:bw], T1[c][:, 0:bw],
                                    Alu.add) for c in chans])
                gE([g.tensor_tensor(T1[c][:, 0:bw], bbb[c][:, 0:bw], T3[c][:, 0:bw],
                                    Alu.mult) for c in chans])
                rec['s13', h] = self.sG.n
            for h, chans in enumerate(H2):
                v.wait_ge(self.sG.h, rec['s13', h])
                for c in chans:
                    vI(v.scalar_tensor_tensor(T1[c][:, 0:bw], T4[c][:, 0:bw], 1e-6,
                                              T1[c][:, 0:bw], Alu.add, Alu.add))
                vE(v.engine_nop())
                for c in chans:
                    vI(v.reciprocal(T1[c][:, 0:bw], T1[c][:, 0:bw]))
                vE(v.engine_nop())
                rec['s15', h] = self.sV.n
            for h, chans in enumerate(H2):
                g.wait_ge(self.sV.h, rec['s15', h])
                gE([g.tensor_tensor(T2[c][:, 0:bw], T2[c][:, 0:bw], T1[c][:, 0:bw],
                                    Alu.mult) for c in chans])
                rec['s16', h] = self.sG.n
            for h, chans in enumerate(H2):
                v.wait_ge(self.sA.h, sig_done)
                v.wait_ge(self.sG.h, rec['s16', h])
                for c in chans:
                    vI(v.tensor_tensor(ry2[c][:, 0:bw], rw2[c][:, 0:bw], T2[c][:, 0:bw],
                                       Alu.mult))
                vE(v.engine_nop())
            for c in range(ND):
                self.sV.inc(v.tensor_copy(ppb[c][:, 0:1], ppb[c][:, bw:bw + 1]))
                self.sV.inc(v.tensor_copy(aab[c][:, 0:1], aab[c][:, bw:bw + 1]))
                self.sV.inc(v.tensor_copy(bbb[c][:, 0:1], bbb[c][:, bw:bw + 1]))
            self.wV(v)
            p2a_prev = [rec['s10', 0], rec['s10', 1]]
            p2v_prev = [rec['s12a', 0], rec['s12a', 1]]
            p2g_prev = [rec['s16', 0], rec['s16', 1]]
            s9_done = rec['s9', 1]
            s11_done = rec['s11', 1]
            mm_base = self.sT.n
            ost_base = self.sS2.n
            ld_base = self.sL.n
            add_base = self.sV.n
            for od in range(ND):
                pso = self.ps[4 + (od % 2)]
                self.wV(te)
                for oc in range(ND):
                    mm = te.matmul(pso[:, 0:bw], wbufB[oc][:, od * 128:(od + 1) * 128],
                                   ry2[oc][:, 0:bw], start=(oc == 0), stop=(oc == ND - 1))
                self.sT.inc(mm)
                ob = x1o[od % 2]
                if od >= 2:
                    nc.sync.wait_ge(self.sS2.h, ost_base + 16 * (od - 1))
                else:
                    prev = getattr(self, '_wo_prev_end', 0)
                    if prev:
                        nc.sync.wait_ge(self.sS2.h, prev - 16 * (1 - od))
                self.ld(ob[:, 0:bw], self.xres_a[od][:, c0:c1])
                v.wait_ge(self.sT.h, mm_base + od + 1)
                v.wait_ge(self.sL.h, ld_base + 16 * (od + 1))
                self.sV.inc(v.tensor_tensor(ob[:, 0:bw], ob[:, 0:bw], pso[:, 0:bw],
                                            Alu.add))
                nc.sync.wait_ge(self.sV.h, add_base + od + 1)
                self.st2(self.xres_b[od][:, c0:c1], ob[:, 0:bw])
            self._wo_prev_end = self.sS2.n
            if j + 1 < len(p2_ranges):
                n0, n1 = p2_ranges[j + 1]
                nc.sync.wait_ge(self.sV.h, s9_done)
                nc.sync.wait_ge(self.sG.h, s11_done)
                for oc in range(ND):
                    self.ld(kw2[oc][:, 0:n1 - n0], self.kv_st[0, oc][:, n0 - 2:n1 - 2])
                    self.ld(vw2[oc][:, 0:n1 - n0], self.kv_st[1, oc][:, n0 - 2:n1 - 2])
                kv_ready = self.sL.n

    def build(self):
        nc = self.nc
        self.p0()
        self.pw()
        self.ffn(0, W, self.xw, self.xres_a, final=False)
        self.attn()
        nc.sync.wait_ge(self.sT.h, self.sT.n)
        self.syncS()
        for i in range(ND):
            self.ld(self.w1q[i][:], self.w1q_st[i])
        nc.tensor.wait_ge(self.sL.h, self.sL.n)
        self.ffn(2, W - 1, self.xres_b, self.out, final=True)
        nc.sync.wait_ge(self.sO.h, self.sO.n)
        nc.vector.wait_ge(self.sO.h, self.sO.n)
        nc.finalize()
        return nc


_CACHE = {}


def _build(shadow):
    key = bool(shadow)
    if key not in _CACHE:
        _CACHE[key] = Builder(shadow).build()
    return _CACHE[key]


def kernel(**inputs):
    x = np.asarray(inputs["L0_attn_x"], np.float32)
    td = np.asarray(inputs["L0_attn_time_decay"], np.float32)
    u = np.asarray(inputs["L0_attn_time_first"], np.float32)
    mk = np.asarray(inputs["L0_attn_time_mix_k"], np.float32).reshape(-1)
    mv = np.asarray(inputs["L0_attn_time_mix_v"], np.float32).reshape(-1)
    mr = np.asarray(inputs["L0_attn_time_mix_r"], np.float32).reshape(-1)
    Wk = np.asarray(inputs["L0_attn_Wk"], np.float32)
    Wv = np.asarray(inputs["L0_attn_Wv"], np.float32)
    Wr = np.asarray(inputs["L0_attn_Wr"], np.float32)
    Wo = np.asarray(inputs["L0_attn_Wo"], np.float32)
    w1 = np.asarray(inputs["L0_ffn_w1"], np.float32)
    cw = np.asarray(inputs["L0_ffn_conv_w"], np.float32)
    cb = np.asarray(inputs["L0_ffn_conv_b"], np.float32)
    w2 = np.asarray(inputs["L0_ffn_w2"], np.float32)

    def fm(vec, nch):
        return np.ascontiguousarray(vec.reshape(nch, P).T)

    shadow = os.environ.get("BASS_SIM_FP64") == "1"
    nc = _build(shadow)

    xfm = np.ascontiguousarray(np.transpose(x, (0, 2, 1)))
    T = x.shape[1]
    common = {
        "w1t": np.ascontiguousarray(w1.T).reshape(ND, P, HD),
        "w2t": np.ascontiguousarray(w2.T).reshape(NH, P, C),
        "wkt": np.ascontiguousarray(Wk.T).reshape(ND, P, C),
        "wvt": np.ascontiguousarray(Wv.T).reshape(ND, P, C),
        "wrt": np.ascontiguousarray(Wr.T).reshape(ND, P, C),
        "wot": np.ascontiguousarray(Wo.T).reshape(ND, P, C),
        "td_t": fm(td, ND), "u_t": fm(u, ND),
        "mk_t": fm(mk, ND), "mv_t": fm(mv, ND), "mr_t": fm(mr, ND),
        "wl_t": fm(cw[:, 0, 0].copy(), NH), "wc_t": fm(cw[:, 0, 1].copy(), NH),
        "wr2_t": fm(cw[:, 0, 2].copy(), NH), "cb_t": fm(cb, NH),
        "iden_in": np.eye(P, dtype=np.float32),
    }
    in_maps = []
    for i in range(8):
        b, c = i // 4, i % 4
        s, e = c * TC, (c + 1) * TC
        win = np.zeros((C, W), np.float32)
        glo, ghi = s - 3, e + 2
        slo, shi = max(glo, 0), min(ghi, T)
        win[:, slo - glo:shi - glo] = xfm[b, :, slo:shi]
        m = dict(common)
        m["xw"] = np.ascontiguousarray(win.reshape(ND, P, W))
        m["startneg_in"] = np.full((P, 1), -1e30 if c == 0 else 0.0, np.float32)
        m["startmask_in"] = np.full((P, 1), 0.0 if c == 0 else 1.0, np.float32)
        m["endmask_in"] = np.full((P, 1), 0.0 if c == 3 else 1.0, np.float32)
        oh = np.zeros((P, 8), np.float32)
        oh[:, i] = 1.0
        m["onehot_in"] = oh
        in_maps.append(m)

    res = run_bass_kernel_spmd(nc, in_maps, core_ids=list(range(8)))
    out = np.zeros((2, C, T), np.float32)
    for i in range(8):
        b, c = i // 4, i % 4
        o = np.asarray(res.results[i]["out"]).reshape(C, TC)
        out[b, :, c * TC:(c + 1) * TC] = o
    return np.ascontiguousarray(np.transpose(out, (0, 2, 1)))


# version 12
# speedup vs baseline: 1.4179x; 1.0135x over previous
import os
import numpy as np
import concourse.bass as bass
import concourse.mybir as mybir
from concourse.bass_utils import run_bass_kernel_spmd

F32 = mybir.dt.float32
BF16 = mybir.dt.bfloat16
I8 = mybir.dt.int8
I16 = mybir.dt.int16
Alu = mybir.AluOpType
Act = mybir.ActivationFunctionType
AX = mybir.AxisListType.X

C = 1024
HD = 4096
ND = 8
NH = 32
W = 2053
TC = 2048
P = 128

SB_BASE = 16512
SB_TOP = 229344
DEBUG_CAPTURE = False
DBG_BLOCK = 1


def _blocks(lo, hi, step=512):
    out = []
    c = lo
    while c < hi:
        out.append((c, min(c + step, hi)))
        c += step
    return out


class Sem:
    def __init__(self, nc, name):
        self.h = nc.alloc_semaphore(name)
        self.n = 0

    def inc(self, instr, k=1):
        instr.then_inc(self.h, k)
        self.n += k
        return instr


class Arena:
    def __init__(self, nc):
        self.nc = nc
        self.cursor = SB_BASE
        self.uid = 0

    def take(self, shape, dtype, at=None):
        per_part = int(np.prod(shape[1:])) * mybir.dt.size(dtype)
        off = self.cursor if at is None else at
        off = (off + 31) & ~31
        self.uid += 1
        t = self.nc.alloc_sbuf_tensor_at(f"sb{self.uid}", list(shape), dtype, offset=off)
        if at is None:
            self.cursor = off + per_part
        assert off + per_part <= SB_TOP, (off, per_part, shape)
        return t


class Builder:
    def __init__(self, shadow):
        self.shadow = shadow
        nc = bass.Bass("TRN2", target_bir_lowering=False, detect_race_conditions=False)
        self.nc = nc
        self.ar = Arena(nc)
        d = nc.dram_tensor
        self.xw = d("xw", [ND, P, W], F32, kind="ExternalInput")
        self.w1t = d("w1t", [ND, P, HD], F32, kind="ExternalInput")
        self.w2t = d("w2t", [NH, P, C], F32, kind="ExternalInput")
        self.wkt = d("wkt", [ND, P, C], F32, kind="ExternalInput")
        self.wvt = d("wvt", [ND, P, C], F32, kind="ExternalInput")
        self.wrt = d("wrt", [ND, P, C], F32, kind="ExternalInput")
        self.wot = d("wot", [ND, P, C], F32, kind="ExternalInput")
        self.td_t = d("td_t", [P, ND], F32, kind="ExternalInput")
        self.u_t = d("u_t", [P, ND], F32, kind="ExternalInput")
        self.mk_t = d("mk_t", [P, ND], F32, kind="ExternalInput")
        self.mv_t = d("mv_t", [P, ND], F32, kind="ExternalInput")
        self.mr_t = d("mr_t", [P, ND], F32, kind="ExternalInput")
        self.wl_t = d("wl_t", [P, NH], F32, kind="ExternalInput")
        self.wc_t = d("wc_t", [P, NH], F32, kind="ExternalInput")
        self.wr2_t = d("wr2_t", [P, NH], F32, kind="ExternalInput")
        self.cb_t = d("cb_t", [P, NH], F32, kind="ExternalInput")
        self.iden_in = d("iden_in", [P, P], F32, kind="ExternalInput")
        self.startneg_in = d("startneg_in", [P, 1], F32, kind="ExternalInput")
        self.startmask_in = d("startmask_in", [P, 1], F32, kind="ExternalInput")
        self.endmask_in = d("endmask_in", [P, 1], F32, kind="ExternalInput")
        self.onehot_in = d("onehot_in", [P, 8], F32, kind="ExternalInput")
        self.out = d("out", [ND, P, TC], F32, kind="ExternalOutput")

        self.xres_a = d("xres_a", [ND, P, W], F32, kind="Internal")
        self.xres_b = d("xres_b", [ND, P, W], F32, kind="Internal")
        self.w1q_st = d("w1q_st", [ND, P, HD], BF16, kind="Internal")
        self.w2q_st = d("w2q_st", [NH, P, C], BF16, kind="Internal")
        self.xr_st = d("xr_st", [ND, P, TC + 2], BF16, kind="Internal")
        self.n2_st = d("n2_st", [NH, P, W + 3], I8, kind="Internal")
        self.kv_st = d("kv_st", [2, ND, P, TC + 8], F32, kind="Internal")
        self.cc_in = d("cc_in", [P, 24], F32, kind="Internal")
        self.cc_out = d("cc_out", [8, P, 24], F32, kind="Internal", addr_space="Local")
        if DEBUG_CAPTURE:
            self.dbg_h = d("dbg_h", [NH, P, 520], F32, kind="Internal")
            self.dbg_g = d("dbg_g", [NH, P, 512], F32, kind="Internal")
            self.dbg_xq = d("dbg_xq", [ND, P, 512], BF16, kind="Internal")

        self.sL = Sem(nc, "sL")
        self.sL2 = Sem(nc, "sL2")
        self.sS = Sem(nc, "sS")
        self.sS2 = Sem(nc, "sS2")
        self.sO = Sem(nc, "sO")
        self.sT = Sem(nc, "sT")
        self.sA = Sem(nc, "sA")
        self.sV = Sem(nc, "sV")
        self.sG = Sem(nc, "sG")
        self.sC = Sem(nc, "sC")

        ar = self.ar
        self.iden = ar.take([P, P], F32)
        self.ones_col = ar.take([P, 1], F32)
        self.ones_row = ar.take([1, P], F32)
        self.td = ar.take([P, ND], F32)
        self.u_c = ar.take([P, ND], F32)
        self.w_c = ar.take([P, ND], F32)
        self.absw = ar.take([P, ND], F32)
        self.mk = ar.take([P, ND], F32)
        self.mv = ar.take([P, ND], F32)
        self.mr = ar.take([P, ND], F32)
        self.wl = ar.take([P, NH], F32)
        self.wcv = ar.take([P, NH], F32)
        self.wr2 = ar.take([P, NH], F32)
        self.cb = ar.take([P, NH], F32)
        self.wlm = ar.take([P, NH], F32)
        self.wcm = ar.take([P, NH], F32)
        self.wrm = ar.take([P, NH], F32)
        self.startneg = ar.take([P, 1], F32)
        self.startmask = ar.take([P, 1], F32)
        self.endmask = ar.take([P, 1], F32)
        self.onehot = ar.take([P, 8], F32)
        self.mu1c = ar.take([P, 1], F32)
        self.swcol = ar.take([P, 1], F32)
        self.scal = ar.take([1, 8], F32)
        self.eps_col = ar.take([P, 1], F32)
        self.mxcol = ar.take([P, 4], F32)
        self.s2col = ar.take([P, 4], F32)
        self.sinp = ar.take([P, 24], F32)
        self.rowA = ar.take([1, 512], F32)
        self.rowB = ar.take([1, 512], F32)
        self.rowC = ar.take([1, 512], F32)
        self.rowD = ar.take([1, 512], F32)
        self.row_s2 = ar.take([1, 512], F32)
        self.row_s2i = ar.take([1, W + 64], F32)
        self.t_xs = ar.take([P, 512], F32)
        self.t_qs = ar.take([P, 512], F32)
        self.t_i16 = ar.take([P, 512], I16)
        self.t_cg = ar.take([P, 512], F32) if shadow else None
        self.PERS_END = ar.cursor
        self.ps = [nc.alloc_psum_tensor(f"ps{i}", [P, 512], F32) for i in range(8)]
        self.R_W = (self.PERS_END + 31) & ~31
        self.R_BIG = self.R_W + 65568
        self.R_MED = self.R_BIG + 65568
        assert self.R_MED + 45 * 1024 < SB_TOP, (self.R_MED, SB_TOP)

    def ld(self, out_ap, in_ap):
        return self.sL.inc(self.nc.sync.dma_start(out_ap, in_ap), 16)

    def ld2(self, out_ap, in_ap):
        return self.sL2.inc(self.nc.gpsimd.dma_start(out_ap, in_ap), 16)

    def st(self, out_ap, in_ap, final=False):
        sem = self.sO if final else self.sS
        return sem.inc(self.nc.sync.dma_start(out_ap, in_ap), 16)

    def st2(self, out_ap, in_ap):
        return self.sS2.inc(self.nc.sync.dma_start(out_ap, in_ap), 16)

    def wA(self, eng):
        eng.wait_ge(self.sA.h, self.sA.n)

    def wV(self, eng):
        eng.wait_ge(self.sV.h, self.sV.n)

    def wG(self, eng):
        eng.wait_ge(self.sG.h, self.sG.n)

    def wT(self, eng):
        eng.wait_ge(self.sT.h, self.sT.n)

    def syncS(self):
        self.nc.sync.wait_ge(self.sS.h, self.sS.n)
        if self.sS2.n:
            self.nc.sync.wait_ge(self.sS2.h, self.sS2.n)

    def wL(self, eng):
        eng.wait_ge(self.sL.h, self.sL.n)
        if self.sL2.n:
            eng.wait_ge(self.sL2.h, self.sL2.n)

    def wS(self, eng):
        eng.wait_ge(self.sS.h, self.sS.n)
        if self.sS2.n:
            eng.wait_ge(self.sS2.h, self.sS2.n)

    def vf(self, instr):
        self.sV.inc(instr)
        self.nc.vector.wait_ge(self.sV.h, self.sV.n)
        return instr

    def af(self, instr):
        self.sA.inc(instr)
        self.nc.scalar.wait_ge(self.sA.h, self.sA.n)
        return instr

    def gf(self, instr):
        self.sG.inc(instr)
        self.nc.gpsimd.wait_ge(self.sG.h, self.sG.n)
        return instr

    def p0(self):
        nc = self.nc
        v, a, g = nc.vector, nc.scalar, nc.gpsimd
        self.ld(self.iden[:], self.iden_in[:])
        self.ld(self.td[:], self.td_t[:])
        self.ld(self.u_c[:], self.u_t[:])
        self.ld(self.mk[:], self.mk_t[:])
        self.ld(self.mv[:], self.mv_t[:])
        self.ld(self.mr[:], self.mr_t[:])
        self.ld(self.wl[:], self.wl_t[:])
        self.ld(self.wcv[:], self.wc_t[:])
        self.ld(self.wr2[:], self.wr2_t[:])
        self.ld(self.cb[:], self.cb_t[:])
        self.ld(self.startneg[:], self.startneg_in[:])
        self.ld(self.startmask[:], self.startmask_in[:])
        self.ld(self.endmask[:], self.endmask_in[:])
        self.ld(self.onehot[:], self.onehot_in[:])
        g.memset(self.ones_col[:], 1.0)
        g.memset(self.ones_row[:], 1.0)
        g.memset(self.mxcol[:], 0.0)
        g.memset(self.s2col[:], 0.0)
        g.memset(self.row_s2i[:], 1.0)
        self.gf(g.memset(self.eps_col[:], 1e-5))
        self.wL(a)
        self.af(a.activation(self.absw[:], self.td[:], Act.Exp, bias=0.0, scale=1.0))
        self.wA(v)
        self.vf(v.tensor_scalar(self.w_c[:], self.absw[:], -1.0, None, Alu.mult))

    def ternarize(self, src_dram, nchunks, width, wq_dst, st_dram, mu_slot,
                  region=None, use_ld2=False):
        nc, ar = self.nc, self.ar
        v, a, te = nc.vector, nc.scalar, nc.tensor
        ar.cursor = self.R_BIG if region is None else region
        ldf = self.ld2 if use_ld2 else self.ld
        lsem = self.sL2 if use_ld2 else self.sL
        stage = [ar.take([P, width], F32) for _ in range(2)]
        junk = ar.take([P, width], F32)
        i16 = ar.take([P, width], I16)
        qstage = [ar.take([P, width], BF16) for _ in range(2)] if wq_dst is None else None
        abscol = ar.take([P, nchunks + 4], F32)
        acc_col = ar.take([P, 4], F32)

        base = lsem.n
        ldf(stage[0][:], src_dram[0])
        if nchunks > 1:
            ldf(stage[1][:], src_dram[1])
        for i in range(nchunks):
            a.wait_ge(lsem.h, base + 16 * (i + 1))
            self.sA.inc(a.activation(junk[:], stage[i % 2][:], Act.Abs, bias=0.0,
                                     scale=1.0, accum_out=abscol[:, i:i + 1]))
            if i + 2 < nchunks:
                if use_ld2:
                    nc.gpsimd.wait_ge(self.sA.h, self.sA.n)
                else:
                    nc.sync.wait_ge(self.sA.h, self.sA.n)
                ldf(stage[i % 2][:], src_dram[i + 2])
        self.wA(v)
        self.vf(v.tensor_reduce(acc_col[:, 0:1], abscol[:, 0:nchunks], AX, Alu.add))
        self.wV(te)
        self.wG(te)
        self.sT.inc(te.matmul(self.ps[0][0:1, 0:1], self.ones_col[:], acc_col[:, 0:1],
                              start=True, stop=True))
        self.wT(v)
        inv_n = 1.0 / float(nchunks * P * width)
        self.vf(v.tensor_scalar(self.scal[:, mu_slot + 1:mu_slot + 2], self.ps[0][0:1, 0:1],
                                inv_n, 1e-5, Alu.mult, Alu.max))
        self.vf(v.reciprocal(self.scal[:, mu_slot:mu_slot + 1],
                             self.scal[:, mu_slot + 1:mu_slot + 2]))
        self.vf(v.reciprocal(self.scal[:, mu_slot + 1:mu_slot + 2],
                             self.scal[:, mu_slot:mu_slot + 1]))
        self.wV(te)
        self.sT.inc(te.matmul(self.ps[1][:, 0:1], self.ones_row[:],
                              self.scal[:, mu_slot:mu_slot + 1], start=True, stop=True))
        self.wT(v)
        self.vf(v.tensor_copy(self.swcol[:], self.ps[1][:, 0:1]))

        base = lsem.n
        ldf(stage[0][:], src_dram[0])
        if nchunks > 1:
            ldf(stage[1][:], src_dram[1])
        st_base = self.sS2.n
        for i in range(nchunks):
            v.wait_ge(lsem.h, base + 16 * (i + 1))
            dst = wq_dst[i][:] if wq_dst is not None else qstage[i % 2][:]
            if wq_dst is None and i >= 2:
                v.wait_ge(self.sS2.h, st_base + 16 * (i - 1))
            v.tensor_scalar(junk[:], stage[i % 2][:], self.swcol[:], None, Alu.mult)
            if not self.shadow:
                v.tensor_scalar(i16[:], junk[:], 0.0, None, Alu.add)
                self.sV.inc(v.tensor_scalar(dst, i16[:], 1.0, -1.0, Alu.min, Alu.max))
            else:
                v.tensor_scalar(junk[:], junk[:], 256.5, None, Alu.add)
                v.tensor_scalar(i16[:], junk[:], 0.0, None, Alu.add)
                cg = stage[i % 2]
                v.tensor_tensor(cg[:], i16[:], junk[:], Alu.is_gt)
                v.scalar_tensor_tensor(junk[:], i16[:], 256.0, cg[:], Alu.subtract,
                                       Alu.subtract)
                self.sV.inc(v.tensor_scalar(dst, junk[:], 1.0, -1.0, Alu.min, Alu.max))
            nc.sync.wait_ge(self.sV.h, self.sV.n)
            self.st2(st_dram[i], dst)
            if i + 2 < nchunks:
                if use_ld2:
                    nc.gpsimd.wait_ge(self.sV.h, self.sV.n)
                ldf(stage[i % 2][:], src_dram[i + 2])

    def pw(self):
        nc = self.nc
        v, te = nc.vector, nc.tensor
        self.w1q = [self.ar.take([P, HD], BF16, at=self.R_W + i * HD * 2) for i in range(ND)]
        self.w2q = [self.ar.take([P, C], BF16, at=self.R_W + i * C * 2) for i in range(NH)]
        self.ternarize(self.w1t, ND, HD, self.w1q, self.w1q_st, mu_slot=0)
        self.ternarize(self.w2t, NH, C, None, self.w2q_st, mu_slot=2,
                       region=self.R_MED, use_ld2=True)
        self.wV(te)
        self.sT.inc(te.matmul(self.ps[2][:, 0:1], self.ones_row[:], self.scal[:, 1:2],
                              start=True, stop=True))
        self.wT(v)
        self.vf(v.tensor_copy(self.mu1c[:], self.ps[2][:, 0:1]))
        self.vf(v.tensor_scalar(self.wlm[:], self.wl[:], self.mu1c[:], None, Alu.mult))
        self.vf(v.tensor_scalar(self.wcm[:], self.wcv[:], self.mu1c[:], None, Alu.mult))
        self.vf(v.tensor_scalar(self.wrm[:], self.wr2[:], self.mu1c[:], None, Alu.mult))
        self.vf(v.tensor_scalar(self.scal[:, 4:5], self.scal[:, 3:4], 0.5, None, Alu.mult))

    def ln_stats(self, xblk, sq_strips, bw, ps_a, ps_b, act_guard=False):
        nc = self.nc
        v, a, te, g = nc.vector, nc.scalar, nc.tensor, nc.gpsimd
        w64 = max(bw, 64)
        self.wL(a)
        self.wV(a)
        self.wG(a)
        for dd in range(ND):
            ins = a.activation(sq_strips[dd][:, 0:w64], xblk[dd][:, 0:w64], Act.Square,
                               bias=0.0, scale=1.0)
        self.sA.inc(ins)
        self.wA(v)
        self.wA(g)
        self.wV(g)
        v.wait_ge(self.sT.h, getattr(self, 'last_tp', 0))
        xs, qs = self.t_xs, self.t_qs
        v.tensor_tensor(xs[:, 0:w64], xblk[0][:, 0:w64], xblk[1][:, 0:w64], Alu.add)
        g.tensor_tensor(qs[:, 0:w64], sq_strips[0][:, 0:w64], sq_strips[1][:, 0:w64],
                        Alu.add)
        for dd in range(2, ND):
            iv = v.tensor_tensor(xs[:, 0:w64], xs[:, 0:w64], xblk[dd][:, 0:w64], Alu.add)
            ig = g.tensor_tensor(qs[:, 0:w64], qs[:, 0:w64], sq_strips[dd][:, 0:w64],
                                 Alu.add)
        self.sV.inc(iv)
        self.sG.inc(ig)
        self.wV(te)
        self.wG(te)
        if act_guard:
            self.wA(te)
        te.matmul(ps_a[0:1, 0:bw], self.ones_col[:], xs[:, 0:bw], start=True, stop=True)
        self.sT.inc(te.matmul(ps_b[0:1, 0:bw], self.ones_col[:], qs[:, 0:bw],
                              start=True, stop=True))

    def ln_rows(self, bw, ps_a, ps_b):
        nc = self.nc
        v, a = nc.vector, nc.scalar
        w64 = max(bw, 64)
        self.wT(v)
        v.tensor_scalar(self.rowA[0:1, 0:w64], ps_a[0:1, 0:w64], 1.0 / C, None, Alu.mult)
        v.tensor_scalar(self.rowB[0:1, 0:w64], ps_b[0:1, 0:w64], 1.0 / C, None, Alu.mult)
        v.tensor_tensor(self.rowC[0:1, 0:w64], self.rowA[0:1, 0:w64],
                        self.rowA[0:1, 0:w64], Alu.mult)
        v.tensor_tensor(self.rowB[0:1, 0:w64], self.rowB[0:1, 0:w64],
                        self.rowC[0:1, 0:w64], Alu.subtract)
        self.sV.inc(v.tensor_scalar(self.rowB[0:1, 0:w64], self.rowB[0:1, 0:w64], 0.0,
                                    None, Alu.max))
        self.wV(a)
        self.sA.inc(a.activation(self.rowC[0:1, 0:w64], self.rowB[0:1, 0:w64], Act.Sqrt,
                                 bias=self.eps_col[0:1, 0:1], scale=1.0))
        self.wA(v)
        self.sV.inc(v.reciprocal(self.rowD[0:1, 0:w64], self.rowC[0:1, 0:w64]))

    def quant1(self, c0, c1, in_dram, xblk, sq, xqd, tmpB, preloaded=False,
               next_rng=None):
        nc = self.nc
        v, a, te, g = nc.vector, nc.scalar, nc.tensor, nc.gpsimd
        bw = c1 - c0
        w64 = max(bw, 64)
        if not preloaded:
            for dd in range(ND):
                self.ld(xblk[dd][:, 0:bw], in_dram[dd][:, c0:c1])
        self.ln_stats(xblk, sq, bw, self.ps[0], self.ps[1])
        self.ln_rows(bw, self.ps[0], self.ps[1])
        self.wV(te)
        te.wait_ge(self.sA.h, getattr(self, 's2b_done', 0))
        self.sT.inc(te.matmul(self.ps[2][:, 0:bw], self.ones_row[:], self.rowA[0:1, 0:bw],
                              start=True, stop=True))
        self.wT(v)
        self.wT(g)
        self.wG(v)
        for dd in range(ND):
            self.sV.inc(v.tensor_tensor(sq[dd][:, 0:w64], xblk[dd][:, 0:w64],
                                        self.ps[2][:, 0:w64], Alu.subtract))
        if next_rng is not None:
            nc.sync.wait_ge(self.sV.h, self.sV.n)
            nc0, nc1 = next_rng
            for dd in range(ND):
                self.ld(xblk[dd][:, 0:nc1 - nc0], in_dram[dd][:, nc0:nc1])
        self.wV(g)
        xs = self.t_xs
        tmps = [self.t_qs, tmpB]
        vmax_done = []
        for dd in range(ND):
            if dd >= 2:
                g.wait_ge(self.sV.h, vmax_done[dd - 2])
            self.sG.inc(g.tensor_tensor(tmps[dd % 2][:, 0:w64], sq[dd][:, 0:w64],
                                        sq[dd][:, 0:w64], Alu.mult))
            v.wait_ge(self.sG.h, self.sG.n)
            if dd == 0:
                iv = v.tensor_copy(xs[:, 0:w64], tmps[0][:, 0:w64])
            else:
                iv = v.tensor_tensor(xs[:, 0:w64], xs[:, 0:w64], tmps[dd % 2][:, 0:w64],
                                     Alu.max)
            self.sV.inc(iv)
            vmax_done.append(self.sV.n)
        self.wV(te)
        nsb = (bw + 127) // 128
        for j in range(nsb):
            sw_ = min(128, bw - j * 128)
            it = te.transpose(self.ps[3][0:sw_, j * 128:j * 128 + 128],
                              xs[:, j * 128:j * 128 + sw_], self.iden[:])
        self.sT.inc(it)
        self.last_tp = self.sT.n
        v.wait_ge(self.sT.h, self.last_tp)
        self.wA(v)
        for j in range(nsb):
            sw_ = min(128, bw - j * 128)
            self.vf(v.tensor_reduce(self.mxcol[0:sw_, j:j + 1],
                                    self.ps[3][0:sw_, j * 128:(j + 1) * 128], AX, Alu.max))
        self.wV(a)
        self.af(a.activation(self.mxcol[:, 0:4], self.mxcol[:, 0:4], Act.Sqrt,
                             bias=0.0, scale=1.0))
        self.wA(te)
        self.wV(te)
        for j in range(nsb):
            it = te.transpose(self.ps[3][0:1, j * 128:(j + 1) * 128],
                              self.mxcol[:, j:j + 1], self.iden[:])
        self.sT.inc(it)
        self.wT(v)
        v.tensor_copy(self.rowC[0:1, 0:w64], self.ps[3][0:1, 0:w64])
        v.tensor_tensor(self.rowC[0:1, 0:w64], self.rowD[0:1, 0:w64],
                        self.rowC[0:1, 0:w64], Alu.mult)
        v.tensor_scalar(self.rowC[0:1, 0:w64], self.rowC[0:1, 0:w64], 1e-5, None, Alu.max)
        v.reciprocal(self.rowB[0:1, 0:w64], self.rowC[0:1, 0:w64])
        self.sV.inc(v.tensor_scalar(self.rowB[0:1, 0:w64], self.rowB[0:1, 0:w64], 127.0,
                                    None, Alu.mult))
        self.sV.inc(v.reciprocal(self.rowC[0:1, 0:w64], self.rowB[0:1, 0:w64]))
        self.sV.inc(v.tensor_tensor(self.rowD[0:1, 0:w64], self.rowD[0:1, 0:w64],
                                    self.rowB[0:1, 0:w64], Alu.mult))
        self.wV(te)
        te.matmul(self.ps[2][:, 0:bw], self.ones_row[:], self.rowD[0:1, 0:bw],
                  start=True, stop=True)
        self.sT.inc(te.matmul(self.ps[3][:, 0:bw], self.ones_row[:], self.rowC[0:1, 0:bw],
                              start=True, stop=True))
        self.wT(v)
        i16 = self.t_i16
        shift = 256.5 if self.shadow else 0.0
        for dd in range(ND):
            v.tensor_tensor(sq[dd][:, 0:w64], sq[dd][:, 0:w64], self.ps[2][:, 0:w64],
                            Alu.mult)
            if not self.shadow:
                v.tensor_scalar(i16[:, 0:w64], sq[dd][:, 0:w64], 0.0, None, Alu.add)
                iv = v.scalar_tensor_tensor(xqd[dd][:, 0:w64], i16[:, 0:w64], 0.0,
                                            self.ps[3][:, 0:w64], Alu.subtract, Alu.mult)
            else:
                cg = self.t_cg
                v.tensor_scalar(sq[dd][:, 0:w64], sq[dd][:, 0:w64], shift, None, Alu.add)
                v.tensor_scalar(i16[:, 0:w64], sq[dd][:, 0:w64], 0.0, None, Alu.add)
                v.tensor_tensor(cg[:, 0:w64], i16[:, 0:w64], sq[dd][:, 0:w64], Alu.is_gt)
                v.scalar_tensor_tensor(cg[:, 0:w64], i16[:, 0:w64], 256.0, cg[:, 0:w64],
                                       Alu.subtract, Alu.subtract)
                iv = v.tensor_tensor(xqd[dd][:, 0:w64], cg[:, 0:w64], self.ps[3][:, 0:w64],
                                     Alu.mult)
            self.sV.inc(iv)
        self.wV(v)

    def ffn_p1(self, lo, hi, in_dram, final):
        nc, ar = self.nc, self.ar
        v, a, te, g = nc.vector, nc.scalar, nc.tensor, nc.gpsimd
        ar.cursor = self.R_BIG
        gstr = [ar.take([P, 512], F32) for _ in range(NH)]
        ar.cursor = self.R_MED
        xblk = [ar.take([P, 512], F32) for _ in range(ND)]
        sq = [ar.take([P, 512], F32) for _ in range(ND)]
        xqd = [ar.take([P, 512], BF16) for _ in range(ND)]
        Hs = [ar.take([P, 520], F32) for _ in range(2)]
        tails = ar.take([P, 2 * NH], F32)
        n2o = [ar.take([P, 512], I8) for _ in range(2)]
        rmA = ar.take([P, 512], F32)
        rmB = ar.take([P, 512], F32)
        s2b_s = ar.take([P, 512], F32)
        u_s = self.t_qs

        self.wA(g)
        self.wV(g)
        g.memset(Hs[0][:], 0.0)
        self.gf(g.memset(Hs[1][:], 0.0))

        blocks = _blocks(lo, hi)
        for bi, (c0, c1) in enumerate(blocks):
            bw = c1 - c0
            first, last = (c0 == lo), (c1 == hi)
            g0 = c0 + 1 if first else c0 - 1
            g1 = hi - 1 if last else c1 - 1
            gw = g1 - g0
            gw64 = max(gw, 64)
            nxt = blocks[bi + 1] if bi + 1 < len(blocks) else None
            self.quant1(c0, c1, in_dram, xblk, sq, xqd, rmA,
                        preloaded=(bi > 0), next_rng=nxt)
            if bi == 0:
                self.wG(a)
            mm_base = self.sT.n
            ev_base = self.sA.n
            self.rm_done = self.sV.n
            conv_done = []
            t1_done = []
            silu_done = []
            o_l = g0 - c0 + 1
            for ch in range(NH):
                pch = self.ps[4 + (ch % 4)]
                if ch >= 4:
                    te.wait_ge(self.sA.h, ev_base + 3 * (ch - 4) + 1)
                elif ch == 0:
                    self.wV(te)
                for dd in range(ND):
                    mm = te.matmul(pch[:, 0:bw], self.w1q[dd][:, ch * 128:(ch + 1) * 128],
                                   xqd[dd][:, 0:bw], start=(dd == 0), stop=(dd == ND - 1))
                self.sT.inc(mm)
                hb = Hs[ch % 2]
                a.wait_ge(self.sT.h, mm_base + ch + 1)
                if ch >= 2:
                    a.wait_ge(self.sV.h, conv_done[ch - 2])
                self.sA.inc(a.copy(hb[:, 2:2 + bw], pch[:, 0:bw]))
                v.wait_ge(self.sA.h, ev_base + 3 * ch + 1)
                if not first:
                    self.sV.inc(v.tensor_copy(hb[:, 0:2], tails[:, 2 * ch:2 * ch + 2]))
                if final and last:
                    lc = (hi - 1 - c0) + 2
                    self.sV.inc(v.tensor_scalar(hb[:, lc:lc + 1], hb[:, lc:lc + 1],
                                                self.endmask[:], None, Alu.mult))
                if final and first:
                    self.sV.inc(v.tensor_scalar(hb[:, 2:3], hb[:, 2:3],
                                                self.startmask[:], None, Alu.mult))
                self.wV(v)
                hb_ready = self.sV.n
                wlc = self.wlm[:, ch:ch + 1]
                wcc = self.wcm[:, ch:ch + 1]
                wrc = self.wrm[:, ch:ch + 1]
                t1 = sq[ch % 2]
                ub = sq[2 + ch % 2]
                a.wait_ge(self.sV.h, hb_ready)
                self.sA.inc(a.activation(t1[:, 0:gw64], hb[:, o_l:o_l + gw64], Act.Copy,
                                         bias=0.0, scale=wlc))
                if ch >= 2:
                    v.wait_ge(self.sA.h, silu_done[ch - 2])
                v.wait_ge(self.sA.h, ev_base + 3 * ch + 2)
                v.scalar_tensor_tensor(ub[:, 0:gw64], hb[:, o_l + 1:o_l + 1 + gw64], wcc,
                                       t1[:, 0:gw64], Alu.mult, Alu.add)
                self.sV.inc(v.scalar_tensor_tensor(ub[:, 0:gw64],
                                                   hb[:, o_l + 2:o_l + 2 + gw64], wrc,
                                                   ub[:, 0:gw64], Alu.mult, Alu.add))
                self.sV.inc(v.tensor_copy(tails[:, 2 * ch:2 * ch + 2], hb[:, bw:bw + 2]))
                conv_done.append(self.sV.n)
                a.wait_ge(self.sV.h, conv_done[-1])
                self.sA.inc(a.activation(gstr[ch][:, 0:gw64], ub[:, 0:gw64], Act.Silu,
                                         bias=self.cb[:, ch:ch + 1], scale=1.0))
                silu_done.append(self.sA.n)
                g.wait_ge(self.sA.h, silu_done[-1])
                if ch >= 1:
                    g.wait_ge(self.sV.h, self.rm_done)
                src0 = gstr[ch][:, 0:gw64]
                self.sG.inc(g.tensor_tensor(rmB[:, 0:gw64], src0, src0, Alu.mult))
                v.wait_ge(self.sG.h, self.sG.n)
                if ch == 0:
                    iv = v.tensor_copy(rmA[:, 0:gw64], rmB[:, 0:gw64])
                else:
                    iv = v.tensor_tensor(rmA[:, 0:gw64], rmA[:, 0:gw64], rmB[:, 0:gw64],
                                         Alu.max)
                self.sV.inc(iv)
                self.rm_done = self.sV.n
            self.wG(te)
            self.wV(te)
            nsb = (gw + 127) // 128
            for j in range(nsb):
                sw_ = min(128, gw - j * 128)
                it = te.transpose(self.ps[0][0:sw_, j * 128:j * 128 + 128],
                                  rmA[:, j * 128:j * 128 + sw_], self.iden[:])
            self.sT.inc(it)
            self.wT(v)
            for j in range(nsb):
                sw_ = min(128, gw - j * 128)
                self.vf(v.tensor_reduce(self.mxcol[0:sw_, j:j + 1],
                                        self.ps[0][0:sw_, j * 128:(j + 1) * 128], AX,
                                        Alu.max))
            self.wV(a)
            self.af(a.activation(self.mxcol[:, 0:4], self.mxcol[:, 0:4], Act.Sqrt,
                                 bias=0.0, scale=1.0))
            self.wA(v)
            self.vf(v.tensor_scalar(self.mxcol[:, 0:4], self.mxcol[:, 0:4], 1e-5, None,
                                    Alu.max))
            self.vf(v.reciprocal(self.s2col[:, 0:4], self.mxcol[:, 0:4]))
            self.vf(v.tensor_scalar(self.s2col[:, 0:4], self.s2col[:, 0:4], 127.0,
                                    None, Alu.mult))
            self.wV(te)
            for j in range(nsb):
                it = te.transpose(self.ps[1][0:1, j * 128:(j + 1) * 128],
                                  self.s2col[:, j:j + 1], self.iden[:])
            self.sT.inc(it)
            self.wT(v)
            self.sV.inc(v.tensor_copy(self.row_s2[0:1, 0:gw64], self.ps[1][0:1, 0:gw64]))
            v.tensor_scalar(self.row_s2[0:1, 0:gw64], self.row_s2[0:1, 0:gw64], 1e-5,
                            None, Alu.max)
            self.sV.inc(v.reciprocal(self.row_s2i[0:1, g0:g0 + gw64],
                                     self.row_s2[0:1, 0:gw64]))
            self.wV(te)
            self.sT.inc(te.matmul(self.ps[2][:, 0:gw], self.ones_row[:],
                                  self.row_s2[0:1, 0:gw], start=True, stop=True))
            self.wT(a)
            self.sA.inc(a.copy(s2b_s[:, 0:gw], self.ps[2][:, 0:gw]))
            self.s2b_done = self.sA.n
            st_base = self.sS2.n
            for ch in range(NH):
                eng, sem = (g, self.sG) if (ch % 2) else (v, self.sV)
                eng.wait_ge(self.sA.h, self.s2b_done)
                sem.inc(eng.tensor_tensor(gstr[ch][:, 0:gw64], gstr[ch][:, 0:gw64],
                                          s2b_s[:, 0:gw64], Alu.mult))
                nb = n2o[ch % 2]
                v.wait_ge(sem.h, sem.n)
                if ch >= 2:
                    v.wait_ge(self.sS2.h, st_base + 16 * (ch - 1))
                if not self.shadow:
                    iv = v.tensor_scalar(nb[:, 0:gw64], gstr[ch][:, 0:gw64], 0.0, None,
                                         Alu.add)
                else:
                    v.tensor_scalar(self.t_cg[:, 0:gw64], gstr[ch][:, 0:gw64], 256.5,
                                    None, Alu.add)
                    v.tensor_scalar(self.t_i16[:, 0:gw64], self.t_cg[:, 0:gw64], 0.0,
                                    None, Alu.add)
                    v.tensor_tensor(u_s[:, 0:gw64], self.t_i16[:, 0:gw64],
                                    self.t_cg[:, 0:gw64], Alu.is_gt)
                    v.scalar_tensor_tensor(u_s[:, 0:gw64], self.t_i16[:, 0:gw64], 256.0,
                                           u_s[:, 0:gw64], Alu.subtract, Alu.subtract)
                    iv = v.tensor_scalar(nb[:, 0:gw64], u_s[:, 0:gw64], 0.0, None, Alu.add)
                self.sV.inc(iv)
                nc.sync.wait_ge(self.sV.h, self.sV.n)
                self.st2(self.n2_st[ch][:, g0:g1], nb[:, 0:gw])

    def ffn_p2(self, lo, hi, in_dram, out_dram, final):
        nc, ar = self.nc, self.ar
        v, a, te = nc.vector, nc.scalar, nc.tensor
        ar.cursor = self.R_BIG
        n2b = [ar.take([P, 512], BF16) for _ in range(NH)]
        n2i = [[ar.take([P, 512], I8) for _ in range(NH)] for _ in range(2)]
        ar.cursor = self.R_MED
        xblk = [[ar.take([P, 512], F32) for _ in range(ND)] for _ in range(2)]
        ostr = [ar.take([P, 512], F32) for _ in range(2)]
        c2s = ar.take([P, 512], F32)

        blocks2 = _blocks(lo + 1, hi - 1)
        nc.sync.wait_ge(self.sA.h, self.sA.n)
        nc.sync.wait_ge(self.sV.h, self.sV.n)
        g0_, g1_ = blocks2[0]
        for ch in range(NH):
            self.ld(n2i[0][ch][:, 0:g1_ - g0_], self.n2_st[ch][:, g0_:g1_])
        n2_ready = self.sL.n
        for dd in range(ND):
            self.ld(xblk[0][dd][:, 0:g1_ - g0_], in_dram[dd][:, g0_:g1_])
        xd_ready = self.sL.n
        cast_end = 0
        add_end = 0
        for bi, (g0, g1) in enumerate(blocks2):
            gw = g1 - g0
            gw64 = max(gw, 64)
            pb = bi % 2
            a.wait_ge(self.sL.h, n2_ready)
            self.wT(a)
            for ch in range(NH):
                ia = a.copy(n2b[ch][:, 0:gw], n2i[pb][ch][:, 0:gw])
            self.sA.inc(ia)
            this_cast = self.sA.n
            this_xd = xd_ready
            this_x = xblk[pb]
            if bi + 1 < len(blocks2):
                n0, n1 = blocks2[bi + 1]
                if cast_end:
                    nc.sync.wait_ge(self.sA.h, cast_end)
                for ch in range(NH):
                    self.ld(n2i[1 - pb][ch][:, 0:n1 - n0], self.n2_st[ch][:, n0:n1])
                n2_ready = self.sL.n
                if add_end:
                    nc.sync.wait_ge(self.sV.h, add_end)
                for dd in range(ND):
                    self.ld(xblk[1 - pb][dd][:, 0:n1 - n0], in_dram[dd][:, n0:n1])
                xd_ready = self.sL.n
            cast_end = this_cast
            self.wA(v)
            self.sV.inc(v.tensor_scalar(self.rowA[0:1, 0:gw64],
                                        self.row_s2i[0:1, g0:g0 + gw64],
                                        self.scal[:, 4:5], None, Alu.mult))
            self.wV(te)
            self.sT.inc(te.matmul(self.ps[2][:, 0:gw], self.ones_row[:],
                                  self.rowA[0:1, 0:gw], start=True, stop=True))
            self.wT(a)
            self.sA.inc(a.copy(c2s[:, 0:gw], self.ps[2][:, 0:gw]))
            c2s_done = self.sA.n
            mm_base = self.sT.n
            ev_base = self.sV.n
            ost_base = self.sO.n if final else self.sS2.n
            for od in range(ND):
                pso = self.ps[6 + (od % 2)]
                if od >= 2:
                    te.wait_ge(self.sV.h, ev_base + (od - 1))
                else:
                    te.wait_ge(self.sA.h, this_cast)
                    self.wV(te)
                for ch in range(NH):
                    mm = te.matmul(pso[:, 0:gw], self.w2q[ch][:, od * 128:(od + 1) * 128],
                                   n2b[ch][:, 0:gw], start=(ch == 0), stop=(ch == NH - 1))
                self.sT.inc(mm)
                ob = ostr[od % 2]
                v.wait_ge(self.sT.h, mm_base + od + 1)
                v.wait_ge(self.sL.h, this_xd)
                v.wait_ge(self.sA.h, c2s_done)
                if od >= 2:
                    sem = self.sO if final else self.sS2
                    v.wait_ge(sem.h, ost_base + 16 * (od - 1))
                v.tensor_tensor(ob[:, 0:gw64], pso[:, 0:gw64], c2s[:, 0:gw64], Alu.mult)
                self.sV.inc(v.tensor_tensor(ob[:, 0:gw64], ob[:, 0:gw64],
                                            this_x[od][:, 0:gw64], Alu.add))
                nc.sync.wait_ge(self.sV.h, self.sV.n)
                if final:
                    self.st(out_dram[od][:, g0 - 3:g1 - 3], ob[:, 0:gw], final=True)
                else:
                    self.st2(out_dram[od][:, g0:g1], ob[:, 0:gw])
            add_end = self.sV.n

    def ffn(self, lo, hi, in_dram, out_dram, final):
        nc = self.nc
        self.ffn_p1(lo, hi, in_dram, final)
        nc.sync.wait_ge(self.sT.h, self.sT.n)
        self.syncS()
        for i in range(NH):
            self.ld(self.w2q[i][:], self.w2q_st[i])
        w2q_ready = self.sL.n
        nc.tensor.wait_ge(self.sL.h, w2q_ready)
        self.ffn_p2(lo, hi, in_dram, out_dram, final)

    def ln_only(self, c0, c1, in_dram, xblk, xn_out, preloaded=False, next_rng=None):
        nc = self.nc
        v, te = nc.vector, nc.tensor
        bw = c1 - c0
        if not preloaded:
            nc.sync.wait_ge(self.sV.h, self.sV.n)
            for dd in range(ND):
                self.ld(xblk[dd][:, 0:bw], in_dram[dd][:, c0:c1])
        self.ln_stats(xblk, xn_out, bw, self.ps[0], self.ps[1], act_guard=True)
        self.ln_rows(bw, self.ps[0], self.ps[1])
        self.wV(te)
        te.matmul(self.ps[2][:, 0:bw], self.ones_row[:], self.rowA[0:1, 0:bw],
                  start=True, stop=True)
        self.sT.inc(te.matmul(self.ps[3][:, 0:bw], self.ones_row[:], self.rowD[0:1, 0:bw],
                              start=True, stop=True))
        self.wT(v)
        w64 = max(bw, 64)
        for dd in range(ND):
            v.tensor_tensor(xn_out[dd][:, 0:w64], xblk[dd][:, 0:w64], self.ps[2][:, 0:w64],
                            Alu.subtract)
            iv = v.tensor_tensor(xn_out[dd][:, 0:w64], xn_out[dd][:, 0:w64],
                                 self.ps[3][:, 0:w64], Alu.mult)
        self.sV.inc(iv)
        if next_rng is not None:
            nc.sync.wait_ge(self.sV.h, self.sV.n)
            n0, n1 = next_rng
            for dd in range(ND):
                self.ld(xblk[dd][:, 0:n1 - n0], in_dram[dd][:, n0:n1])

    def attn(self):
        nc, ar = self.nc, self.ar
        v, a, te, g = nc.vector, nc.scalar, nc.tensor, nc.gpsimd
        AW = 256
        wbufA = [self.ar.take([P, C], BF16, at=self.R_W + i * C * 2) for i in range(ND)]
        wbufB = [self.ar.take([P, C], BF16, at=self.R_W + (ND + i) * C * 2)
                 for i in range(ND)]
        ar.cursor = self.R_BIG
        PASS2_OFF = ar.cursor
        xnw = [[ar.take([P, AW], F32) for _ in range(ND)] for _ in range(2)]
        ramp = ar.take([P, TC], F32)
        ar.cursor = self.R_MED
        kwin = [ar.take([P, AW], F32) for _ in range(ND)]
        vwin = [ar.take([P, AW], F32) for _ in range(ND)]
        xb1 = [ar.take([P, AW], F32) for _ in range(ND)]
        xk_b = [ar.take([P, AW], BF16) for _ in range(ND)]
        xv_b = [ar.take([P, AW], BF16) for _ in range(ND)]
        xr_b = [ar.take([P, AW], BF16) for _ in range(2)]
        t_a = ar.take([P, AW], F32)
        sumbuf = ar.take([P, 24], F32)
        kapw = ar.take([P, ND], F32)
        kapn = ar.take([P, 1], F32)
        sa_w = ar.take([P, ND], F32)
        sb_w = ar.take([P, ND], F32)
        tmp8 = ar.take([P, ND], F32)
        tmp8b = ar.take([P, ND], F32)
        tmp8c = ar.take([P, ND], F32)
        aggb = ar.take([P, 8 * 24], F32)
        ccs = ar.take([P, 24], F32)

        self.wV(g)
        self.wA(g)
        self.sG.inc(g.iota(ramp[:], [[1, TC]], channel_multiplier=0,
                           allow_small_or_imprecise_dtypes=True))

        self.syncS()
        self.wV(g)
        if self.sS2.n:
            g.wait_ge(self.sS2.h, self.sS2.n)
        for i in range(ND):
            self.ld2(wbufA[i][:], self.wkt[i])
            self.ld2(wbufB[i][:], self.wvt[i])
        wk_ready = self.sL2.n

        ln_blocks = _blocks(1, W - 1, AW)
        mr_blocks = _blocks(2, W - 1, AW)
        nb = len(ln_blocks)
        xr_sts = []

        def ln2(j):
            nxt = ln_blocks[j + 1] if j + 1 < len(ln_blocks) else None
            self.ln_only(ln_blocks[j][0], ln_blocks[j][1], self.xres_a, xb1, xnw[j % 2],
                         preloaded=(j > 0), next_rng=nxt)
            if j == 0:
                self.wV(v)
                for dd in range(ND):
                    self.sV.inc(v.tensor_scalar(xnw[0][dd][:, 1:2], xnw[0][dd][:, 1:2],
                                                self.startmask[:], None, Alu.mult))
                self.wV(v)

        ln2(0)
        for j in range(len(mr_blocks)):
            if j + 1 < nb:
                ln2(j + 1)
            c0, c1 = mr_blocks[j]
            bw = c1 - c0
            bw64 = max(bw, 64)
            xj = xnw[j % 2]
            xj1 = xnw[(j + 1) % 2]
            l0 = c0 - ln_blocks[j][0]
            wid_in = ln_blocks[j][1] - c0
            for dd in range(ND):
                if wid_in >= bw64:
                    v.tensor_tensor(t_a[:, 0:bw64], xj[dd][:, l0:l0 + bw64],
                                    xj[dd][:, l0 - 1:l0 - 1 + bw64], Alu.subtract)
                elif wid_in >= bw:
                    wd = min(bw64, AW - l0)
                    v.tensor_tensor(t_a[:, 0:wd], xj[dd][:, l0:l0 + wd],
                                    xj[dd][:, l0 - 1:l0 - 1 + wd], Alu.subtract)
                else:
                    v.tensor_tensor(t_a[:, 0:wid_in], xj[dd][:, l0:l0 + wid_in],
                                    xj[dd][:, l0 - 1:l0 - 1 + wid_in], Alu.subtract)
                    self.sV.inc(v.tensor_tensor(t_a[:, wid_in:bw],
                                                xj1[dd][:, 0:bw - wid_in],
                                                xj[dd][:, l0 - 1 + wid_in:l0 - 1 + bw],
                                                Alu.subtract))
                    self.wV(v)
                sh0 = l0 - 1
                wd = min(bw64, AW - sh0)
                v.scalar_tensor_tensor(xk_b[dd][:, 0:wd], t_a[:, 0:wd],
                                       self.mk[:, dd:dd + 1], xj[dd][:, sh0:sh0 + wd],
                                       Alu.mult, Alu.add)
                v.scalar_tensor_tensor(xv_b[dd][:, 0:wd], t_a[:, 0:wd],
                                       self.mv[:, dd:dd + 1], xj[dd][:, sh0:sh0 + wd],
                                       Alu.mult, Alu.add)
                if len(xr_sts) >= 2:
                    v.wait_ge(self.sS2.h, xr_sts[-2])
                self.sV.inc(v.scalar_tensor_tensor(xr_b[dd % 2][:, 0:wd], t_a[:, 0:wd],
                                                   self.mr[:, dd:dd + 1],
                                                   xj[dd][:, sh0:sh0 + wd],
                                                   Alu.mult, Alu.add))
                nc.sync.wait_ge(self.sV.h, self.sV.n)
                self.st2(self.xr_st[dd][:, c0 - 2:c0 - 2 + bw], xr_b[dd % 2][:, 0:bw])
                xr_sts.append(self.sS2.n)
            self.wV(te)
            te.wait_ge(self.sL2.h, wk_ready)
            a.wait_ge(self.sS2.h, getattr(self, '_kv_stash_done', 0))
            kv_ev = self.sA.n
            for oc in range(ND):
                if oc >= 2:
                    te.wait_ge(self.sA.h, kv_ev + (oc - 1))
                for dd in range(ND):
                    mm = te.matmul(self.ps[0 + (oc % 2)][:, 0:bw],
                                   wbufA[dd][:, oc * 128:(oc + 1) * 128],
                                   xk_b[dd][:, 0:bw], start=(dd == 0), stop=(dd == ND - 1))
                self.sT.inc(mm)
                self.wT(a)
                self.sA.inc(a.copy(kwin[oc][:, 0:bw], self.ps[0 + (oc % 2)][:, 0:bw]))
            kv_ev = self.sA.n
            for oc in range(ND):
                if oc >= 2:
                    te.wait_ge(self.sA.h, kv_ev + (oc - 1))
                for dd in range(ND):
                    mm = te.matmul(self.ps[2 + (oc % 2)][:, 0:bw],
                                   wbufB[dd][:, oc * 128:(oc + 1) * 128],
                                   xv_b[dd][:, 0:bw], start=(dd == 0), stop=(dd == ND - 1))
                self.sT.inc(mm)
                self.wT(a)
                self.sA.inc(a.copy(vwin[oc][:, 0:bw], self.ps[2 + (oc % 2)][:, 0:bw]))
            self.wA(v)
            if j == 0:
                for oc in range(ND):
                    self.sV.inc(v.tensor_tensor(kwin[oc][:, 0:1], kwin[oc][:, 0:1],
                                                self.startneg[:], Alu.add))
                self.wV(v)
            nc.sync.wait_ge(self.sV.h, self.sV.n)
            nc.sync.wait_ge(self.sA.h, self.sA.n)
            for oc in range(ND):
                self.st2(self.kv_st[0, oc][:, c0 - 2:c0 - 2 + bw], kwin[oc][:, 0:bw])
                self.st2(self.kv_st[1, oc][:, c0 - 2:c0 - 2 + bw], vwin[oc][:, 0:bw])
            self._kv_stash_done = self.sS2.n
            if c0 < TC + 2:
                sw = min(c1, TC + 2) - c0
                l_off = c0 - 2
                self.wG(v)
                for oc in range(ND):
                    v.scalar_tensor_tensor(t_a[:, 0:sw], ramp[:, l_off:l_off + sw],
                                           self.absw[:, oc:oc + 1], kwin[oc][:, 0:sw],
                                           Alu.mult, Alu.add)
                    self.vf(v.tensor_reduce(kapw[:, oc:oc + 1], t_a[:, 0:sw], AX, Alu.max))
                    self.vf(v.tensor_scalar(kapn[:, 0:1], kapw[:, oc:oc + 1], -1.0,
                                            None, Alu.mult))
                    self.wV(a)
                    self.sA.inc(a.activation(t_a[:, 0:sw], t_a[:, 0:sw], Act.Exp,
                                             bias=kapn[:, 0:1], scale=1.0,
                                             accum_out=sb_w[:, oc:oc + 1]))
                    self.wA(v)
                    self.vf(v.scalar_tensor_tensor(t_a[:, 0:sw], t_a[:, 0:sw], 1.0,
                                                   vwin[oc][:, 0:sw], Alu.mult, Alu.mult,
                                                   accum_out=sa_w[:, oc:oc + 1]))
                if j == 0:
                    self.vf(v.tensor_copy(sumbuf[:, 0:8], sa_w[:]))
                    self.vf(v.tensor_copy(sumbuf[:, 8:16], sb_w[:]))
                    self.vf(v.tensor_copy(sumbuf[:, 16:24], kapw[:]))
                else:
                    self.vf(v.tensor_tensor(tmp8b[:], sumbuf[:, 16:24], kapw[:], Alu.max))
                    self.vf(v.tensor_tensor(tmp8c[:], sumbuf[:, 16:24], tmp8b[:],
                                            Alu.subtract))
                    self.vf(v.tensor_tensor(tmp8[:], kapw[:], tmp8b[:], Alu.subtract))
                    self.wV(a)
                    self.sA.inc(a.activation(tmp8c[:], tmp8c[:], Act.Exp, bias=0.0,
                                             scale=1.0))
                    self.af(a.activation(tmp8[:], tmp8[:], Act.Exp, bias=0.0, scale=1.0))
                    self.wA(v)
                    self.vf(v.tensor_tensor(sumbuf[:, 0:8], sumbuf[:, 0:8], tmp8c[:],
                                            Alu.mult))
                    self.vf(v.tensor_tensor(sumbuf[:, 8:16], sumbuf[:, 8:16], tmp8c[:],
                                            Alu.mult))
                    self.vf(v.scalar_tensor_tensor(tmp8c[:], sa_w[:], 1.0, tmp8[:],
                                                   Alu.mult, Alu.mult))
                    self.vf(v.tensor_tensor(sumbuf[:, 0:8], sumbuf[:, 0:8], tmp8c[:],
                                            Alu.add))
                    self.vf(v.scalar_tensor_tensor(tmp8c[:], sb_w[:], 1.0, tmp8[:],
                                                   Alu.mult, Alu.mult))
                    self.vf(v.tensor_tensor(sumbuf[:, 8:16], sumbuf[:, 8:16], tmp8c[:],
                                            Alu.add))
                    self.vf(v.tensor_copy(sumbuf[:, 16:24], tmp8b[:]))
        self.vf(v.scalar_tensor_tensor(sumbuf[:, 16:24], self.w_c[:], 2047.0,
                                       sumbuf[:, 16:24], Alu.mult, Alu.add))
        nc.sync.wait_ge(self.sV.h, self.sV.n)
        self.st(self.cc_in[:], sumbuf[:, 0:24])
        g.wait_ge(self.sS.h, self.sS.n)
        self.sC.inc(g.collective_compute(
            "AllGather", Alu.bypass, replica_groups=[[0, 1, 2, 3, 4, 5, 6, 7]],
            ins=[self.cc_in[:]], outs=[self.cc_out[:]]))
        nc.sync.wait_ge(self.sC.h, self.sC.n)
        for j in range(8):
            self.ld(aggb[:, j * 24:(j + 1) * 24], self.cc_out[j])
        self.wL(v)
        self.wL(g)

        def slot(jj, q):
            return aggb[:, jj * 24 + q * 8:jj * 24 + (q + 1) * 8]

        first_cand = True
        for b in range(2):
            self.wV(g)
            g.memset(ccs[:, 0:16], 0.0)
            self.gf(g.memset(ccs[:, 16:24], -1e30))
            self.wG(v)
            for cpos in range(4):
                jj = b * 4 + cpos
                if first_cand:
                    for q in range(3):
                        self.vf(v.tensor_scalar(self.sinp[:, q * 8:(q + 1) * 8],
                                                ccs[:, q * 8:(q + 1) * 8],
                                                self.onehot[:, jj:jj + 1], None, Alu.mult))
                    first_cand = False
                else:
                    for q in range(3):
                        self.vf(v.scalar_tensor_tensor(self.sinp[:, q * 8:(q + 1) * 8],
                                                       ccs[:, q * 8:(q + 1) * 8],
                                                       self.onehot[:, jj:jj + 1],
                                                       self.sinp[:, q * 8:(q + 1) * 8],
                                                       Alu.mult, Alu.add))
                if cpos < 3:
                    self.vf(v.scalar_tensor_tensor(tmp8[:], self.w_c[:], 2048.0,
                                                   ccs[:, 16:24], Alu.mult, Alu.add))
                    self.vf(v.tensor_tensor(tmp8b[:], tmp8[:], slot(jj, 2), Alu.max))
                    self.vf(v.tensor_tensor(tmp8c[:], tmp8[:], tmp8b[:], Alu.subtract))
                    self.vf(v.tensor_tensor(tmp8[:], slot(jj, 2), tmp8b[:], Alu.subtract))
                    self.wV(a)
                    self.af(a.activation(tmp8c[:], tmp8c[:], Act.Exp, bias=0.0, scale=1.0))
                    self.af(a.activation(tmp8[:], tmp8[:], Act.Exp, bias=0.0, scale=1.0))
                    self.wA(v)
                    self.vf(v.tensor_tensor(ccs[:, 0:8], ccs[:, 0:8], tmp8c[:], Alu.mult))
                    self.vf(v.tensor_tensor(ccs[:, 8:16], ccs[:, 8:16], tmp8c[:],
                                            Alu.mult))
                    self.vf(v.scalar_tensor_tensor(tmp8c[:], slot(jj, 0), 1.0, tmp8[:],
                                                   Alu.mult, Alu.mult))
                    self.vf(v.tensor_tensor(ccs[:, 0:8], ccs[:, 0:8], tmp8c[:], Alu.add))
                    self.vf(v.scalar_tensor_tensor(tmp8c[:], slot(jj, 1), 1.0, tmp8[:],
                                                   Alu.mult, Alu.mult))
                    self.vf(v.tensor_tensor(ccs[:, 8:16], ccs[:, 8:16], tmp8c[:], Alu.add))
                    self.vf(v.tensor_copy(ccs[:, 16:24], tmp8b[:]))

        self.wT(g)
        for i in range(ND):
            self.ld2(wbufA[i][:], self.wrt[i])
            self.ld2(wbufB[i][:], self.wot[i])
        wr_ready = self.sL2.n

        ar.cursor = PASS2_OFF
        ppb = [ar.take([P, AW + 8], F32) for _ in range(ND)]
        aab = [ar.take([P, AW + 8], F32) for _ in range(ND)]
        bbb = [ar.take([P, AW + 8], F32) for _ in range(ND)]
        wz = [ar.take([P, AW], F32) for _ in range(ND)]
        T1 = [ar.take([P, AW], F32) for _ in range(ND)]
        T2 = [ar.take([P, AW], F32) for _ in range(ND)]
        ar.cursor = self.R_MED
        kw2 = [ar.take([P, AW], F32) for _ in range(ND)]
        vw2 = [ar.take([P, AW], F32) for _ in range(ND)]
        rw2 = [ar.take([P, AW], F32) for _ in range(ND)]
        ry2 = [ar.take([P, AW], BF16) for _ in range(ND)]
        xr2 = [ar.take([P, AW], BF16) for _ in range(ND)]
        x1o = [ar.take([P, AW], F32) for _ in range(2)]
        T3 = [ar.take([P, AW], F32) for _ in range(ND)]
        T4 = [ar.take([P, AW], F32) for _ in range(ND)]

        self.wV(g)
        self.wA(g)
        for c in range(ND):
            ig = g.memset(wz[c][:], 0.0)
        self.sG.inc(ig)
        self.wG(v)
        for c in range(ND):
            self.sV.inc(v.tensor_scalar(wz[c][:], wz[c][:], self.w_c[:, c:c + 1], None,
                                        Alu.add))
        for c in range(ND):
            self.sV.inc(v.tensor_copy(aab[c][:, 0:1], self.sinp[:, c:c + 1]))
            self.sV.inc(v.tensor_copy(bbb[c][:, 0:1], self.sinp[:, 8 + c:9 + c]))
            self.sV.inc(v.tensor_copy(ppb[c][:, 0:1], self.sinp[:, 16 + c:17 + c]))
        self.wV(v)

        self.syncS()
        p2a_prev = [self.sA.n] * 2
        p2v_prev = [self.sV.n] * 2
        p2g_prev = [self.sG.n] * 2
        p2_ranges = _blocks(2, W - 1, AW)
        nc.sync.wait_ge(self.sV.h, self.sV.n)
        nc.sync.wait_ge(self.sT.h, self.sT.n)
        c0_, c1_ = p2_ranges[0]
        for oc in range(ND):
            self.ld(xr2[oc][:, 0:c1_ - c0_], self.xr_st[oc][:, c0_ - 2:c1_ - 2])
        xr_ready = self.sL.n
        for oc in range(ND):
            self.ld(kw2[oc][:, 0:c1_ - c0_], self.kv_st[0, oc][:, c0_ - 2:c1_ - 2])
            self.ld(vw2[oc][:, 0:c1_ - c0_], self.kv_st[1, oc][:, c0_ - 2:c1_ - 2])
        kv_ready = self.sL.n
        prev_sig_end = 0
        for j, (c0, c1) in enumerate(p2_ranges):
            bw = c1 - c0
            nrw = bw < 64

            def vI(instr):
                if nrw:
                    self.sV.inc(instr)
                return instr

            def vE(instr):
                self.sV.inc(instr)
                if nrw:
                    v.wait_ge(self.sV.h, self.sV.n)
                return instr

            te.wait_ge(self.sL2.h, wr_ready)
            te.wait_ge(self.sL.h, xr_ready)
            r_ev = self.sA.n
            for oc in range(ND):
                if oc >= 2:
                    te.wait_ge(self.sA.h, r_ev + (oc - 1))
                elif prev_sig_end:
                    te.wait_ge(self.sA.h, prev_sig_end)
                for dd in range(ND):
                    mm = te.matmul(self.ps[0 + (oc % 2)][:, 0:bw],
                                   wbufA[dd][:, oc * 128:(oc + 1) * 128],
                                   xr2[dd][:, 0:bw], start=(dd == 0), stop=(dd == ND - 1))
                self.sT.inc(mm)
                self.wT(a)
                self.sA.inc(a.activation(rw2[oc][:, 0:bw], self.ps[0 + (oc % 2)][:, 0:bw],
                                         Act.Sigmoid, bias=0.0, scale=1.0))
            sig_done = self.sA.n
            prev_sig_end = sig_done
            rmm_end = self.sT.n
            if j + 1 < len(p2_ranges):
                n0, n1 = p2_ranges[j + 1]
                nc.sync.wait_ge(self.sT.h, rmm_end)
                for oc in range(ND):
                    self.ld(xr2[oc][:, 0:n1 - n0], self.xr_st[oc][:, n0 - 2:n1 - 2])
                xr_ready = self.sL.n
            v.wait_ge(self.sL.h, kv_ready)
            g.wait_ge(self.sL.h, kv_ready)
            H2 = [range(0, 4), range(4, 8)]
            rec = {}

            def gE(instrs):
                for i_, ig in enumerate(instrs):
                    if nrw or i_ == len(instrs) - 1:
                        self.sG.inc(ig)

            def aE(instrs):
                for i_, ia in enumerate(instrs):
                    if nrw or i_ == len(instrs) - 1:
                        self.sA.inc(ia)

            for h, chans in enumerate(H2):
                for c in chans:
                    vI(v.tensor_tensor_scan(ppb[c][:, 1:1 + bw], wz[c][:, 0:bw],
                                            kw2[c][:, 0:bw], ppb[c][:, 0:1],
                                            Alu.add, Alu.max))
                vE(v.engine_nop())
            for h, chans in enumerate(H2):
                v.wait_ge(self.sA.h, p2a_prev[h])
                v.wait_ge(self.sG.h, p2g_prev[h])
                for c in chans:
                    vI(v.tensor_tensor(T1[c][:, 0:bw], ppb[c][:, 0:bw],
                                       ppb[c][:, 1:1 + bw], Alu.subtract))
                    vI(v.tensor_tensor(T2[c][:, 0:bw], kw2[c][:, 0:bw],
                                       ppb[c][:, 1:1 + bw], Alu.subtract))
                vE(v.engine_nop())
                rec['s2', h] = self.sV.n
            for h, chans in enumerate(H2):
                a.wait_ge(self.sV.h, rec['s2', h])
                ii = []
                for c in chans:
                    ii.append(a.activation(T1[c][:, 0:bw], T1[c][:, 0:bw], Act.Exp,
                                           bias=self.w_c[:, c:c + 1], scale=1.0))
                    ii.append(a.activation(T2[c][:, 0:bw], T2[c][:, 0:bw], Act.Exp,
                                           bias=0.0, scale=1.0))
                aE(ii)
                rec['s3', h] = self.sA.n
            for h, chans in enumerate(H2):
                g.wait_ge(self.sA.h, rec['s3', h])
                g.wait_ge(self.sV.h, p2v_prev[h])
                gE([g.tensor_tensor(T3[c][:, 0:bw], vw2[c][:, 0:bw], T2[c][:, 0:bw],
                                    Alu.mult) for c in chans])
                rec['s4', h] = self.sG.n
            for h, chans in enumerate(H2):
                v.wait_ge(self.sG.h, rec['s4', h])
                for c in chans:
                    vI(v.tensor_tensor_scan(aab[c][:, 1:1 + bw], T1[c][:, 0:bw],
                                            T3[c][:, 0:bw], aab[c][:, 0:1],
                                            Alu.mult, Alu.add))
                    vI(v.tensor_tensor_scan(bbb[c][:, 1:1 + bw], T1[c][:, 0:bw],
                                            T2[c][:, 0:bw], bbb[c][:, 0:1],
                                            Alu.mult, Alu.add))
                vE(v.engine_nop())
            for h, chans in enumerate(H2):
                for c in chans:
                    vI(v.tensor_scalar(T3[c][:, 0:bw], ppb[c][:, 0:bw],
                                       self.u_c[:, c:c + 1], None, Alu.add))
                vE(v.engine_nop())
                for c in chans:
                    vI(v.tensor_tensor(T4[c][:, 0:bw], T3[c][:, 0:bw], kw2[c][:, 0:bw],
                                       Alu.max))
                vE(v.engine_nop())
                for c in chans:
                    vI(v.tensor_tensor(T3[c][:, 0:bw], T3[c][:, 0:bw], T4[c][:, 0:bw],
                                       Alu.subtract))
                    vI(v.tensor_tensor(T4[c][:, 0:bw], kw2[c][:, 0:bw], T4[c][:, 0:bw],
                                       Alu.subtract))
                vE(v.engine_nop())
                rec['s9', h] = self.sV.n
            for h, chans in enumerate(H2):
                a.wait_ge(self.sV.h, rec['s9', h])
                ii = []
                for c in chans:
                    ii.append(a.activation(T3[c][:, 0:bw], T3[c][:, 0:bw], Act.Exp,
                                           bias=0.0, scale=1.0))
                    ii.append(a.activation(T4[c][:, 0:bw], T4[c][:, 0:bw], Act.Exp,
                                           bias=0.0, scale=1.0))
                aE(ii)
                rec['s10', h] = self.sA.n
            for h, chans in enumerate(H2):
                g.wait_ge(self.sA.h, rec['s10', h])
                g.wait_ge(self.sV.h, rec['s9', h])
                gE([g.tensor_tensor(T1[c][:, 0:bw], vw2[c][:, 0:bw], T4[c][:, 0:bw],
                                    Alu.mult) for c in chans])
                rec['s11', h] = self.sG.n
            for h, chans in enumerate(H2):
                v.wait_ge(self.sA.h, rec['s10', h])
                for c in chans:
                    vI(v.tensor_tensor(T2[c][:, 0:bw], aab[c][:, 0:bw], T3[c][:, 0:bw],
                                       Alu.mult))
                vE(v.engine_nop())
                rec['s12a', h] = self.sV.n
            for h, chans in enumerate(H2):
                g.wait_ge(self.sV.h, rec['s12a', h])
                gE([g.tensor_tensor(T2[c][:, 0:bw], T2[c][:, 0:bw], T1[c][:, 0:bw],
                                    Alu.add) for c in chans])
                gE([g.tensor_tensor(T1[c][:, 0:bw], bbb[c][:, 0:bw], T3[c][:, 0:bw],
                                    Alu.mult) for c in chans])
                rec['s13', h] = self.sG.n
            for h, chans in enumerate(H2):
                v.wait_ge(self.sG.h, rec['s13', h])
                for c in chans:
                    vI(v.scalar_tensor_tensor(T1[c][:, 0:bw], T4[c][:, 0:bw], 1e-6,
                                              T1[c][:, 0:bw], Alu.add, Alu.add))
                vE(v.engine_nop())
                for c in chans:
                    vI(v.reciprocal(T1[c][:, 0:bw], T1[c][:, 0:bw]))
                vE(v.engine_nop())
                rec['s15', h] = self.sV.n
            for h, chans in enumerate(H2):
                g.wait_ge(self.sV.h, rec['s15', h])
                gE([g.tensor_tensor(T2[c][:, 0:bw], T2[c][:, 0:bw], T1[c][:, 0:bw],
                                    Alu.mult) for c in chans])
                rec['s16', h] = self.sG.n
            for h, chans in enumerate(H2):
                v.wait_ge(self.sA.h, sig_done)
                v.wait_ge(self.sG.h, rec['s16', h])
                for c in chans:
                    vI(v.tensor_tensor(ry2[c][:, 0:bw], rw2[c][:, 0:bw], T2[c][:, 0:bw],
                                       Alu.mult))
                vE(v.engine_nop())
            for c in range(ND):
                self.sV.inc(v.tensor_copy(ppb[c][:, 0:1], ppb[c][:, bw:bw + 1]))
                self.sV.inc(v.tensor_copy(aab[c][:, 0:1], aab[c][:, bw:bw + 1]))
                self.sV.inc(v.tensor_copy(bbb[c][:, 0:1], bbb[c][:, bw:bw + 1]))
            self.wV(v)
            p2a_prev = [rec['s10', 0], rec['s10', 1]]
            p2v_prev = [rec['s12a', 0], rec['s12a', 1]]
            p2g_prev = [rec['s16', 0], rec['s16', 1]]
            s9_done = rec['s9', 1]
            s11_done = rec['s11', 1]
            mm_base = self.sT.n
            ost_base = self.sS2.n
            ld_base = self.sL.n
            add_base = self.sV.n
            for od in range(ND):
                pso = self.ps[4 + (od % 2)]
                self.wV(te)
                for oc in range(ND):
                    mm = te.matmul(pso[:, 0:bw], wbufB[oc][:, od * 128:(od + 1) * 128],
                                   ry2[oc][:, 0:bw], start=(oc == 0), stop=(oc == ND - 1))
                self.sT.inc(mm)
                ob = x1o[od % 2]
                if od >= 2:
                    nc.sync.wait_ge(self.sS2.h, ost_base + 16 * (od - 1))
                else:
                    prev = getattr(self, '_wo_prev_end', 0)
                    if prev:
                        nc.sync.wait_ge(self.sS2.h, prev - 16 * (1 - od))
                self.ld(ob[:, 0:bw], self.xres_a[od][:, c0:c1])
                v.wait_ge(self.sT.h, mm_base + od + 1)
                v.wait_ge(self.sL.h, ld_base + 16 * (od + 1))
                self.sV.inc(v.tensor_tensor(ob[:, 0:bw], ob[:, 0:bw], pso[:, 0:bw],
                                            Alu.add))
                nc.sync.wait_ge(self.sV.h, add_base + od + 1)
                self.st2(self.xres_b[od][:, c0:c1], ob[:, 0:bw])
            self._wo_prev_end = self.sS2.n
            if j + 1 < len(p2_ranges):
                n0, n1 = p2_ranges[j + 1]
                nc.sync.wait_ge(self.sV.h, s9_done)
                nc.sync.wait_ge(self.sG.h, s11_done)
                for oc in range(ND):
                    self.ld(kw2[oc][:, 0:n1 - n0], self.kv_st[0, oc][:, n0 - 2:n1 - 2])
                    self.ld(vw2[oc][:, 0:n1 - n0], self.kv_st[1, oc][:, n0 - 2:n1 - 2])
                kv_ready = self.sL.n

    def build(self):
        nc = self.nc
        self.p0()
        self.pw()
        self.ffn(0, W, self.xw, self.xres_a, final=False)
        self.attn()
        nc.sync.wait_ge(self.sT.h, self.sT.n)
        self.syncS()
        for i in range(ND):
            self.ld(self.w1q[i][:], self.w1q_st[i])
        nc.tensor.wait_ge(self.sL.h, self.sL.n)
        self.ffn(2, W - 1, self.xres_b, self.out, final=True)
        nc.sync.wait_ge(self.sO.h, self.sO.n)
        nc.vector.wait_ge(self.sO.h, self.sO.n)
        nc.finalize()
        return nc


_CACHE = {}


def _build(shadow):
    key = bool(shadow)
    if key not in _CACHE:
        _CACHE[key] = Builder(shadow).build()
    return _CACHE[key]


def kernel(**inputs):
    x = np.asarray(inputs["L0_attn_x"], np.float32)
    td = np.asarray(inputs["L0_attn_time_decay"], np.float32)
    u = np.asarray(inputs["L0_attn_time_first"], np.float32)
    mk = np.asarray(inputs["L0_attn_time_mix_k"], np.float32).reshape(-1)
    mv = np.asarray(inputs["L0_attn_time_mix_v"], np.float32).reshape(-1)
    mr = np.asarray(inputs["L0_attn_time_mix_r"], np.float32).reshape(-1)
    Wk = np.asarray(inputs["L0_attn_Wk"], np.float32)
    Wv = np.asarray(inputs["L0_attn_Wv"], np.float32)
    Wr = np.asarray(inputs["L0_attn_Wr"], np.float32)
    Wo = np.asarray(inputs["L0_attn_Wo"], np.float32)
    w1 = np.asarray(inputs["L0_ffn_w1"], np.float32)
    cw = np.asarray(inputs["L0_ffn_conv_w"], np.float32)
    cb = np.asarray(inputs["L0_ffn_conv_b"], np.float32)
    w2 = np.asarray(inputs["L0_ffn_w2"], np.float32)

    def fm(vec, nch):
        return np.ascontiguousarray(vec.reshape(nch, P).T)

    shadow = os.environ.get("BASS_SIM_FP64") == "1"
    nc = _build(shadow)

    xfm = np.ascontiguousarray(np.transpose(x, (0, 2, 1)))
    T = x.shape[1]
    common = {
        "w1t": np.ascontiguousarray(w1.T).reshape(ND, P, HD),
        "w2t": np.ascontiguousarray(w2.T).reshape(NH, P, C),
        "wkt": np.ascontiguousarray(Wk.T).reshape(ND, P, C),
        "wvt": np.ascontiguousarray(Wv.T).reshape(ND, P, C),
        "wrt": np.ascontiguousarray(Wr.T).reshape(ND, P, C),
        "wot": np.ascontiguousarray(Wo.T).reshape(ND, P, C),
        "td_t": fm(td, ND), "u_t": fm(u, ND),
        "mk_t": fm(mk, ND), "mv_t": fm(mv, ND), "mr_t": fm(mr, ND),
        "wl_t": fm(cw[:, 0, 0].copy(), NH), "wc_t": fm(cw[:, 0, 1].copy(), NH),
        "wr2_t": fm(cw[:, 0, 2].copy(), NH), "cb_t": fm(cb, NH),
        "iden_in": np.eye(P, dtype=np.float32),
    }
    in_maps = []
    for i in range(8):
        b, c = i // 4, i % 4
        s, e = c * TC, (c + 1) * TC
        win = np.zeros((C, W), np.float32)
        glo, ghi = s - 3, e + 2
        slo, shi = max(glo, 0), min(ghi, T)
        win[:, slo - glo:shi - glo] = xfm[b, :, slo:shi]
        m = dict(common)
        m["xw"] = np.ascontiguousarray(win.reshape(ND, P, W))
        m["startneg_in"] = np.full((P, 1), -1e30 if c == 0 else 0.0, np.float32)
        m["startmask_in"] = np.full((P, 1), 0.0 if c == 0 else 1.0, np.float32)
        m["endmask_in"] = np.full((P, 1), 0.0 if c == 3 else 1.0, np.float32)
        oh = np.zeros((P, 8), np.float32)
        oh[:, i] = 1.0
        m["onehot_in"] = oh
        in_maps.append(m)

    res = run_bass_kernel_spmd(nc, in_maps, core_ids=list(range(8)))
    out = np.zeros((2, C, T), np.float32)
    for i in range(8):
        b, c = i // 4, i % 4
        o = np.asarray(res.results[i]["out"]).reshape(C, TC)
        out[b, :, c * TC:(c + 1) * TC] = o
    return np.ascontiguousarray(np.transpose(out, (0, 2, 1)))


# version 13 (final)
# speedup vs baseline: 1.4301x; 1.0086x over previous
import os
import numpy as np
import concourse.bass as bass
import concourse.mybir as mybir
from concourse.bass_utils import run_bass_kernel_spmd

F32 = mybir.dt.float32
BF16 = mybir.dt.bfloat16
I8 = mybir.dt.int8
I16 = mybir.dt.int16
Alu = mybir.AluOpType
Act = mybir.ActivationFunctionType
AX = mybir.AxisListType.X

C = 1024
HD = 4096
ND = 8
NH = 32
W = 2053
TC = 2048
P = 128

SB_BASE = 16512
SB_TOP = 229344
DEBUG_CAPTURE = False
DBG_BLOCK = 1


def _blocks(lo, hi, step=512):
    out = []
    c = lo
    while c < hi:
        out.append((c, min(c + step, hi)))
        c += step
    return out


class Sem:
    def __init__(self, nc, name):
        self.h = nc.alloc_semaphore(name)
        self.n = 0

    def inc(self, instr, k=1):
        instr.then_inc(self.h, k)
        self.n += k
        return instr


class Arena:
    def __init__(self, nc):
        self.nc = nc
        self.cursor = SB_BASE
        self.uid = 0

    def take(self, shape, dtype, at=None):
        per_part = int(np.prod(shape[1:])) * mybir.dt.size(dtype)
        off = self.cursor if at is None else at
        off = (off + 31) & ~31
        self.uid += 1
        t = self.nc.alloc_sbuf_tensor_at(f"sb{self.uid}", list(shape), dtype, offset=off)
        if at is None:
            self.cursor = off + per_part
        assert off + per_part <= SB_TOP, (off, per_part, shape)
        return t


class Builder:
    def __init__(self, shadow):
        self.shadow = shadow
        nc = bass.Bass("TRN2", target_bir_lowering=False, detect_race_conditions=False)
        self.nc = nc
        self.ar = Arena(nc)
        d = nc.dram_tensor
        self.xw = d("xw", [ND, P, W], F32, kind="ExternalInput")
        self.w1t = d("w1t", [ND, P, HD], F32, kind="ExternalInput")
        self.w2t = d("w2t", [NH, P, C], F32, kind="ExternalInput")
        self.wkt = d("wkt", [ND, P, C], F32, kind="ExternalInput")
        self.wvt = d("wvt", [ND, P, C], F32, kind="ExternalInput")
        self.wrt = d("wrt", [ND, P, C], F32, kind="ExternalInput")
        self.wot = d("wot", [ND, P, C], F32, kind="ExternalInput")
        self.td_t = d("td_t", [P, ND], F32, kind="ExternalInput")
        self.u_t = d("u_t", [P, ND], F32, kind="ExternalInput")
        self.mk_t = d("mk_t", [P, ND], F32, kind="ExternalInput")
        self.mv_t = d("mv_t", [P, ND], F32, kind="ExternalInput")
        self.mr_t = d("mr_t", [P, ND], F32, kind="ExternalInput")
        self.wl_t = d("wl_t", [P, NH], F32, kind="ExternalInput")
        self.wc_t = d("wc_t", [P, NH], F32, kind="ExternalInput")
        self.wr2_t = d("wr2_t", [P, NH], F32, kind="ExternalInput")
        self.cb_t = d("cb_t", [P, NH], F32, kind="ExternalInput")
        self.iden_in = d("iden_in", [P, P], F32, kind="ExternalInput")
        self.startneg_in = d("startneg_in", [P, 1], F32, kind="ExternalInput")
        self.startmask_in = d("startmask_in", [P, 1], F32, kind="ExternalInput")
        self.endmask_in = d("endmask_in", [P, 1], F32, kind="ExternalInput")
        self.onehot_in = d("onehot_in", [P, 8], F32, kind="ExternalInput")
        self.out = d("out", [ND, P, TC], F32, kind="ExternalOutput")

        self.xres_a = d("xres_a", [ND, P, W], F32, kind="Internal")
        self.xres_b = d("xres_b", [ND, P, W], F32, kind="Internal")
        self.w1q_st = d("w1q_st", [ND, P, HD], BF16, kind="Internal")
        self.w2q_st = d("w2q_st", [NH, P, C], BF16, kind="Internal")
        self.xr_st = d("xr_st", [ND, P, TC + 2], BF16, kind="Internal")
        self.n2_st = d("n2_st", [NH, P, W + 3], I8, kind="Internal")
        self.kv_st = d("kv_st", [2, ND, P, TC + 8], F32, kind="Internal")
        self.cc_in = d("cc_in", [P, 24], F32, kind="Internal")
        self.cc_out = d("cc_out", [8, P, 24], F32, kind="Internal", addr_space="Local")
        if DEBUG_CAPTURE:
            self.dbg_h = d("dbg_h", [NH, P, 520], F32, kind="Internal")
            self.dbg_g = d("dbg_g", [NH, P, 512], F32, kind="Internal")
            self.dbg_xq = d("dbg_xq", [ND, P, 512], BF16, kind="Internal")

        self.sL = Sem(nc, "sL")
        self.sL2 = Sem(nc, "sL2")
        self.sS = Sem(nc, "sS")
        self.sS2 = Sem(nc, "sS2")
        self.sO = Sem(nc, "sO")
        self.sT = Sem(nc, "sT")
        self.sA = Sem(nc, "sA")
        self.sV = Sem(nc, "sV")
        self.sG = Sem(nc, "sG")
        self.sC = Sem(nc, "sC")

        ar = self.ar
        self.iden = ar.take([P, P], F32)
        self.ones_col = ar.take([P, 1], F32)
        self.ones_row = ar.take([1, P], F32)
        self.td = ar.take([P, ND], F32)
        self.u_c = ar.take([P, ND], F32)
        self.w_c = ar.take([P, ND], F32)
        self.absw = ar.take([P, ND], F32)
        self.mk = ar.take([P, ND], F32)
        self.mv = ar.take([P, ND], F32)
        self.mr = ar.take([P, ND], F32)
        self.wl = ar.take([P, NH], F32)
        self.wcv = ar.take([P, NH], F32)
        self.wr2 = ar.take([P, NH], F32)
        self.cb = ar.take([P, NH], F32)
        self.wlm = ar.take([P, NH], F32)
        self.wcm = ar.take([P, NH], F32)
        self.wrm = ar.take([P, NH], F32)
        self.startneg = ar.take([P, 1], F32)
        self.startmask = ar.take([P, 1], F32)
        self.endmask = ar.take([P, 1], F32)
        self.onehot = ar.take([P, 8], F32)
        self.mu1c = ar.take([P, 1], F32)
        self.swcol = ar.take([P, 1], F32)
        self.scal = ar.take([1, 8], F32)
        self.eps_col = ar.take([P, 1], F32)
        self.mxcol = ar.take([P, 4], F32)
        self.s2col = ar.take([P, 4], F32)
        self.sinp = ar.take([P, 24], F32)
        self.rowA = ar.take([1, 512], F32)
        self.rowB = ar.take([1, 512], F32)
        self.rowC = ar.take([1, 512], F32)
        self.rowD = ar.take([1, 512], F32)
        self.row_s2 = ar.take([1, 512], F32)
        self.row_s2i = ar.take([1, W + 64], F32)
        self.t_xs = ar.take([P, 512], F32)
        self.t_qs = ar.take([P, 512], F32)
        self.t_i16 = ar.take([P, 512], I16)
        self.t_cg = ar.take([P, 512], F32) if shadow else None
        self.PERS_END = ar.cursor
        self.ps = [nc.alloc_psum_tensor(f"ps{i}", [P, 512], F32) for i in range(8)]
        self.R_W = (self.PERS_END + 31) & ~31
        self.R_BIG = self.R_W + 65568
        self.R_MED = self.R_BIG + 65568
        assert self.R_MED + 45 * 1024 < SB_TOP, (self.R_MED, SB_TOP)

    def ld(self, out_ap, in_ap):
        return self.sL.inc(self.nc.sync.dma_start(out_ap, in_ap), 16)

    def ld2(self, out_ap, in_ap):
        return self.sL2.inc(self.nc.gpsimd.dma_start(out_ap, in_ap), 16)

    def st(self, out_ap, in_ap, final=False):
        sem = self.sO if final else self.sS
        return sem.inc(self.nc.sync.dma_start(out_ap, in_ap), 16)

    def st2(self, out_ap, in_ap):
        return self.sS2.inc(self.nc.sync.dma_start(out_ap, in_ap), 16)

    def wA(self, eng):
        eng.wait_ge(self.sA.h, self.sA.n)

    def wV(self, eng):
        eng.wait_ge(self.sV.h, self.sV.n)

    def wG(self, eng):
        eng.wait_ge(self.sG.h, self.sG.n)

    def wT(self, eng):
        eng.wait_ge(self.sT.h, self.sT.n)

    def syncS(self):
        self.nc.sync.wait_ge(self.sS.h, self.sS.n)
        if self.sS2.n:
            self.nc.sync.wait_ge(self.sS2.h, self.sS2.n)

    def wL(self, eng):
        eng.wait_ge(self.sL.h, self.sL.n)
        if self.sL2.n:
            eng.wait_ge(self.sL2.h, self.sL2.n)

    def wS(self, eng):
        eng.wait_ge(self.sS.h, self.sS.n)
        if self.sS2.n:
            eng.wait_ge(self.sS2.h, self.sS2.n)

    def vf(self, instr):
        self.sV.inc(instr)
        self.nc.vector.wait_ge(self.sV.h, self.sV.n)
        return instr

    def af(self, instr):
        self.sA.inc(instr)
        self.nc.scalar.wait_ge(self.sA.h, self.sA.n)
        return instr

    def gf(self, instr):
        self.sG.inc(instr)
        self.nc.gpsimd.wait_ge(self.sG.h, self.sG.n)
        return instr

    def p0(self):
        nc = self.nc
        v, a, g = nc.vector, nc.scalar, nc.gpsimd
        self.ld(self.iden[:], self.iden_in[:])
        self.ld(self.td[:], self.td_t[:])
        self.ld(self.u_c[:], self.u_t[:])
        self.ld(self.mk[:], self.mk_t[:])
        self.ld(self.mv[:], self.mv_t[:])
        self.ld(self.mr[:], self.mr_t[:])
        self.ld(self.wl[:], self.wl_t[:])
        self.ld(self.wcv[:], self.wc_t[:])
        self.ld(self.wr2[:], self.wr2_t[:])
        self.ld(self.cb[:], self.cb_t[:])
        self.ld(self.startneg[:], self.startneg_in[:])
        self.ld(self.startmask[:], self.startmask_in[:])
        self.ld(self.endmask[:], self.endmask_in[:])
        self.ld(self.onehot[:], self.onehot_in[:])
        g.memset(self.ones_col[:], 1.0)
        g.memset(self.ones_row[:], 1.0)
        g.memset(self.mxcol[:], 0.0)
        g.memset(self.s2col[:], 0.0)
        g.memset(self.row_s2i[:], 1.0)
        self.gf(g.memset(self.eps_col[:], 1e-5))
        self.wL(a)
        self.af(a.activation(self.absw[:], self.td[:], Act.Exp, bias=0.0, scale=1.0))
        self.wA(v)
        self.vf(v.tensor_scalar(self.w_c[:], self.absw[:], -1.0, None, Alu.mult))

    def ternarize(self, src_dram, nchunks, width, wq_dst, st_dram, mu_slot,
                  region=None, use_ld2=False):
        nc, ar = self.nc, self.ar
        v, a, te = nc.vector, nc.scalar, nc.tensor
        ar.cursor = self.R_BIG if region is None else region
        ldf = self.ld2 if use_ld2 else self.ld
        lsem = self.sL2 if use_ld2 else self.sL
        stage = [ar.take([P, width], F32) for _ in range(2)]
        junk = ar.take([P, width], F32)
        i16 = ar.take([P, width], I16)
        qstage = [ar.take([P, width], BF16) for _ in range(2)] if wq_dst is None else None
        abscol = ar.take([P, nchunks + 4], F32)
        acc_col = ar.take([P, 4], F32)

        base = lsem.n
        ldf(stage[0][:], src_dram[0])
        if nchunks > 1:
            ldf(stage[1][:], src_dram[1])
        for i in range(nchunks):
            a.wait_ge(lsem.h, base + 16 * (i + 1))
            self.sA.inc(a.activation(junk[:], stage[i % 2][:], Act.Abs, bias=0.0,
                                     scale=1.0, accum_out=abscol[:, i:i + 1]))
            if i + 2 < nchunks:
                if use_ld2:
                    nc.gpsimd.wait_ge(self.sA.h, self.sA.n)
                else:
                    nc.sync.wait_ge(self.sA.h, self.sA.n)
                ldf(stage[i % 2][:], src_dram[i + 2])
        self.wA(v)
        self.vf(v.tensor_reduce(acc_col[:, 0:1], abscol[:, 0:nchunks], AX, Alu.add))
        self.wV(te)
        self.wG(te)
        self.sT.inc(te.matmul(self.ps[0][0:1, 0:1], self.ones_col[:], acc_col[:, 0:1],
                              start=True, stop=True))
        self.wT(v)
        inv_n = 1.0 / float(nchunks * P * width)
        self.vf(v.tensor_scalar(self.scal[:, mu_slot + 1:mu_slot + 2], self.ps[0][0:1, 0:1],
                                inv_n, 1e-5, Alu.mult, Alu.max))
        self.vf(v.reciprocal(self.scal[:, mu_slot:mu_slot + 1],
                             self.scal[:, mu_slot + 1:mu_slot + 2]))
        self.vf(v.reciprocal(self.scal[:, mu_slot + 1:mu_slot + 2],
                             self.scal[:, mu_slot:mu_slot + 1]))
        self.wV(te)
        self.sT.inc(te.matmul(self.ps[1][:, 0:1], self.ones_row[:],
                              self.scal[:, mu_slot:mu_slot + 1], start=True, stop=True))
        self.wT(v)
        self.vf(v.tensor_copy(self.swcol[:], self.ps[1][:, 0:1]))

        base = lsem.n
        ldf(stage[0][:], src_dram[0])
        if nchunks > 1:
            ldf(stage[1][:], src_dram[1])
        st_base = self.sS2.n
        for i in range(nchunks):
            v.wait_ge(lsem.h, base + 16 * (i + 1))
            dst = wq_dst[i][:] if wq_dst is not None else qstage[i % 2][:]
            if wq_dst is None and i >= 2:
                v.wait_ge(self.sS2.h, st_base + 16 * (i - 1))
            v.tensor_scalar(junk[:], stage[i % 2][:], self.swcol[:], None, Alu.mult)
            if not self.shadow:
                v.tensor_scalar(i16[:], junk[:], 0.0, None, Alu.add)
                self.sV.inc(v.tensor_scalar(dst, i16[:], 1.0, -1.0, Alu.min, Alu.max))
            else:
                v.tensor_scalar(junk[:], junk[:], 256.5, None, Alu.add)
                v.tensor_scalar(i16[:], junk[:], 0.0, None, Alu.add)
                cg = stage[i % 2]
                v.tensor_tensor(cg[:], i16[:], junk[:], Alu.is_gt)
                v.scalar_tensor_tensor(junk[:], i16[:], 256.0, cg[:], Alu.subtract,
                                       Alu.subtract)
                self.sV.inc(v.tensor_scalar(dst, junk[:], 1.0, -1.0, Alu.min, Alu.max))
            nc.sync.wait_ge(self.sV.h, self.sV.n)
            self.st2(st_dram[i], dst)
            if i + 2 < nchunks:
                if use_ld2:
                    nc.gpsimd.wait_ge(self.sV.h, self.sV.n)
                ldf(stage[i % 2][:], src_dram[i + 2])

    def pw(self):
        nc = self.nc
        v, te = nc.vector, nc.tensor
        self.w1q = [self.ar.take([P, HD], BF16, at=self.R_W + i * HD * 2) for i in range(ND)]
        self.w2q = [self.ar.take([P, C], BF16, at=self.R_W + i * C * 2) for i in range(NH)]
        self.ternarize(self.w1t, ND, HD, self.w1q, self.w1q_st, mu_slot=0)
        self.ternarize(self.w2t, NH, C, None, self.w2q_st, mu_slot=2,
                       region=self.R_MED, use_ld2=True)
        self.wV(te)
        self.sT.inc(te.matmul(self.ps[2][:, 0:1], self.ones_row[:], self.scal[:, 1:2],
                              start=True, stop=True))
        self.wT(v)
        self.vf(v.tensor_copy(self.mu1c[:], self.ps[2][:, 0:1]))
        self.vf(v.tensor_scalar(self.wlm[:], self.wl[:], self.mu1c[:], None, Alu.mult))
        self.vf(v.tensor_scalar(self.wcm[:], self.wcv[:], self.mu1c[:], None, Alu.mult))
        self.vf(v.tensor_scalar(self.wrm[:], self.wr2[:], self.mu1c[:], None, Alu.mult))
        self.vf(v.tensor_scalar(self.scal[:, 4:5], self.scal[:, 3:4], 0.5, None, Alu.mult))

    def ln_stats(self, xblk, sq_strips, bw, ps_a, ps_b, act_guard=False):
        nc = self.nc
        v, a, te, g = nc.vector, nc.scalar, nc.tensor, nc.gpsimd
        w64 = max(bw, 64)
        self.wL(a)
        self.wV(a)
        self.wG(a)
        for dd in range(ND):
            ins = a.activation(sq_strips[dd][:, 0:w64], xblk[dd][:, 0:w64], Act.Square,
                               bias=0.0, scale=1.0)
        self.sA.inc(ins)
        self.wA(v)
        self.wA(g)
        self.wV(g)
        v.wait_ge(self.sT.h, getattr(self, 'last_tp', 0))
        xs, qs = self.t_xs, self.t_qs
        v.tensor_tensor(xs[:, 0:w64], xblk[0][:, 0:w64], xblk[1][:, 0:w64], Alu.add)
        g.tensor_tensor(qs[:, 0:w64], sq_strips[0][:, 0:w64], sq_strips[1][:, 0:w64],
                        Alu.add)
        for dd in range(2, ND):
            iv = v.tensor_tensor(xs[:, 0:w64], xs[:, 0:w64], xblk[dd][:, 0:w64], Alu.add)
            ig = g.tensor_tensor(qs[:, 0:w64], qs[:, 0:w64], sq_strips[dd][:, 0:w64],
                                 Alu.add)
        self.sV.inc(iv)
        self.sG.inc(ig)
        self.wV(te)
        self.wG(te)
        if act_guard:
            self.wA(te)
        te.matmul(ps_a[0:1, 0:bw], self.ones_col[:], xs[:, 0:bw], start=True, stop=True)
        self.sT.inc(te.matmul(ps_b[0:1, 0:bw], self.ones_col[:], qs[:, 0:bw],
                              start=True, stop=True))

    def ln_rows(self, bw, ps_a, ps_b):
        nc = self.nc
        v, a = nc.vector, nc.scalar
        w64 = max(bw, 64)
        self.wT(v)
        v.tensor_scalar(self.rowA[0:1, 0:w64], ps_a[0:1, 0:w64], 1.0 / C, None, Alu.mult)
        v.tensor_scalar(self.rowB[0:1, 0:w64], ps_b[0:1, 0:w64], 1.0 / C, None, Alu.mult)
        v.tensor_tensor(self.rowC[0:1, 0:w64], self.rowA[0:1, 0:w64],
                        self.rowA[0:1, 0:w64], Alu.mult)
        v.tensor_tensor(self.rowB[0:1, 0:w64], self.rowB[0:1, 0:w64],
                        self.rowC[0:1, 0:w64], Alu.subtract)
        self.sV.inc(v.tensor_scalar(self.rowB[0:1, 0:w64], self.rowB[0:1, 0:w64], 0.0,
                                    None, Alu.max))
        self.wV(a)
        self.sA.inc(a.activation(self.rowC[0:1, 0:w64], self.rowB[0:1, 0:w64], Act.Sqrt,
                                 bias=self.eps_col[0:1, 0:1], scale=1.0))
        self.wA(v)
        self.sV.inc(v.reciprocal(self.rowD[0:1, 0:w64], self.rowC[0:1, 0:w64]))

    def quant1(self, c0, c1, in_dram, xblk, sq, xqd, tmpB, preloaded=False,
               next_rng=None):
        nc = self.nc
        v, a, te, g = nc.vector, nc.scalar, nc.tensor, nc.gpsimd
        bw = c1 - c0
        w64 = max(bw, 64)
        if not preloaded:
            for dd in range(ND):
                self.ld(xblk[dd][:, 0:bw], in_dram[dd][:, c0:c1])
        self.ln_stats(xblk, sq, bw, self.ps[0], self.ps[1])
        self.ln_rows(bw, self.ps[0], self.ps[1])
        self.wV(te)
        te.wait_ge(self.sA.h, getattr(self, 's2b_done', 0))
        self.sT.inc(te.matmul(self.ps[2][:, 0:bw], self.ones_row[:], self.rowA[0:1, 0:bw],
                              start=True, stop=True))
        self.wT(v)
        self.wT(g)
        self.wG(v)
        for dd in range(ND):
            self.sV.inc(v.tensor_tensor(sq[dd][:, 0:w64], xblk[dd][:, 0:w64],
                                        self.ps[2][:, 0:w64], Alu.subtract))
        if next_rng is not None:
            nc.sync.wait_ge(self.sV.h, self.sV.n)
            nc0, nc1 = next_rng
            for dd in range(ND):
                self.ld(xblk[dd][:, 0:nc1 - nc0], in_dram[dd][:, nc0:nc1])
        self.wV(g)
        xs = self.t_xs
        tmps = [self.t_qs, tmpB]
        vmax_done = []
        for dd in range(ND):
            if dd >= 2:
                g.wait_ge(self.sV.h, vmax_done[dd - 2])
            self.sG.inc(g.tensor_tensor(tmps[dd % 2][:, 0:w64], sq[dd][:, 0:w64],
                                        sq[dd][:, 0:w64], Alu.mult))
            v.wait_ge(self.sG.h, self.sG.n)
            if dd == 0:
                iv = v.tensor_copy(xs[:, 0:w64], tmps[0][:, 0:w64])
            else:
                iv = v.tensor_tensor(xs[:, 0:w64], xs[:, 0:w64], tmps[dd % 2][:, 0:w64],
                                     Alu.max)
            self.sV.inc(iv)
            vmax_done.append(self.sV.n)
        self.wV(te)
        nsb = (bw + 127) // 128
        for j in range(nsb):
            sw_ = min(128, bw - j * 128)
            it = te.transpose(self.ps[3][0:sw_, j * 128:j * 128 + 128],
                              xs[:, j * 128:j * 128 + sw_], self.iden[:])
        self.sT.inc(it)
        self.last_tp = self.sT.n
        v.wait_ge(self.sT.h, self.last_tp)
        self.wA(v)
        for j in range(nsb):
            sw_ = min(128, bw - j * 128)
            self.vf(v.tensor_reduce(self.mxcol[0:sw_, j:j + 1],
                                    self.ps[3][0:sw_, j * 128:(j + 1) * 128], AX, Alu.max))
        self.wV(a)
        self.af(a.activation(self.mxcol[:, 0:4], self.mxcol[:, 0:4], Act.Sqrt,
                             bias=0.0, scale=1.0))
        self.wA(te)
        self.wV(te)
        for j in range(nsb):
            it = te.transpose(self.ps[3][0:1, j * 128:(j + 1) * 128],
                              self.mxcol[:, j:j + 1], self.iden[:])
        self.sT.inc(it)
        self.wT(v)
        v.tensor_copy(self.rowC[0:1, 0:w64], self.ps[3][0:1, 0:w64])
        v.tensor_tensor(self.rowC[0:1, 0:w64], self.rowD[0:1, 0:w64],
                        self.rowC[0:1, 0:w64], Alu.mult)
        v.tensor_scalar(self.rowC[0:1, 0:w64], self.rowC[0:1, 0:w64], 1e-5, None, Alu.max)
        v.reciprocal(self.rowB[0:1, 0:w64], self.rowC[0:1, 0:w64])
        self.sV.inc(v.tensor_scalar(self.rowB[0:1, 0:w64], self.rowB[0:1, 0:w64], 127.0,
                                    None, Alu.mult))
        self.sV.inc(v.reciprocal(self.rowC[0:1, 0:w64], self.rowB[0:1, 0:w64]))
        self.sV.inc(v.tensor_tensor(self.rowD[0:1, 0:w64], self.rowD[0:1, 0:w64],
                                    self.rowB[0:1, 0:w64], Alu.mult))
        self.wV(te)
        te.matmul(self.ps[2][:, 0:bw], self.ones_row[:], self.rowD[0:1, 0:bw],
                  start=True, stop=True)
        self.sT.inc(te.matmul(self.ps[3][:, 0:bw], self.ones_row[:], self.rowC[0:1, 0:bw],
                              start=True, stop=True))
        self.wT(v)
        i16 = self.t_i16
        shift = 256.5 if self.shadow else 0.0
        for dd in range(ND):
            v.tensor_tensor(sq[dd][:, 0:w64], sq[dd][:, 0:w64], self.ps[2][:, 0:w64],
                            Alu.mult)
            if not self.shadow:
                v.tensor_scalar(i16[:, 0:w64], sq[dd][:, 0:w64], 0.0, None, Alu.add)
                iv = v.scalar_tensor_tensor(xqd[dd][:, 0:w64], i16[:, 0:w64], 0.0,
                                            self.ps[3][:, 0:w64], Alu.subtract, Alu.mult)
            else:
                cg = self.t_cg
                v.tensor_scalar(sq[dd][:, 0:w64], sq[dd][:, 0:w64], shift, None, Alu.add)
                v.tensor_scalar(i16[:, 0:w64], sq[dd][:, 0:w64], 0.0, None, Alu.add)
                v.tensor_tensor(cg[:, 0:w64], i16[:, 0:w64], sq[dd][:, 0:w64], Alu.is_gt)
                v.scalar_tensor_tensor(cg[:, 0:w64], i16[:, 0:w64], 256.0, cg[:, 0:w64],
                                       Alu.subtract, Alu.subtract)
                iv = v.tensor_tensor(xqd[dd][:, 0:w64], cg[:, 0:w64], self.ps[3][:, 0:w64],
                                     Alu.mult)
            self.sV.inc(iv)
        self.wV(v)

    def ffn_p1(self, lo, hi, in_dram, final):
        nc, ar = self.nc, self.ar
        v, a, te, g = nc.vector, nc.scalar, nc.tensor, nc.gpsimd
        ar.cursor = self.R_BIG
        gstr = [ar.take([P, 512], F32) for _ in range(NH)]
        ar.cursor = self.R_MED
        xblk = [ar.take([P, 512], F32) for _ in range(ND)]
        sq = [ar.take([P, 512], F32) for _ in range(ND)]
        xqd = [ar.take([P, 512], BF16) for _ in range(ND)]
        Hs = [ar.take([P, 520], F32) for _ in range(2)]
        tails = ar.take([P, 2 * NH], F32)
        n2o = [ar.take([P, 512], I8) for _ in range(2)]
        rmA = ar.take([P, 512], F32)
        rmB = ar.take([P, 512], F32)
        s2b_s = ar.take([P, 512], F32)
        u_s = self.t_qs

        self.wA(g)
        self.wV(g)
        g.memset(Hs[0][:], 0.0)
        self.gf(g.memset(Hs[1][:], 0.0))

        blocks = _blocks(lo, hi)
        for bi, (c0, c1) in enumerate(blocks):
            bw = c1 - c0
            first, last = (c0 == lo), (c1 == hi)
            g0 = c0 + 1 if first else c0 - 1
            g1 = hi - 1 if last else c1 - 1
            gw = g1 - g0
            gw64 = max(gw, 64)
            nxt = blocks[bi + 1] if bi + 1 < len(blocks) else None
            self.quant1(c0, c1, in_dram, xblk, sq, xqd, rmA,
                        preloaded=(bi > 0), next_rng=nxt)
            if bi == 0:
                self.wG(a)
            mm_base = self.sT.n
            ev_base = self.sA.n
            self.rm_done = self.sV.n
            conv_done = []
            t1_done = []
            silu_done = []
            o_l = g0 - c0 + 1
            for ch in range(NH):
                pch = self.ps[4 + (ch % 4)]
                if ch >= 4:
                    te.wait_ge(self.sA.h, ev_base + 3 * (ch - 4) + 1)
                elif ch == 0:
                    self.wV(te)
                for dd in range(ND):
                    mm = te.matmul(pch[:, 0:bw], self.w1q[dd][:, ch * 128:(ch + 1) * 128],
                                   xqd[dd][:, 0:bw], start=(dd == 0), stop=(dd == ND - 1))
                self.sT.inc(mm)
                hb = Hs[ch % 2]
                a.wait_ge(self.sT.h, mm_base + ch + 1)
                if ch >= 2:
                    a.wait_ge(self.sV.h, conv_done[ch - 2])
                self.sA.inc(a.copy(hb[:, 2:2 + bw], pch[:, 0:bw]))
                v.wait_ge(self.sA.h, ev_base + 3 * ch + 1)
                if not first:
                    self.sV.inc(v.tensor_copy(hb[:, 0:2], tails[:, 2 * ch:2 * ch + 2]))
                if final and last:
                    lc = (hi - 1 - c0) + 2
                    self.sV.inc(v.tensor_scalar(hb[:, lc:lc + 1], hb[:, lc:lc + 1],
                                                self.endmask[:], None, Alu.mult))
                if final and first:
                    self.sV.inc(v.tensor_scalar(hb[:, 2:3], hb[:, 2:3],
                                                self.startmask[:], None, Alu.mult))
                self.wV(v)
                hb_ready = self.sV.n
                wlc = self.wlm[:, ch:ch + 1]
                wcc = self.wcm[:, ch:ch + 1]
                wrc = self.wrm[:, ch:ch + 1]
                t1 = sq[ch % 2]
                ub = sq[2 + ch % 2]
                a.wait_ge(self.sV.h, hb_ready)
                self.sA.inc(a.activation(t1[:, 0:gw64], hb[:, o_l:o_l + gw64], Act.Copy,
                                         bias=0.0, scale=wlc))
                if ch >= 2:
                    v.wait_ge(self.sA.h, silu_done[ch - 2])
                v.wait_ge(self.sA.h, ev_base + 3 * ch + 2)
                v.scalar_tensor_tensor(ub[:, 0:gw64], hb[:, o_l + 1:o_l + 1 + gw64], wcc,
                                       t1[:, 0:gw64], Alu.mult, Alu.add)
                self.sV.inc(v.scalar_tensor_tensor(ub[:, 0:gw64],
                                                   hb[:, o_l + 2:o_l + 2 + gw64], wrc,
                                                   ub[:, 0:gw64], Alu.mult, Alu.add))
                self.sV.inc(v.tensor_copy(tails[:, 2 * ch:2 * ch + 2], hb[:, bw:bw + 2]))
                conv_done.append(self.sV.n)
                a.wait_ge(self.sV.h, conv_done[-1])
                self.sA.inc(a.activation(gstr[ch][:, 0:gw64], ub[:, 0:gw64], Act.Silu,
                                         bias=self.cb[:, ch:ch + 1], scale=1.0))
                silu_done.append(self.sA.n)
                g.wait_ge(self.sA.h, silu_done[-1])
                if ch >= 1:
                    g.wait_ge(self.sV.h, self.rm_done)
                src0 = gstr[ch][:, 0:gw64]
                self.sG.inc(g.tensor_tensor(rmB[:, 0:gw64], src0, src0, Alu.mult))
                v.wait_ge(self.sG.h, self.sG.n)
                if ch == 0:
                    iv = v.tensor_copy(rmA[:, 0:gw64], rmB[:, 0:gw64])
                else:
                    iv = v.tensor_tensor(rmA[:, 0:gw64], rmA[:, 0:gw64], rmB[:, 0:gw64],
                                         Alu.max)
                self.sV.inc(iv)
                self.rm_done = self.sV.n
            self.wG(te)
            self.wV(te)
            nsb = (gw + 127) // 128
            for j in range(nsb):
                sw_ = min(128, gw - j * 128)
                it = te.transpose(self.ps[0][0:sw_, j * 128:j * 128 + 128],
                                  rmA[:, j * 128:j * 128 + sw_], self.iden[:])
            self.sT.inc(it)
            self.wT(v)
            for j in range(nsb):
                sw_ = min(128, gw - j * 128)
                self.vf(v.tensor_reduce(self.mxcol[0:sw_, j:j + 1],
                                        self.ps[0][0:sw_, j * 128:(j + 1) * 128], AX,
                                        Alu.max))
            self.wV(a)
            self.af(a.activation(self.mxcol[:, 0:4], self.mxcol[:, 0:4], Act.Sqrt,
                                 bias=0.0, scale=1.0))
            self.wA(v)
            self.vf(v.tensor_scalar(self.mxcol[:, 0:4], self.mxcol[:, 0:4], 1e-5, None,
                                    Alu.max))
            self.vf(v.reciprocal(self.s2col[:, 0:4], self.mxcol[:, 0:4]))
            self.vf(v.tensor_scalar(self.s2col[:, 0:4], self.s2col[:, 0:4], 127.0,
                                    None, Alu.mult))
            self.wV(te)
            for j in range(nsb):
                it = te.transpose(self.ps[1][0:1, j * 128:(j + 1) * 128],
                                  self.s2col[:, j:j + 1], self.iden[:])
            self.sT.inc(it)
            self.wT(v)
            self.sV.inc(v.tensor_copy(self.row_s2[0:1, 0:gw64], self.ps[1][0:1, 0:gw64]))
            v.tensor_scalar(self.row_s2[0:1, 0:gw64], self.row_s2[0:1, 0:gw64], 1e-5,
                            None, Alu.max)
            self.sV.inc(v.reciprocal(self.row_s2i[0:1, g0:g0 + gw64],
                                     self.row_s2[0:1, 0:gw64]))
            self.wV(te)
            self.sT.inc(te.matmul(self.ps[2][:, 0:gw], self.ones_row[:],
                                  self.row_s2[0:1, 0:gw], start=True, stop=True))
            self.wT(a)
            self.sA.inc(a.copy(s2b_s[:, 0:gw], self.ps[2][:, 0:gw]))
            self.s2b_done = self.sA.n
            st_base = self.sS2.n
            for ch in range(NH):
                eng, sem = (g, self.sG) if (ch % 2) else (v, self.sV)
                eng.wait_ge(self.sA.h, self.s2b_done)
                sem.inc(eng.tensor_tensor(gstr[ch][:, 0:gw64], gstr[ch][:, 0:gw64],
                                          s2b_s[:, 0:gw64], Alu.mult))
                nb = n2o[ch % 2]
                v.wait_ge(sem.h, sem.n)
                if ch >= 2:
                    v.wait_ge(self.sS2.h, st_base + 16 * (ch - 1))
                if not self.shadow:
                    iv = v.tensor_scalar(nb[:, 0:gw64], gstr[ch][:, 0:gw64], 0.0, None,
                                         Alu.add)
                else:
                    v.tensor_scalar(self.t_cg[:, 0:gw64], gstr[ch][:, 0:gw64], 256.5,
                                    None, Alu.add)
                    v.tensor_scalar(self.t_i16[:, 0:gw64], self.t_cg[:, 0:gw64], 0.0,
                                    None, Alu.add)
                    v.tensor_tensor(u_s[:, 0:gw64], self.t_i16[:, 0:gw64],
                                    self.t_cg[:, 0:gw64], Alu.is_gt)
                    v.scalar_tensor_tensor(u_s[:, 0:gw64], self.t_i16[:, 0:gw64], 256.0,
                                           u_s[:, 0:gw64], Alu.subtract, Alu.subtract)
                    iv = v.tensor_scalar(nb[:, 0:gw64], u_s[:, 0:gw64], 0.0, None, Alu.add)
                self.sV.inc(iv)
                nc.sync.wait_ge(self.sV.h, self.sV.n)
                self.st2(self.n2_st[ch][:, g0:g1], nb[:, 0:gw])

    def ffn_p2(self, lo, hi, in_dram, out_dram, final):
        nc, ar = self.nc, self.ar
        v, a, te = nc.vector, nc.scalar, nc.tensor
        ar.cursor = self.R_BIG
        n2b = [ar.take([P, 512], BF16) for _ in range(NH)]
        n2i = [[ar.take([P, 512], I8) for _ in range(NH)] for _ in range(2)]
        ar.cursor = self.R_MED
        xblk = [[ar.take([P, 512], F32) for _ in range(ND)] for _ in range(2)]
        ostr = [ar.take([P, 512], F32) for _ in range(2)]
        c2s = ar.take([P, 512], F32)

        blocks2 = _blocks(lo + 1, hi - 1)
        nc.sync.wait_ge(self.sA.h, self.sA.n)
        nc.sync.wait_ge(self.sV.h, self.sV.n)
        g0_, g1_ = blocks2[0]
        for ch in range(NH):
            self.ld(n2i[0][ch][:, 0:g1_ - g0_], self.n2_st[ch][:, g0_:g1_])
        n2_ready = self.sL.n
        for dd in range(ND):
            self.ld(xblk[0][dd][:, 0:g1_ - g0_], in_dram[dd][:, g0_:g1_])
        xd_ready = self.sL.n
        cast_end = 0
        add_end = 0
        for bi, (g0, g1) in enumerate(blocks2):
            gw = g1 - g0
            gw64 = max(gw, 64)
            pb = bi % 2
            a.wait_ge(self.sL.h, n2_ready)
            self.wT(a)
            for ch in range(NH):
                ia = a.copy(n2b[ch][:, 0:gw], n2i[pb][ch][:, 0:gw])
            self.sA.inc(ia)
            this_cast = self.sA.n
            this_xd = xd_ready
            this_x = xblk[pb]
            if bi + 1 < len(blocks2):
                n0, n1 = blocks2[bi + 1]
                if cast_end:
                    nc.sync.wait_ge(self.sA.h, cast_end)
                for ch in range(NH):
                    self.ld(n2i[1 - pb][ch][:, 0:n1 - n0], self.n2_st[ch][:, n0:n1])
                n2_ready = self.sL.n
                if add_end:
                    nc.sync.wait_ge(self.sV.h, add_end)
                for dd in range(ND):
                    self.ld(xblk[1 - pb][dd][:, 0:n1 - n0], in_dram[dd][:, n0:n1])
                xd_ready = self.sL.n
            cast_end = this_cast
            self.wA(v)
            self.sV.inc(v.tensor_scalar(self.rowA[0:1, 0:gw64],
                                        self.row_s2i[0:1, g0:g0 + gw64],
                                        self.scal[:, 4:5], None, Alu.mult))
            self.wV(te)
            self.sT.inc(te.matmul(self.ps[2][:, 0:gw], self.ones_row[:],
                                  self.rowA[0:1, 0:gw], start=True, stop=True))
            self.wT(a)
            self.sA.inc(a.copy(c2s[:, 0:gw], self.ps[2][:, 0:gw]))
            c2s_done = self.sA.n
            mm_base = self.sT.n
            ev_base = self.sV.n
            ost_base = self.sO.n if final else self.sS2.n
            for od in range(ND):
                pso = self.ps[6 + (od % 2)]
                if od >= 2:
                    te.wait_ge(self.sV.h, ev_base + (od - 1))
                else:
                    te.wait_ge(self.sA.h, this_cast)
                    self.wV(te)
                for ch in range(NH):
                    mm = te.matmul(pso[:, 0:gw], self.w2q[ch][:, od * 128:(od + 1) * 128],
                                   n2b[ch][:, 0:gw], start=(ch == 0), stop=(ch == NH - 1))
                self.sT.inc(mm)
                ob = ostr[od % 2]
                v.wait_ge(self.sT.h, mm_base + od + 1)
                v.wait_ge(self.sL.h, this_xd)
                v.wait_ge(self.sA.h, c2s_done)
                if od >= 2:
                    sem = self.sO if final else self.sS2
                    v.wait_ge(sem.h, ost_base + 16 * (od - 1))
                v.tensor_tensor(ob[:, 0:gw64], pso[:, 0:gw64], c2s[:, 0:gw64], Alu.mult)
                self.sV.inc(v.tensor_tensor(ob[:, 0:gw64], ob[:, 0:gw64],
                                            this_x[od][:, 0:gw64], Alu.add))
                nc.sync.wait_ge(self.sV.h, self.sV.n)
                if final:
                    self.st(out_dram[od][:, g0 - 3:g1 - 3], ob[:, 0:gw], final=True)
                else:
                    self.st2(out_dram[od][:, g0:g1], ob[:, 0:gw])
            add_end = self.sV.n

    def ffn(self, lo, hi, in_dram, out_dram, final):
        nc = self.nc
        self.ffn_p1(lo, hi, in_dram, final)
        nc.sync.wait_ge(self.sT.h, self.sT.n)
        self.syncS()
        for i in range(NH):
            self.ld(self.w2q[i][:], self.w2q_st[i])
        w2q_ready = self.sL.n
        nc.tensor.wait_ge(self.sL.h, w2q_ready)
        self.ffn_p2(lo, hi, in_dram, out_dram, final)

    def ln_only(self, c0, c1, in_dram, xblk, xn_out, preloaded=False, next_rng=None):
        nc = self.nc
        v, te = nc.vector, nc.tensor
        bw = c1 - c0
        if not preloaded:
            nc.sync.wait_ge(self.sV.h, self.sV.n)
            for dd in range(ND):
                self.ld(xblk[dd][:, 0:bw], in_dram[dd][:, c0:c1])
        self.ln_stats(xblk, xn_out, bw, self.ps[0], self.ps[1], act_guard=True)
        self.ln_rows(bw, self.ps[0], self.ps[1])
        self.wV(te)
        te.matmul(self.ps[2][:, 0:bw], self.ones_row[:], self.rowA[0:1, 0:bw],
                  start=True, stop=True)
        self.sT.inc(te.matmul(self.ps[3][:, 0:bw], self.ones_row[:], self.rowD[0:1, 0:bw],
                              start=True, stop=True))
        self.wT(v)
        w64 = max(bw, 64)
        for dd in range(ND):
            v.tensor_tensor(xn_out[dd][:, 0:w64], xblk[dd][:, 0:w64], self.ps[2][:, 0:w64],
                            Alu.subtract)
            iv = v.tensor_tensor(xn_out[dd][:, 0:w64], xn_out[dd][:, 0:w64],
                                 self.ps[3][:, 0:w64], Alu.mult)
        self.sV.inc(iv)
        if next_rng is not None:
            nc.sync.wait_ge(self.sV.h, self.sV.n)
            n0, n1 = next_rng
            for dd in range(ND):
                self.ld(xblk[dd][:, 0:n1 - n0], in_dram[dd][:, n0:n1])

    def attn(self):
        nc, ar = self.nc, self.ar
        v, a, te, g = nc.vector, nc.scalar, nc.tensor, nc.gpsimd
        AW = 256
        wbufA = [self.ar.take([P, C], BF16, at=self.R_W + i * C * 2) for i in range(ND)]
        wbufB = [self.ar.take([P, C], BF16, at=self.R_W + (ND + i) * C * 2)
                 for i in range(ND)]
        ar.cursor = self.R_BIG
        PASS2_OFF = ar.cursor
        xnw = [[ar.take([P, AW], F32) for _ in range(ND)] for _ in range(2)]
        ramp = ar.take([P, TC], F32)
        ar.cursor = self.R_MED
        kwin = [ar.take([P, AW], F32) for _ in range(ND)]
        vwin = [ar.take([P, AW], F32) for _ in range(ND)]
        xb1 = [ar.take([P, AW], F32) for _ in range(ND)]
        xk_b = [ar.take([P, AW], BF16) for _ in range(ND)]
        xv_b = [ar.take([P, AW], BF16) for _ in range(ND)]
        xr_b = [ar.take([P, AW], BF16) for _ in range(2)]
        t_a8 = [ar.take([P, AW], F32) for _ in range(ND)]
        sumbuf = ar.take([P, 24], F32)
        kapw = ar.take([P, ND], F32)
        kapn8 = ar.take([P, ND], F32)
        sa_w = ar.take([P, ND], F32)
        sb_w = ar.take([P, ND], F32)
        tmp8 = ar.take([P, ND], F32)
        tmp8b = ar.take([P, ND], F32)
        tmp8c = ar.take([P, ND], F32)
        aggb = ar.take([P, 8 * 24], F32)
        ccs = ar.take([P, 24], F32)

        self.wV(g)
        self.wA(g)
        self.sG.inc(g.iota(ramp[:], [[1, TC]], channel_multiplier=0,
                           allow_small_or_imprecise_dtypes=True))

        self.syncS()
        self.wV(g)
        if self.sS2.n:
            g.wait_ge(self.sS2.h, self.sS2.n)
        for i in range(ND):
            self.ld2(wbufA[i][:], self.wkt[i])
            self.ld2(wbufB[i][:], self.wvt[i])
        wk_ready = self.sL2.n

        ln_blocks = _blocks(1, W - 1, AW)
        mr_blocks = _blocks(2, W - 1, AW)
        nb = len(ln_blocks)
        xr_sts = []

        def ln2(j):
            nxt = ln_blocks[j + 1] if j + 1 < len(ln_blocks) else None
            self.ln_only(ln_blocks[j][0], ln_blocks[j][1], self.xres_a, xb1, xnw[j % 2],
                         preloaded=(j > 0), next_rng=nxt)
            if j == 0:
                self.wV(v)
                for dd in range(ND):
                    self.sV.inc(v.tensor_scalar(xnw[0][dd][:, 1:2], xnw[0][dd][:, 1:2],
                                                self.startmask[:], None, Alu.mult))
                self.wV(v)

        ln2(0)
        for j in range(len(mr_blocks)):
            if j + 1 < nb:
                ln2(j + 1)
            c0, c1 = mr_blocks[j]
            bw = c1 - c0
            bw64 = max(bw, 64)
            xj = xnw[j % 2]
            xj1 = xnw[(j + 1) % 2]
            l0 = c0 - ln_blocks[j][0]
            wid_in = ln_blocks[j][1] - c0
            for dd in range(ND):
                if wid_in >= bw64:
                    v.tensor_tensor(t_a8[dd][:, 0:bw64], xj[dd][:, l0:l0 + bw64],
                                    xj[dd][:, l0 - 1:l0 - 1 + bw64], Alu.subtract)
                elif wid_in >= bw:
                    wd = min(bw64, AW - l0)
                    v.tensor_tensor(t_a8[dd][:, 0:wd], xj[dd][:, l0:l0 + wd],
                                    xj[dd][:, l0 - 1:l0 - 1 + wd], Alu.subtract)
                else:
                    v.tensor_tensor(t_a8[dd][:, 0:wid_in], xj[dd][:, l0:l0 + wid_in],
                                    xj[dd][:, l0 - 1:l0 - 1 + wid_in], Alu.subtract)
                    self.sV.inc(v.tensor_tensor(t_a8[dd][:, wid_in:bw],
                                                xj1[dd][:, 0:bw - wid_in],
                                                xj[dd][:, l0 - 1 + wid_in:l0 - 1 + bw],
                                                Alu.subtract))
                    self.wV(v)
                sh0 = l0 - 1
                wd = min(bw64, AW - sh0)
                v.scalar_tensor_tensor(xk_b[dd][:, 0:wd], t_a8[dd][:, 0:wd],
                                       self.mk[:, dd:dd + 1], xj[dd][:, sh0:sh0 + wd],
                                       Alu.mult, Alu.add)
                v.scalar_tensor_tensor(xv_b[dd][:, 0:wd], t_a8[dd][:, 0:wd],
                                       self.mv[:, dd:dd + 1], xj[dd][:, sh0:sh0 + wd],
                                       Alu.mult, Alu.add)
                if len(xr_sts) >= 2:
                    v.wait_ge(self.sS2.h, xr_sts[-2])
                self.sV.inc(v.scalar_tensor_tensor(xr_b[dd % 2][:, 0:wd], t_a8[dd][:, 0:wd],
                                                   self.mr[:, dd:dd + 1],
                                                   xj[dd][:, sh0:sh0 + wd],
                                                   Alu.mult, Alu.add))
                nc.sync.wait_ge(self.sV.h, self.sV.n)
                self.st2(self.xr_st[dd][:, c0 - 2:c0 - 2 + bw], xr_b[dd % 2][:, 0:bw])
                xr_sts.append(self.sS2.n)
            self.wV(te)
            te.wait_ge(self.sL2.h, wk_ready)
            a.wait_ge(self.sS2.h, getattr(self, '_kv_stash_done', 0))
            kv_ev = self.sA.n
            for oc in range(ND):
                if oc >= 2:
                    te.wait_ge(self.sA.h, kv_ev + (oc - 1))
                for dd in range(ND):
                    mm = te.matmul(self.ps[0 + (oc % 2)][:, 0:bw],
                                   wbufA[dd][:, oc * 128:(oc + 1) * 128],
                                   xk_b[dd][:, 0:bw], start=(dd == 0), stop=(dd == ND - 1))
                self.sT.inc(mm)
                self.wT(a)
                self.sA.inc(a.copy(kwin[oc][:, 0:bw], self.ps[0 + (oc % 2)][:, 0:bw]))
            kv_ev = self.sA.n
            for oc in range(ND):
                if oc >= 2:
                    te.wait_ge(self.sA.h, kv_ev + (oc - 1))
                for dd in range(ND):
                    mm = te.matmul(self.ps[2 + (oc % 2)][:, 0:bw],
                                   wbufB[dd][:, oc * 128:(oc + 1) * 128],
                                   xv_b[dd][:, 0:bw], start=(dd == 0), stop=(dd == ND - 1))
                self.sT.inc(mm)
                self.wT(a)
                self.sA.inc(a.copy(vwin[oc][:, 0:bw], self.ps[2 + (oc % 2)][:, 0:bw]))
            self.wA(v)
            if j == 0:
                for oc in range(ND):
                    self.sV.inc(v.tensor_tensor(kwin[oc][:, 0:1], kwin[oc][:, 0:1],
                                                self.startneg[:], Alu.add))
                self.wV(v)
            nc.sync.wait_ge(self.sV.h, self.sV.n)
            nc.sync.wait_ge(self.sA.h, self.sA.n)
            for oc in range(ND):
                self.st2(self.kv_st[0, oc][:, c0 - 2:c0 - 2 + bw], kwin[oc][:, 0:bw])
                self.st2(self.kv_st[1, oc][:, c0 - 2:c0 - 2 + bw], vwin[oc][:, 0:bw])
            self._kv_stash_done = self.sS2.n
            if c0 < TC + 2:
                sw = min(c1, TC + 2) - c0
                l_off = c0 - 2
                self.wG(v)
                for oc in range(ND):
                    v.scalar_tensor_tensor(t_a8[oc][:, 0:sw], ramp[:, l_off:l_off + sw],
                                           self.absw[:, oc:oc + 1], kwin[oc][:, 0:sw],
                                           Alu.mult, Alu.add)
                for oc in range(ND):
                    self.sV.inc(v.tensor_reduce(kapw[:, oc:oc + 1], t_a8[oc][:, 0:sw],
                                                AX, Alu.max))
                self.wV(v)
                self.vf(v.tensor_scalar(kapn8[:, 0:8], kapw[:, 0:8], -1.0, None,
                                        Alu.mult))
                self.wV(a)
                sexp = []
                for oc in range(ND):
                    sexp.append(a.activation(t_a8[oc][:, 0:sw], t_a8[oc][:, 0:sw],
                                             Act.Exp, bias=kapn8[:, oc:oc + 1], scale=1.0,
                                             accum_out=sb_w[:, oc:oc + 1]))
                for ia in sexp:
                    self.sA.inc(ia)
                self.wA(v)
                for oc in range(ND):
                    self.sV.inc(v.scalar_tensor_tensor(t_a8[oc][:, 0:sw],
                                                       t_a8[oc][:, 0:sw], 1.0,
                                                       vwin[oc][:, 0:sw], Alu.mult,
                                                       Alu.mult,
                                                       accum_out=sa_w[:, oc:oc + 1]))
                self.wV(v)
                if j == 0:
                    self.vf(v.tensor_copy(sumbuf[:, 0:8], sa_w[:]))
                    self.vf(v.tensor_copy(sumbuf[:, 8:16], sb_w[:]))
                    self.vf(v.tensor_copy(sumbuf[:, 16:24], kapw[:]))
                else:
                    self.vf(v.tensor_tensor(tmp8b[:], sumbuf[:, 16:24], kapw[:], Alu.max))
                    self.vf(v.tensor_tensor(tmp8c[:], sumbuf[:, 16:24], tmp8b[:],
                                            Alu.subtract))
                    self.vf(v.tensor_tensor(tmp8[:], kapw[:], tmp8b[:], Alu.subtract))
                    self.wV(a)
                    self.sA.inc(a.activation(tmp8c[:], tmp8c[:], Act.Exp, bias=0.0,
                                             scale=1.0))
                    self.af(a.activation(tmp8[:], tmp8[:], Act.Exp, bias=0.0, scale=1.0))
                    self.wA(v)
                    self.vf(v.tensor_tensor(sumbuf[:, 0:8], sumbuf[:, 0:8], tmp8c[:],
                                            Alu.mult))
                    self.vf(v.tensor_tensor(sumbuf[:, 8:16], sumbuf[:, 8:16], tmp8c[:],
                                            Alu.mult))
                    self.vf(v.scalar_tensor_tensor(tmp8c[:], sa_w[:], 1.0, tmp8[:],
                                                   Alu.mult, Alu.mult))
                    self.vf(v.tensor_tensor(sumbuf[:, 0:8], sumbuf[:, 0:8], tmp8c[:],
                                            Alu.add))
                    self.vf(v.scalar_tensor_tensor(tmp8c[:], sb_w[:], 1.0, tmp8[:],
                                                   Alu.mult, Alu.mult))
                    self.vf(v.tensor_tensor(sumbuf[:, 8:16], sumbuf[:, 8:16], tmp8c[:],
                                            Alu.add))
                    self.vf(v.tensor_copy(sumbuf[:, 16:24], tmp8b[:]))
        self.vf(v.scalar_tensor_tensor(sumbuf[:, 16:24], self.w_c[:], 2047.0,
                                       sumbuf[:, 16:24], Alu.mult, Alu.add))
        nc.sync.wait_ge(self.sV.h, self.sV.n)
        self.st(self.cc_in[:], sumbuf[:, 0:24])
        g.wait_ge(self.sS.h, self.sS.n)
        self.sC.inc(g.collective_compute(
            "AllGather", Alu.bypass, replica_groups=[[0, 1, 2, 3, 4, 5, 6, 7]],
            ins=[self.cc_in[:]], outs=[self.cc_out[:]]))
        nc.sync.wait_ge(self.sC.h, self.sC.n)
        for j in range(8):
            self.ld(aggb[:, j * 24:(j + 1) * 24], self.cc_out[j])
        self.wL(v)
        self.wL(g)

        def slot(jj, q):
            return aggb[:, jj * 24 + q * 8:jj * 24 + (q + 1) * 8]

        first_cand = True
        for b in range(2):
            self.wV(g)
            g.memset(ccs[:, 0:16], 0.0)
            self.gf(g.memset(ccs[:, 16:24], -1e30))
            self.wG(v)
            for cpos in range(4):
                jj = b * 4 + cpos
                if first_cand:
                    for q in range(3):
                        self.vf(v.tensor_scalar(self.sinp[:, q * 8:(q + 1) * 8],
                                                ccs[:, q * 8:(q + 1) * 8],
                                                self.onehot[:, jj:jj + 1], None, Alu.mult))
                    first_cand = False
                else:
                    for q in range(3):
                        self.vf(v.scalar_tensor_tensor(self.sinp[:, q * 8:(q + 1) * 8],
                                                       ccs[:, q * 8:(q + 1) * 8],
                                                       self.onehot[:, jj:jj + 1],
                                                       self.sinp[:, q * 8:(q + 1) * 8],
                                                       Alu.mult, Alu.add))
                if cpos < 3:
                    self.vf(v.scalar_tensor_tensor(tmp8[:], self.w_c[:], 2048.0,
                                                   ccs[:, 16:24], Alu.mult, Alu.add))
                    self.vf(v.tensor_tensor(tmp8b[:], tmp8[:], slot(jj, 2), Alu.max))
                    self.vf(v.tensor_tensor(tmp8c[:], tmp8[:], tmp8b[:], Alu.subtract))
                    self.vf(v.tensor_tensor(tmp8[:], slot(jj, 2), tmp8b[:], Alu.subtract))
                    self.wV(a)
                    self.af(a.activation(tmp8c[:], tmp8c[:], Act.Exp, bias=0.0, scale=1.0))
                    self.af(a.activation(tmp8[:], tmp8[:], Act.Exp, bias=0.0, scale=1.0))
                    self.wA(v)
                    self.vf(v.tensor_tensor(ccs[:, 0:8], ccs[:, 0:8], tmp8c[:], Alu.mult))
                    self.vf(v.tensor_tensor(ccs[:, 8:16], ccs[:, 8:16], tmp8c[:],
                                            Alu.mult))
                    self.vf(v.scalar_tensor_tensor(tmp8c[:], slot(jj, 0), 1.0, tmp8[:],
                                                   Alu.mult, Alu.mult))
                    self.vf(v.tensor_tensor(ccs[:, 0:8], ccs[:, 0:8], tmp8c[:], Alu.add))
                    self.vf(v.scalar_tensor_tensor(tmp8c[:], slot(jj, 1), 1.0, tmp8[:],
                                                   Alu.mult, Alu.mult))
                    self.vf(v.tensor_tensor(ccs[:, 8:16], ccs[:, 8:16], tmp8c[:], Alu.add))
                    self.vf(v.tensor_copy(ccs[:, 16:24], tmp8b[:]))

        self.wT(g)
        for i in range(ND):
            self.ld2(wbufA[i][:], self.wrt[i])
            self.ld2(wbufB[i][:], self.wot[i])
        wr_ready = self.sL2.n

        ar.cursor = PASS2_OFF
        ppb = [ar.take([P, AW + 8], F32) for _ in range(ND)]
        aab = [ar.take([P, AW + 8], F32) for _ in range(ND)]
        bbb = [ar.take([P, AW + 8], F32) for _ in range(ND)]
        wz = [ar.take([P, AW], F32) for _ in range(ND)]
        T1 = [ar.take([P, AW], F32) for _ in range(ND)]
        T2 = [ar.take([P, AW], F32) for _ in range(ND)]
        ar.cursor = self.R_MED
        kw2 = [ar.take([P, AW], F32) for _ in range(ND)]
        vw2 = [ar.take([P, AW], F32) for _ in range(ND)]
        rw2 = [ar.take([P, AW], F32) for _ in range(ND)]
        ry2 = [ar.take([P, AW], BF16) for _ in range(ND)]
        xr2 = [ar.take([P, AW], BF16) for _ in range(ND)]
        x1o = [ar.take([P, AW], F32) for _ in range(2)]
        T3 = [ar.take([P, AW], F32) for _ in range(ND)]
        T4 = [ar.take([P, AW], F32) for _ in range(ND)]

        self.wV(g)
        self.wA(g)
        for c in range(ND):
            ig = g.memset(wz[c][:], 0.0)
        self.sG.inc(ig)
        self.wG(v)
        for c in range(ND):
            self.sV.inc(v.tensor_scalar(wz[c][:], wz[c][:], self.w_c[:, c:c + 1], None,
                                        Alu.add))
        for c in range(ND):
            self.sV.inc(v.tensor_copy(aab[c][:, 0:1], self.sinp[:, c:c + 1]))
            self.sV.inc(v.tensor_copy(bbb[c][:, 0:1], self.sinp[:, 8 + c:9 + c]))
            self.sV.inc(v.tensor_copy(ppb[c][:, 0:1], self.sinp[:, 16 + c:17 + c]))
        self.wV(v)

        self.syncS()
        p2a_prev = [self.sA.n] * 2
        p2v_prev = [self.sV.n] * 2
        p2g_prev = [self.sG.n] * 2
        p2_ranges = _blocks(2, W - 1, AW)
        nc.sync.wait_ge(self.sV.h, self.sV.n)
        nc.sync.wait_ge(self.sT.h, self.sT.n)
        c0_, c1_ = p2_ranges[0]
        for oc in range(ND):
            self.ld(xr2[oc][:, 0:c1_ - c0_], self.xr_st[oc][:, c0_ - 2:c1_ - 2])
        xr_ready = self.sL.n
        for oc in range(ND):
            self.ld(kw2[oc][:, 0:c1_ - c0_], self.kv_st[0, oc][:, c0_ - 2:c1_ - 2])
            self.ld(vw2[oc][:, 0:c1_ - c0_], self.kv_st[1, oc][:, c0_ - 2:c1_ - 2])
        kv_ready = self.sL.n
        prev_sig_end = 0
        for j, (c0, c1) in enumerate(p2_ranges):
            bw = c1 - c0
            nrw = bw < 64

            def vI(instr):
                if nrw:
                    self.sV.inc(instr)
                return instr

            def vE(instr):
                self.sV.inc(instr)
                if nrw:
                    v.wait_ge(self.sV.h, self.sV.n)
                return instr

            te.wait_ge(self.sL2.h, wr_ready)
            te.wait_ge(self.sL.h, xr_ready)
            r_ev = self.sA.n
            for oc in range(ND):
                if oc >= 2:
                    te.wait_ge(self.sA.h, r_ev + (oc - 1))
                elif prev_sig_end:
                    te.wait_ge(self.sA.h, prev_sig_end)
                for dd in range(ND):
                    mm = te.matmul(self.ps[0 + (oc % 2)][:, 0:bw],
                                   wbufA[dd][:, oc * 128:(oc + 1) * 128],
                                   xr2[dd][:, 0:bw], start=(dd == 0), stop=(dd == ND - 1))
                self.sT.inc(mm)
                self.wT(a)
                self.sA.inc(a.activation(rw2[oc][:, 0:bw], self.ps[0 + (oc % 2)][:, 0:bw],
                                         Act.Sigmoid, bias=0.0, scale=1.0))
            sig_done = self.sA.n
            prev_sig_end = sig_done
            rmm_end = self.sT.n
            if j + 1 < len(p2_ranges):
                n0, n1 = p2_ranges[j + 1]
                nc.sync.wait_ge(self.sT.h, rmm_end)
                for oc in range(ND):
                    self.ld(xr2[oc][:, 0:n1 - n0], self.xr_st[oc][:, n0 - 2:n1 - 2])
                xr_ready = self.sL.n
            v.wait_ge(self.sL.h, kv_ready)
            g.wait_ge(self.sL.h, kv_ready)
            H2 = [range(0, 4), range(4, 8)]
            rec = {}

            def gE(instrs):
                for i_, ig in enumerate(instrs):
                    if nrw or i_ == len(instrs) - 1:
                        self.sG.inc(ig)

            def aE(instrs):
                for i_, ia in enumerate(instrs):
                    if nrw or i_ == len(instrs) - 1:
                        self.sA.inc(ia)

            for h, chans in enumerate(H2):
                for c in chans:
                    vI(v.tensor_tensor_scan(ppb[c][:, 1:1 + bw], wz[c][:, 0:bw],
                                            kw2[c][:, 0:bw], ppb[c][:, 0:1],
                                            Alu.add, Alu.max))
                vE(v.engine_nop())
            for h, chans in enumerate(H2):
                v.wait_ge(self.sA.h, p2a_prev[h])
                v.wait_ge(self.sG.h, p2g_prev[h])
                for c in chans:
                    vI(v.tensor_tensor(T1[c][:, 0:bw], ppb[c][:, 0:bw],
                                       ppb[c][:, 1:1 + bw], Alu.subtract))
                    vI(v.tensor_tensor(T2[c][:, 0:bw], kw2[c][:, 0:bw],
                                       ppb[c][:, 1:1 + bw], Alu.subtract))
                vE(v.engine_nop())
                rec['s2', h] = self.sV.n
            for h, chans in enumerate(H2):
                a.wait_ge(self.sV.h, rec['s2', h])
                ii = []
                for c in chans:
                    ii.append(a.activation(T1[c][:, 0:bw], T1[c][:, 0:bw], Act.Exp,
                                           bias=self.w_c[:, c:c + 1], scale=1.0))
                    ii.append(a.activation(T2[c][:, 0:bw], T2[c][:, 0:bw], Act.Exp,
                                           bias=0.0, scale=1.0))
                aE(ii)
                rec['s3', h] = self.sA.n
            for h, chans in enumerate(H2):
                g.wait_ge(self.sA.h, rec['s3', h])
                g.wait_ge(self.sV.h, p2v_prev[h])
                gE([g.tensor_tensor(T3[c][:, 0:bw], vw2[c][:, 0:bw], T2[c][:, 0:bw],
                                    Alu.mult) for c in chans])
                rec['s4', h] = self.sG.n
            for h, chans in enumerate(H2):
                v.wait_ge(self.sG.h, rec['s4', h])
                for c in chans:
                    vI(v.tensor_tensor_scan(aab[c][:, 1:1 + bw], T1[c][:, 0:bw],
                                            T3[c][:, 0:bw], aab[c][:, 0:1],
                                            Alu.mult, Alu.add))
                    vI(v.tensor_tensor_scan(bbb[c][:, 1:1 + bw], T1[c][:, 0:bw],
                                            T2[c][:, 0:bw], bbb[c][:, 0:1],
                                            Alu.mult, Alu.add))
                vE(v.engine_nop())
            for h, chans in enumerate(H2):
                for c in chans:
                    vI(v.tensor_scalar(T3[c][:, 0:bw], ppb[c][:, 0:bw],
                                       self.u_c[:, c:c + 1], None, Alu.add))
                vE(v.engine_nop())
                for c in chans:
                    vI(v.tensor_tensor(T4[c][:, 0:bw], T3[c][:, 0:bw], kw2[c][:, 0:bw],
                                       Alu.max))
                vE(v.engine_nop())
                for c in chans:
                    vI(v.tensor_tensor(T3[c][:, 0:bw], T3[c][:, 0:bw], T4[c][:, 0:bw],
                                       Alu.subtract))
                    vI(v.tensor_tensor(T4[c][:, 0:bw], kw2[c][:, 0:bw], T4[c][:, 0:bw],
                                       Alu.subtract))
                vE(v.engine_nop())
                rec['s9', h] = self.sV.n
            for h, chans in enumerate(H2):
                a.wait_ge(self.sV.h, rec['s9', h])
                ii = []
                for c in chans:
                    ii.append(a.activation(T3[c][:, 0:bw], T3[c][:, 0:bw], Act.Exp,
                                           bias=0.0, scale=1.0))
                    ii.append(a.activation(T4[c][:, 0:bw], T4[c][:, 0:bw], Act.Exp,
                                           bias=0.0, scale=1.0))
                aE(ii)
                rec['s10', h] = self.sA.n
            for h, chans in enumerate(H2):
                g.wait_ge(self.sA.h, rec['s10', h])
                g.wait_ge(self.sV.h, rec['s9', h])
                gE([g.tensor_tensor(T1[c][:, 0:bw], vw2[c][:, 0:bw], T4[c][:, 0:bw],
                                    Alu.mult) for c in chans])
                rec['s11', h] = self.sG.n
            for h, chans in enumerate(H2):
                v.wait_ge(self.sA.h, rec['s10', h])
                for c in chans:
                    vI(v.tensor_tensor(T2[c][:, 0:bw], aab[c][:, 0:bw], T3[c][:, 0:bw],
                                       Alu.mult))
                vE(v.engine_nop())
                rec['s12a', h] = self.sV.n
            for h, chans in enumerate(H2):
                g.wait_ge(self.sV.h, rec['s12a', h])
                gE([g.tensor_tensor(T2[c][:, 0:bw], T2[c][:, 0:bw], T1[c][:, 0:bw],
                                    Alu.add) for c in chans])
                gE([g.tensor_tensor(T1[c][:, 0:bw], bbb[c][:, 0:bw], T3[c][:, 0:bw],
                                    Alu.mult) for c in chans])
                rec['s13', h] = self.sG.n
            for h, chans in enumerate(H2):
                v.wait_ge(self.sG.h, rec['s13', h])
                for c in chans:
                    vI(v.scalar_tensor_tensor(T1[c][:, 0:bw], T4[c][:, 0:bw], 1e-6,
                                              T1[c][:, 0:bw], Alu.add, Alu.add))
                vE(v.engine_nop())
                for c in chans:
                    vI(v.reciprocal(T1[c][:, 0:bw], T1[c][:, 0:bw]))
                vE(v.engine_nop())
                rec['s15', h] = self.sV.n
            for h, chans in enumerate(H2):
                g.wait_ge(self.sV.h, rec['s15', h])
                gE([g.tensor_tensor(T2[c][:, 0:bw], T2[c][:, 0:bw], T1[c][:, 0:bw],
                                    Alu.mult) for c in chans])
                rec['s16', h] = self.sG.n
            for h, chans in enumerate(H2):
                v.wait_ge(self.sA.h, sig_done)
                v.wait_ge(self.sG.h, rec['s16', h])
                for c in chans:
                    vI(v.tensor_tensor(ry2[c][:, 0:bw], rw2[c][:, 0:bw], T2[c][:, 0:bw],
                                       Alu.mult))
                vE(v.engine_nop())
            for c in range(ND):
                self.sV.inc(v.tensor_copy(ppb[c][:, 0:1], ppb[c][:, bw:bw + 1]))
                self.sV.inc(v.tensor_copy(aab[c][:, 0:1], aab[c][:, bw:bw + 1]))
                self.sV.inc(v.tensor_copy(bbb[c][:, 0:1], bbb[c][:, bw:bw + 1]))
            self.wV(v)
            p2a_prev = [rec['s10', 0], rec['s10', 1]]
            p2v_prev = [rec['s12a', 0], rec['s12a', 1]]
            p2g_prev = [rec['s16', 0], rec['s16', 1]]
            s9_done = rec['s9', 1]
            s11_done = rec['s11', 1]
            mm_base = self.sT.n
            ost_base = self.sS2.n
            ld_base = self.sL.n
            add_base = self.sV.n
            for od in range(ND):
                pso = self.ps[4 + (od % 2)]
                self.wV(te)
                for oc in range(ND):
                    mm = te.matmul(pso[:, 0:bw], wbufB[oc][:, od * 128:(od + 1) * 128],
                                   ry2[oc][:, 0:bw], start=(oc == 0), stop=(oc == ND - 1))
                self.sT.inc(mm)
                ob = x1o[od % 2]
                if od >= 2:
                    nc.sync.wait_ge(self.sS2.h, ost_base + 16 * (od - 1))
                else:
                    prev = getattr(self, '_wo_prev_end', 0)
                    if prev:
                        nc.sync.wait_ge(self.sS2.h, prev - 16 * (1 - od))
                self.ld(ob[:, 0:bw], self.xres_a[od][:, c0:c1])
                v.wait_ge(self.sT.h, mm_base + od + 1)
                v.wait_ge(self.sL.h, ld_base + 16 * (od + 1))
                self.sV.inc(v.tensor_tensor(ob[:, 0:bw], ob[:, 0:bw], pso[:, 0:bw],
                                            Alu.add))
                nc.sync.wait_ge(self.sV.h, add_base + od + 1)
                self.st2(self.xres_b[od][:, c0:c1], ob[:, 0:bw])
            self._wo_prev_end = self.sS2.n
            if j + 1 < len(p2_ranges):
                n0, n1 = p2_ranges[j + 1]
                nc.sync.wait_ge(self.sV.h, s9_done)
                nc.sync.wait_ge(self.sG.h, s11_done)
                for oc in range(ND):
                    self.ld(kw2[oc][:, 0:n1 - n0], self.kv_st[0, oc][:, n0 - 2:n1 - 2])
                    self.ld(vw2[oc][:, 0:n1 - n0], self.kv_st[1, oc][:, n0 - 2:n1 - 2])
                kv_ready = self.sL.n

    def build(self):
        nc = self.nc
        self.p0()
        self.pw()
        self.ffn(0, W, self.xw, self.xres_a, final=False)
        self.attn()
        nc.sync.wait_ge(self.sT.h, self.sT.n)
        self.syncS()
        for i in range(ND):
            self.ld(self.w1q[i][:], self.w1q_st[i])
        nc.tensor.wait_ge(self.sL.h, self.sL.n)
        self.ffn(2, W - 1, self.xres_b, self.out, final=True)
        nc.sync.wait_ge(self.sO.h, self.sO.n)
        nc.vector.wait_ge(self.sO.h, self.sO.n)
        nc.finalize()
        return nc


_CACHE = {}


def _build(shadow):
    key = bool(shadow)
    if key not in _CACHE:
        _CACHE[key] = Builder(shadow).build()
    return _CACHE[key]


def kernel(**inputs):
    x = np.asarray(inputs["L0_attn_x"], np.float32)
    td = np.asarray(inputs["L0_attn_time_decay"], np.float32)
    u = np.asarray(inputs["L0_attn_time_first"], np.float32)
    mk = np.asarray(inputs["L0_attn_time_mix_k"], np.float32).reshape(-1)
    mv = np.asarray(inputs["L0_attn_time_mix_v"], np.float32).reshape(-1)
    mr = np.asarray(inputs["L0_attn_time_mix_r"], np.float32).reshape(-1)
    Wk = np.asarray(inputs["L0_attn_Wk"], np.float32)
    Wv = np.asarray(inputs["L0_attn_Wv"], np.float32)
    Wr = np.asarray(inputs["L0_attn_Wr"], np.float32)
    Wo = np.asarray(inputs["L0_attn_Wo"], np.float32)
    w1 = np.asarray(inputs["L0_ffn_w1"], np.float32)
    cw = np.asarray(inputs["L0_ffn_conv_w"], np.float32)
    cb = np.asarray(inputs["L0_ffn_conv_b"], np.float32)
    w2 = np.asarray(inputs["L0_ffn_w2"], np.float32)

    def fm(vec, nch):
        return np.ascontiguousarray(vec.reshape(nch, P).T)

    shadow = os.environ.get("BASS_SIM_FP64") == "1"
    nc = _build(shadow)

    xfm = np.ascontiguousarray(np.transpose(x, (0, 2, 1)))
    T = x.shape[1]
    common = {
        "w1t": np.ascontiguousarray(w1.T).reshape(ND, P, HD),
        "w2t": np.ascontiguousarray(w2.T).reshape(NH, P, C),
        "wkt": np.ascontiguousarray(Wk.T).reshape(ND, P, C),
        "wvt": np.ascontiguousarray(Wv.T).reshape(ND, P, C),
        "wrt": np.ascontiguousarray(Wr.T).reshape(ND, P, C),
        "wot": np.ascontiguousarray(Wo.T).reshape(ND, P, C),
        "td_t": fm(td, ND), "u_t": fm(u, ND),
        "mk_t": fm(mk, ND), "mv_t": fm(mv, ND), "mr_t": fm(mr, ND),
        "wl_t": fm(cw[:, 0, 0].copy(), NH), "wc_t": fm(cw[:, 0, 1].copy(), NH),
        "wr2_t": fm(cw[:, 0, 2].copy(), NH), "cb_t": fm(cb, NH),
        "iden_in": np.eye(P, dtype=np.float32),
    }
    in_maps = []
    for i in range(8):
        b, c = i // 4, i % 4
        s, e = c * TC, (c + 1) * TC
        win = np.zeros((C, W), np.float32)
        glo, ghi = s - 3, e + 2
        slo, shi = max(glo, 0), min(ghi, T)
        win[:, slo - glo:shi - glo] = xfm[b, :, slo:shi]
        m = dict(common)
        m["xw"] = np.ascontiguousarray(win.reshape(ND, P, W))
        m["startneg_in"] = np.full((P, 1), -1e30 if c == 0 else 0.0, np.float32)
        m["startmask_in"] = np.full((P, 1), 0.0 if c == 0 else 1.0, np.float32)
        m["endmask_in"] = np.full((P, 1), 0.0 if c == 3 else 1.0, np.float32)
        oh = np.zeros((P, 8), np.float32)
        oh[:, i] = 1.0
        m["onehot_in"] = oh
        in_maps.append(m)

    res = run_bass_kernel_spmd(nc, in_maps, core_ids=list(range(8)))
    out = np.zeros((2, C, T), np.float32)
    for i in range(8):
        b, c = i // 4, i % 4
        o = np.asarray(res.results[i]["out"]).reshape(C, TC)
        out[b, :, c * TC:(c + 1) * TC] = o
    return np.ascontiguousarray(np.transpose(out, (0, 2, 1)))
